# version 18
# baseline (speedup 1.0000x reference)
import numpy as np
from contextlib import ExitStack

import concourse.bass as bass
import concourse.mybir as mybir
from concourse.bass_utils import run_bass_kernel_spmd

F32 = mybir.dt.float32
BF16 = mybir.dt.bfloat16
AF = mybir.ActivationFunctionType
ALU = mybir.AluOpType
AX = mybir.AxisListType

ENGINES = ("sync", "scalar", "vector", "tensor", "gpsimd")

B, S, DM = 4, 4096, 2048
NCORE = 8
T = 2048
NT = T // 128
ND = DM // 128


class Prog:

    def __init__(self):
        self.nc = bass.Bass("TRN2", target_bir_lowering=False)
        self.stack = ExitStack()
        self.ops = {e: [] for e in ENGINES}
        self.exec_cnt = {e: 0 for e in ENGINES}
        self.NSLOT = 14
        self.dma_seq = {}
        self.dma_cnt = {}
        self.res = {}
        self.pending_quiet = {e: [] for e in ENGINES}
        self._name_i = 0
        self.bg = []

    def pump(self, n):
        for _ in range(min(n, len(self.bg))):
            self.bg.pop(0)()

    def pump_all(self):
        self.pump(len(self.bg))

    def uname(self, prefix):
        self._name_i += 1
        return f"{prefix}_{self._name_i}"

    def sbuf(self, name, shape, dtype):
        return self.stack.enter_context(self.nc.sbuf_tensor(name, list(shape), dtype))

    def psum(self, name, shape, dtype=F32):
        return self.stack.enter_context(self.nc.psum_tensor(name, list(shape), dtype))

    def dram(self, name, shape, dtype, kind="Internal", **kw):
        return self.nc.dram_tensor(name, list(shape), dtype, kind=kind, **kw).ap()

    def sem(self, name):
        return self.stack.enter_context(self.nc.semaphore(name))

    def _dep(self, reads, writes):
        waits = {}

        def add(ent):
            if ent is None:
                return
            eng, kind, cnt = ent
            key = (eng, kind)
            if waits.get(key, -1) < cnt:
                waits[key] = cnt

        for r in reads:
            ent = self.res.get(r)
            if ent:
                add(ent.get("w"))
        for w in writes:
            ent = self.res.get(w)
            if ent:
                add(ent.get("w"))
                for rd in ent.get("r", ()):
                    add(rd)
        return waits

    def _register(self, eng, kind, cnt, reads, writes):
        for w in writes:
            self.res[w] = {"w": (eng, kind, cnt), "r": []}
        for r in reads:
            ent = self.res.setdefault(r, {"w": None, "r": []})
            ent["r"].append((eng, kind, cnt))

    def op(self, eng, fn, reads=(), writes=(), quiet=False):
        waits = self._dep(reads, writes)
        if quiet:
            assert eng == "tensor", "quiet ops only on the PE"
            fwd = self.exec_cnt[eng] + 1
            self.pending_quiet[eng] = [True]
            self._register(eng, "exec", fwd, reads, writes)
            self.ops[eng].append((waits, fn, None))
        else:
            self.exec_cnt[eng] += 1
            cnt = self.exec_cnt[eng]
            self.pending_quiet[eng] = []
            self._register(eng, "exec", cnt, reads, writes)
            self.ops[eng].append((waits, fn, ("exec", cnt)))

    def dma(self, eng, fn, reads=(), writes=(), cls="ld"):
        waits = self._dep(reads, writes)
        if cls == "ld":
            slots = (0, 1, 2, 3, 4, 5)
        elif cls == "cvt":
            slots = (8, 9, 10, 11, 12, 13)
        else:
            slots = (6, 7)
        key2 = (eng, cls)
        i = self.dma_seq.get(key2, 0)
        self.dma_seq[key2] = i + 1
        slot = slots[i % len(slots)]
        seq = i // len(slots) + 1
        kind = f"dma{slot}"
        if seq > 1:
            key = (eng, kind)
            if waits.get(key, -1) < 16 * (seq - 1):
                waits[key] = 16 * (seq - 1)
        self.dma_cnt[(eng, slot)] = 16 * seq
        self._register(eng, kind, 16 * seq, reads, writes)
        self.ops[eng].append((waits, fn, (kind, 16 * seq)))

    def barrier(self):
        snapshot = {}
        for e in ENGINES:
            if self.exec_cnt[e]:
                snapshot[(e, "exec")] = self.exec_cnt[e]
        for (e, slot), cnt in self.dma_cnt.items():
            snapshot[(e, f"dma{slot}")] = cnt
        for e in ENGINES:
            self.ops[e].append((dict(snapshot), None, None))
        self.res = {}
        self.pending_quiet = {e: [] for e in ENGINES}

    def finalize(self):
        assert all(not v for v in self.pending_quiet.values()), \
            "quiet ops not covered by a later inc"
        nc = self.nc
        sems = {}
        for e in ENGINES:
            sems[(e, "exec")] = self.stack.enter_context(nc.semaphore(f"sem_x_{e}"))
            for s in range(self.NSLOT):
                if (e, s) in self.dma_cnt:
                    sems[(e, f"dma{s}")] = self.stack.enter_context(
                        nc.semaphore(f"sem_d{s}_{e}"))
        block = self.stack.enter_context(nc.Block())

        final_waits = {}
        for e in ENGINES:
            if self.exec_cnt[e]:
                final_waits[(e, "exec")] = self.exec_cnt[e]
        for (e, slot), cnt in self.dma_cnt.items():
            final_waits[(e, f"dma{slot}")] = cnt

        needed = {}
        for e in ENGINES:
            s = set()
            for e2 in ENGINES:
                for waits, _, _ in self.ops[e2]:
                    c = waits.get((e, "exec"))
                    if c is not None:
                        s.add(c)
            if self.exec_cnt[e]:
                s.add(self.exec_cnt[e])
            needed[e] = sorted(c for c in s if c <= self.exec_cnt[e])

        import bisect

        def remap(e, c):
            i = bisect.bisect_left(needed[e], c)
            assert i < len(needed[e]) and needed[e][i] == c, (e, c)
            return i + 1

        def mk(ename):
            def body(eng):
                watermark = {}
                own_incs = 0
                for waits, fn, inc in self.ops[ename]:
                    for key, cnt in sorted(waits.items()):
                        if key == (ename, "exec") and cnt > own_incs:
                            continue
                        w = remap(key[0], cnt) if key[1] == "exec" else cnt
                        if watermark.get(key, -1) >= w:
                            continue
                        eng.wait_ge(sems[key], w)
                        watermark[key] = w
                    if inc is not None and inc[0] == "exec":
                        own_incs = inc[1]
                    if fn is None:
                        continue
                    inst = fn(eng)
                    if inc is not None:
                        kind, cnt = inc
                        assert inst is not None, "inc op returned no instruction"
                        if kind.startswith("dma"):
                            inst.then_inc(sems[(ename, kind)], 16)
                        elif cnt in needed_set[ename]:
                            inst.then_inc(sems[(ename, kind)], 1)
                if ename == "sync":
                    for key, cnt in sorted(final_waits.items()):
                        w = remap(key[0], cnt) if key[1] == "exec" else cnt
                        eng.wait_ge(sems[key], w)
            return body

        needed_set = {e: set(needed[e]) for e in ENGINES}

        for e in ENGINES:
            getattr(block, e)(mk(e))
        self.stack.close()
        return nc


class APView:

    def __init__(self, t, parts, c0, cols):
        self.t, self.parts, self.c0, self.cols = t, parts, c0, cols
        self.shape = (parts, cols)

    def ap(self):
        return self.t[0:self.parts, self.c0:self.c0 + self.cols]

    def __getitem__(self, key):
        if not isinstance(key, tuple):
            key = (key, slice(None))
        ps, cs = key
        if isinstance(ps, slice):
            p0, p1, pst = ps.indices(self.parts)
            assert pst == 1
        else:
            raise TypeError("int partition index not supported on APView")
        c0, c1, cst = cs.indices(self.cols)
        return self.t[p0:p1, self.c0 + c0:self.c0 + c1:cst] if cst != 1 else             self.t[p0:p1, self.c0 + c0:self.c0 + c1]


_FIXED_PREFIXES = ("lnst", "ln_", "gw0", "gwC", "cw12_", "c_ones", "c_identf")


class WS:
    POOLB_COLS = 25600
    POOLF_COLS = 5120

    def __init__(self, P, pooled=False):
        self.P = P
        self.bufs = {}
        self.pooled = pooled
        self.pool_t = {}
        self.pool_off = {BF16: 0, F32: 0}
        self.epoch_cache = {}

    def _pool(self, dtype):
        if dtype not in self.pool_t:
            if dtype == BF16:
                self.pool_t[dtype] = self.P.sbuf("POOLB", [128, self.POOLB_COLS], BF16)
            else:
                self.pool_t[dtype] = self.P.sbuf("POOLF", [128, self.POOLF_COLS], F32)
        return self.pool_t[dtype]

    def new_epoch(self):
        self.pool_off = {BF16: 0, F32: 0}
        self.epoch_cache = {}

    BLOCKS = {"gw0_b": ("gw0T", 32, BF16), "gwC_b": ("gwCT", 32, BF16),
              "lnst_x": ("lnstXT", 16, F32), "lnst_o": ("lnstOT", 16, BF16),
              "hc": ("hcT", 8, BF16)}

    def get(self, name, shape, dtype, psum=False):
        for pref, (bname, n, bdt) in self.BLOCKS.items():
            if name.startswith(pref) and name[len(pref):].isdigit():
                i = int(name[len(pref):])
                assert i < n and dtype == bdt and tuple(shape) == (128, 512), name
                bt = self.get(bname, [128, n * 512], bdt)
                return APView(bt, 128, i * 512, 512)
        key = name
        if psum or not self.pooled or any(name.startswith(p) for p in _FIXED_PREFIXES):
            if key in self.bufs:
                t, sh, dt = self.bufs[key]
                assert tuple(sh) == tuple(shape) and dt == dtype,                     f"ws {name} shape clash"
                return t
            t = self.P.psum(name, shape, dtype) if psum else                 self.P.sbuf(name, shape, dtype)
            self.bufs[key] = (t, tuple(shape), dtype)
            return t
        if key in self.epoch_cache:
            v, sh, dt = self.epoch_cache[key]
            assert tuple(sh) == tuple(shape) and dt == dtype, f"pool {name} clash"
            return v
        t = self._pool(dtype)
        cols = shape[1]
        off = self.pool_off[dtype]
        cap = self.POOLB_COLS if dtype == BF16 else self.POOLF_COLS
        assert off + cols <= cap,             f"pool {dtype} overflow at {name}: {off}+{cols} > {cap}"
        self.pool_off[dtype] = off + cols
        v = APView(t, shape[0], off, cols)
        self.epoch_cache[key] = (v, tuple(shape), dtype)
        return v

    def get_block(self, name, n_tiles, dtype, cols=512):
        blk = self.get(name, [128, n_tiles * cols], dtype)
        views = [APView(blk.t if isinstance(blk, APView) else blk, 128,
                        (blk.c0 if isinstance(blk, APView) else 0) + i * cols, cols)
                 for i in range(n_tiles)]
        return blk, views

    def psa(self, i):
        return self.get(f"PSA{i}", [128, 512], F32, psum=True), f"PSA{i}"


def emit_consts(P, ws):
    ones1 = ws.get("c_ones1", [1, 128], BF16)
    onesK = ws.get("c_onesK", [128, 1], BF16)
    onesM = ws.get("c_onesM", [128, 128], BF16)
    P.op("vector", lambda e: e.memset(ones1[:], 1.0), writes=["c_ones1"])
    P.op("vector", lambda e: e.memset(onesK[:], 1.0), writes=["c_onesK"])
    P.op("vector", lambda e: e.memset(onesM[:], 1.0), writes=["c_onesM"])


def emit_ln(P, ws, xt, xkeys, xlnt, xlnkeys, seg):
    onesM = ws.get("c_onesM", [128, 128], BF16)
    xb = [ws.get(f"ln_xb{i}", [128, 512], BF16) for i in range(2)]
    sq = [ws.get(f"ln_sq{i}", [128, 512], BF16) for i in range(2)]
    p_sum, k_sum = ws.psa(4)
    p_sq, k_sq = ws.psa(5)
    mbc = ws.get("ln_mbc", [128, 512], F32)
    rbc = ws.get("ln_rbc", [128, 512], F32)
    tmp = [ws.get(f"ln_tmp{i}", [128, 512], F32) for i in range(2)]

    sl = slice(0, seg)
    for i in range(ND):
        b, bk = xb[i % 2], f"ln_xb{i % 2}"
        s, sk2 = sq[i % 2], f"ln_sq{i % 2}"
        P.op("scalar", lambda e, b=b, i=i: e.copy(b[:, sl], xt[i][:, sl]),
             reads=[xkeys[i]], writes=[bk])
        P.op("tensor", lambda e, b=b, i=i: e.matmul(
            p_sum[:, sl], onesM[:], b[:, sl], start=(i == 0), stop=(i == ND - 1),
            skip_group_check=True),
            reads=[bk, "c_onesM"], writes=[k_sum])
        P.op("scalar", lambda e, s=s, i=i: e.activation(s[:, sl], xt[i][:, sl], AF.Square),
             reads=[xkeys[i]], writes=[sk2])
        P.op("tensor", lambda e, s=s, i=i: e.matmul(
            p_sq[:, sl], onesM[:], s[:, sl], start=(i == 0), stop=(i == ND - 1),
            skip_group_check=True),
            reads=[sk2, "c_onesM"], writes=[k_sq])
    P.op("vector", lambda e: e.tensor_scalar(mbc[:, sl], p_sum[:, sl], 1.0 / DM, None,
                                             op0=ALU.mult),
         reads=[k_sum], writes=["ln_mbc"])
    P.op("vector", lambda e: e.tensor_tensor(rbc[:, sl], mbc[:, sl], mbc[:, sl],
                                             op=ALU.mult),
         reads=["ln_mbc"], writes=["ln_rbc"])
    P.op("vector", lambda e: e.scalar_tensor_tensor(
        rbc[:, sl], p_sq[:, sl], 1.0 / DM, rbc[:, sl], op0=ALU.mult, op1=ALU.subtract),
        reads=[k_sq, "ln_rbc"], writes=["ln_rbc"])
    P.op("vector", lambda e: e.tensor_scalar(rbc[:, sl], rbc[:, sl], 1e-5, None,
                                             op0=ALU.add),
         reads=["ln_rbc"], writes=["ln_rbc"])
    P.op("scalar", lambda e: e.activation(rbc[:, sl], rbc[:, sl], AF.Ln),
         reads=["ln_rbc"], writes=["ln_rbc"])
    P.op("scalar", lambda e: e.activation(rbc[:, sl], rbc[:, sl], AF.Exp, scale=-0.5),
         reads=["ln_rbc"], writes=["ln_rbc"])
    for i in range(ND):
        t, tk = tmp[i % 2], f"ln_tmp{i % 2}"
        P.op("gpsimd", lambda e, i=i, t=t: e.tensor_tensor(
            t[:, sl], xt[i][:, sl], mbc[:, sl], op=ALU.subtract),
            reads=[xkeys[i], "ln_mbc"], writes=[tk])
        P.op("vector", lambda e, i=i, t=t: e.tensor_tensor(
            xlnt[i][:, sl], t[:, sl], rbc[:, sl], op=ALU.mult),
            reads=[tk, "ln_rbc"], writes=[xlnkeys[i]])


def load_w_bf16(P, ws, wdram, wkey, row_tiles, col0, cols, parity, conv_eng=None):
    assert cols == 512
    src3 = wdram[0:row_tiles * 128, col0:col0 + 512].rearrange(
        "(t p) c -> p t c", p=128)
    return load_wblock(P, ws, f"gw{parity}", 0, src3, wkey, row_tiles, True,
                       P.uname(f"wb{parity}"))


def load_bias_col(P, ws, bdram, bkey, o0, name):
    t = ws.get(name, [128, 1], F32)
    P.dma("sync", lambda e: e.dma_start(t[:], bdram[o0:o0 + 128, None]),
          reads=[bkey], writes=[name])
    return t


def emit_dense_fm(P, ws, tag, wT_dram, wkey, bias_dram, bias_key, in_dram, in_key,
                  out_dram, out_key, d_in, d_out, act_func,
                  res_dram=None, res_key=None, out_dtype=BF16, nseg=4, seg=512,
                  in_granular=False, out_granular=False):
    OG = 4
    n_din = d_in // 128
    assert n_din <= 32
    n_og = d_out // (OG * 128)
    rhblk, rh = ws.get_block("dn_rhall", 32, BF16)
    resb = ws.get("dn_res", [128, 512], F32) if res_dram is not None else None
    outb = [ws.get(f"dn_out{i}_{'f' if out_dtype==F32 else 'b'}", [128, 512], out_dtype)
            for i in range(2)]
    wts = {}

    def loadw(og):
        if og < n_og and og not in wts:
            wts[og] = load_w_bf16(P, ws, wT_dram, wkey, n_din, og * OG * 128, 512,
                                  "0" if og % 2 == 0 else "C")
        return wts.get(og)

    loadw(0)
    for og in range(n_og):
        wt = loadw(og)
        wts.pop(og - 1, None)
        btiles = []
        for o in range(OG):
            if bias_dram is not None:
                btiles.append(load_bias_col(P, ws, bias_dram, bias_key,
                                            og * OG * 128 + o * 128,
                                            f"dn_bias{og % 2}_{o}"))
        for sg in range(nseg):
            pb = 4 * ((og * nseg + sg) % 2)
            for bc in range(0, n_din, 8):
                bn = min(8, n_din - bc)
                if in_granular:
                    rkeys = [f"{in_key}.{sg}.{b}"
                             for b in range(bc // 4, (bc + bn + 3) // 4)]
                else:
                    rkeys = [in_key]
                P.dma("sync",
                      lambda e, bc=bc, bn=bn, sg=sg: e.dma_start(
                          rhblk[0:128, bc * 512:(bc + bn) * 512].rearrange(
                              "p (t c) -> p t c", t=bn)[:, :, 0:seg],
                          in_dram[bc * 128:(bc + bn) * 128,
                                  sg * seg:(sg + 1) * seg].rearrange(
                              "(t p) c -> p t c", p=128)),
                      reads=rkeys, writes=[f"dn_rh{i}" for i in range(bc, bc + bn)])
            for dt in range(n_din):
                r, rk = rh[dt], f"dn_rh{dt}"
                for o in range(OG):
                    ps, pk = ws.psa(pb + o)
                    P.op("tensor", lambda e, o=o, r=r, ps=ps, w=wt[dt][0], dt=dt: e.matmul(
                        ps[:, 0:seg], w[:, o * 128:(o + 1) * 128], r[:, 0:seg],
                        start=(dt == 0), stop=(dt == n_din - 1),
                        skip_group_check=True),
                        reads=[wt[dt][1], rk], writes=[pk],
                        quiet=not (o == OG - 1 and dt % 4 == 3))
            if sg == 0:
                loadw(og + 1)
            for o in range(OG):
                ps, pk = ws.psa(pb + o)
                orow = og * OG * 128 + o * 128
                ob, obk = outb[o % 2], f"dn_out{o % 2}_{'f' if out_dtype==F32 else 'b'}"
                if res_dram is not None:
                    P.dma("gpsimd", lambda e, sg=sg, orow=orow: e.dma_start(
                        resb[:, 0:seg], res_dram[orow:orow + 128, sg * seg:(sg + 1) * seg]),
                        reads=[res_key], writes=["dn_res"])
                    P.op("vector", lambda e, ps=ps, ob=ob, bt=btiles[o]: e.scalar_tensor_tensor(
                        ob[:, 0:seg], ps[:, 0:seg], bt[:], resb[:, 0:seg],
                        op0=ALU.add, op1=ALU.add),
                        reads=[pk, f"dn_bias{og % 2}_{o}", "dn_res"], writes=[obk])
                else:
                    P.op("scalar", lambda e, ps=ps, ob=ob, bt=btiles[o]: e.activation(
                        ob[:, 0:seg], ps[:, 0:seg], act_func, bias=bt[:]),
                        reads=[pk, f"dn_bias{og % 2}_{o}"], writes=[obk])
                wkeys = [out_key]
                if out_granular:
                    wkeys.append(f"{out_key}.{sg}.{orow // 512}")
                P.dma("sync", lambda e, sg=sg, orow=orow, ob=ob: e.dma_start(
                    out_dram[orow:orow + 128, sg * seg:(sg + 1) * seg], ob[:, 0:seg]),
                    reads=[obk], writes=wkeys, cls="st")


def emit_ln_stream(P, ws, tag, x_dram, x_key, xln_dram, xln_key, nseg=4, seg=512):
    xt = [ws.get(f"lnst_x{i}", [128, 512], F32) for i in range(ND)]
    xlnt = [ws.get(f"lnst_o{i}", [128, 512], BF16) for i in range(ND)]
    xkeys = [f"lnst_x{i}" for i in range(ND)]
    xlnkeys = [f"lnst_o{i}" for i in range(ND)]
    xblk = ws.get("lnstXT", [128, 16 * 512], F32)
    oblk = ws.get("lnstOT", [128, 16 * 512], BF16)
    for sg in range(nseg):
        for b0 in (0, 4, 8, 12):
            P.dma("sync", lambda e, b0=b0, sg=sg: e.dma_start(
                xblk[0:128, b0 * 512:(b0 + 4) * 512].rearrange(
                    "p (t c) -> p t c", t=4)[:, :, 0:seg],
                x_dram[b0 * 128:(b0 + 4) * 128,
                       sg * seg:(sg + 1) * seg].rearrange("(t p) c -> p t c", p=128)),
                reads=[x_key], writes=xkeys[b0:b0 + 4])
        emit_ln(P, ws, xt, xkeys, xlnt, xlnkeys, seg)
        for b0 in (0, 4, 8, 12):
            P.dma("gpsimd", lambda e, b0=b0, sg=sg: e.dma_start(
                xln_dram[b0 * 128:(b0 + 4) * 128,
                         sg * seg:(sg + 1) * seg].rearrange("(t p) c -> p t c", p=128),
                oblk[0:128, b0 * 512:(b0 + 4) * 512].rearrange(
                    "p (t c) -> p t c", t=4)[:, :, 0:seg]),
                reads=xlnkeys[b0:b0 + 4],
                writes=[xln_key, f"{xln_key}.{sg}.{b0 // 4}"], cls="st")


def emit_mlp(P, ws, io):
    xln = P.dram("mlp_xln", [DM, T], BF16)
    h1 = P.dram("mlp_h1", [4096, T], BF16)
    h2 = P.dram("mlp_h2", [4096, T], BF16)
    emit_ln_stream(P, ws, "mlpln", io["x3"], "x3", xln, "mlp_xln")
    emit_dense_fm(P, ws, "mw1", io["W1T"], "W1T", io["b1"], "b1",
                  xln, "mlp_xln", h1, "mlp_h1", DM, 4096, AF.Relu,
                  in_granular=True, out_granular=True)
    emit_dense_fm(P, ws, "mw2", io["W2T"], "W2T", io["b2"], "b2",
                  h1, "mlp_h1", h2, "mlp_h2", 4096, 4096, AF.Relu,
                  in_granular=True, out_granular=True)
    emit_dense_fm(P, ws, "mw3", io["W3T"], "W3T", io["b3"], "b3",
                  h2, "mlp_h2", io["x4"], "x4", 4096, DM, None,
                  res_dram=io["x3"], res_key="x3", out_dtype=F32,
                  in_granular=True)


def build_mlp_phase():
    P = Prog()
    ws = WS(P)
    io = dict(
        x3=P.dram("x3", [DM, T], F32, kind="ExternalInput"),
        W1T=P.dram("W1T", [DM, 4096], F32, kind="ExternalInput"),
        b1=P.dram("b1", [4096], F32, kind="ExternalInput"),
        W2T=P.dram("W2T", [4096, 4096], F32, kind="ExternalInput"),
        b2=P.dram("b2", [4096], F32, kind="ExternalInput"),
        W3T=P.dram("W3T", [4096, DM], F32, kind="ExternalInput"),
        b3=P.dram("b3", [DM], F32, kind="ExternalInput"),
        x4=P.dram("x4", [DM, T], F32, kind="ExternalOutput"),
    )
    emit_consts(P, ws)
    emit_mlp(P, ws, io)
    return P.finalize()


def emit_bf16_cache(P, ws, src, src_key, dst, dst_key, rows, cols, name,
                    conv_eng="scalar", bg=False):
    CW = 1024
    nr, ncol = rows // 128, cols // CW
    for rt in range(nr):
        for cc in range(ncol):
            i = rt * ncol + cc
            sl_in = src[rt * 128:(rt + 1) * 128, cc * CW:(cc + 1) * CW]
            sl_out = dst[rt * 128:(rt + 1) * 128, cc * CW:(cc + 1) * CW]
            def emit_tile(i=i, sl_in=sl_in, sl_out=sl_out):
                _cache_tile(P, ws, i, sl_in, sl_out, src_key, dst_key, name,
                            conv_eng)
            if bg:
                P.bg.append(emit_tile)
            else:
                emit_tile()
    return


def _cache_tile(P, ws, i, sl_in, sl_out, src_key, dst_key, name, conv_eng):
            P.dma("gpsimd",
                  lambda e, sl_in=sl_in, sl_out=sl_out: e.dma_start(sl_out, sl_in),
                  reads=[src_key], writes=[dst_key], cls="cvt")


def load_wblock(P, ws, pool, rt_off, src_ap, src_key, n_tiles, cast, key_tag,
                chunk=8):
    bt = ws.get(f"{pool}_b0", [128, 512], BF16)
    blk = ws.get(ws.BLOCKS[f"{pool}_b"][0],
                 [128, ws.BLOCKS[f"{pool}_b"][1] * 512], BF16)
    g = src_ap.shape[1]
    gc = src_ap.shape[2]
    tpg = gc // 512
    assert g * tpg == n_tiles
    out = []
    for g0 in range(0, g, max(1, chunk // tpg)):
        gn = min(max(1, chunk // tpg), g - g0)
        t0 = rt_off + g0 * tpg
        tn = gn * tpg
        tkeys = [f"{pool}_b{t0 + t}" for t in range(tn)]
        dst = blk[0:128, t0 * 512:(t0 + tn) * 512].rearrange(
            "p (g c) -> p g c", g=gn)
        sl = src_ap[:, g0:g0 + gn, :]
        P.dma("gpsimd" if cast else "sync",
              lambda e, dst=dst, sl=sl: e.dma_start(dst, sl),
              reads=[src_key], writes=tkeys)
        for t in range(tn):
            out.append((APView(blk, 128, (t0 + t) * 512, 512), tkeys[t]))
    return out


def load_wtiles(P, ws, pool, n_tiles, srcs, src_key, bf16_src=True,
                rt_off=0, conv_eng="gpsimd"):
    out = []
    for i in range(n_tiles):
        bt = ws.get(f"{pool}_b{rt_off + i}", [128, 512], BF16)
        bk = f"{pool}_b{rt_off + i}"
        if bf16_src:
            P.dma("sync", lambda e, bt=bt, s=srcs[i]: e.dma_start(bt[:], s),
                  reads=[src_key], writes=[bk])
        else:
            P.dma("gpsimd", lambda e, bt=bt, s=srcs[i]: e.dma_start(bt[:], s),
                  reads=[src_key], writes=[bk])
        out.append((bt, bk))
    return out


def emit_moe(P, ws, io, cache=None):
    xln = P.dram("moe_xln", [DM, T], BF16)
    hsh = P.dram("moe_hsh", [8192, T], BF16)
    x2p = P.dram("moe_x2p", [DM, T], F32)
    ones1 = ws.get("c_ones1", [1, 128], BF16)

    if cache is not None:
        w12b, w3b = cache
    else:
        w12b, w3b = io["w12T"], io["w3T"]

    emit_ln_stream(P, ws, "moeln", io["x1"], "x1", xln, "moe_xln")

    iota8 = ws.get("r_iota8", [128, 8], F32)
    P.op("gpsimd", lambda e: e.iota(iota8[:], pattern=[[1, 8]], base=0,
                                    channel_multiplier=0,
                                    allow_small_or_imprecise_dtypes=True),
         writes=["r_iota8"])
    ebrow = ws.get("r_ebrow", [1, 8], BF16)
    P.dma("gpsimd", lambda e: e.dma_start(ebrow[:], io["ebias"][None, :]),
          reads=["ebias"], writes=["r_ebrow"])
    p_eb, k_eb = ws.psa(4)
    ebbc = ws.get("r_ebbc", [128, 8], F32)
    P.op("tensor", lambda e: e.matmul(p_eb[:, 0:8], ones1[:], ebrow[:],
                                      start=True, stop=True, skip_group_check=True),
         reads=["r_ebrow", "c_ones1"], writes=[k_eb])
    P.op("vector", lambda e: e.tensor_copy(ebbc[:], p_eb[:, 0:8]),
         reads=[k_eb], writes=["r_ebbc"])
    rwt = []
    for dt in range(ND):
        rw = ws.get(f"r_w{dt}", [128, 8], BF16)
        P.dma("gpsimd", lambda e, dt=dt, rw=rw: e.dma_start(
            rw[:], io["rwT"][dt * 128:(dt + 1) * 128, :]),
            reads=["rwT"], writes=[f"r_w{dt}"])
        rwt.append((rw, f"r_w{dt}"))
    combT = ws.get("r_combT", [8, T], BF16)
    exsel = ws.get("r_exsel", [8, 1024], BF16)
    P.op("gpsimd", lambda e: e.memset(exsel[:], 0.0), writes=["r_exsel"])
    P.op("gpsimd", lambda e: e.affine_select(
        out=exsel.ap().rearrange("k (x m) -> k x m", x=8),
        in_=exsel.ap().rearrange("k (x m) -> k x m", x=8),
        compare_op=ALU.not_equal, fill=1.0, base=0,
        channel_multiplier=1, pattern=[[-1, 8], [0, 128]]),
        reads=["r_exsel"], writes=["r_exsel"])
    xs_t = ws.get("r_x", [128, 512], BF16)
    p_sc, k_sc = ws.psa(5)
    sc = ws.get("r_sc", [128, 8], F32)
    sel = ws.get("r_sel", [128, 8], F32)
    m1 = ws.get("r_m1", [128, 1], F32)
    cand = ws.get("r_cand", [128, 8], F32)
    i1 = ws.get("r_i1", [128, 1], F32)
    oh1 = ws.get("r_oh1", [128, 8], F32)
    s12 = ws.get("r_s12", [128, 4], F32)
    tmp8 = ws.get("r_tmp8", [128, 8], F32)
    comb = ws.get("r_comb", [128, 8], F32)
    p_tr, k_tr = ws.psa(6)
    identf = ws.get("c_identf", [128, 128], F32)
    P.op("gpsimd", lambda e: e.memset(identf[:], 0.0), writes=["c_identf"])
    P.op("gpsimd", lambda e: e.affine_select(
        out=identf[:], in_=identf[:], compare_op=ALU.not_equal, fill=1.0,
        base=0, pattern=[[-1, 128]], channel_multiplier=1),
        reads=["c_identf"], writes=["c_identf"])
    for ts in range(NT):
        sg, of = ts // 4, (ts % 4) * 128
        for dt in range(ND):
            P.dma("gpsimd", lambda e, dt=dt, sg=sg, of=of: e.dma_start(
                xs_t[:, 0:128], xln[dt * 128:(dt + 1) * 128,
                                    sg * 512 + of:sg * 512 + of + 128]),
                reads=["moe_xln"], writes=["r_x"])
            P.op("tensor", lambda e, dt=dt, w=rwt[dt][0]: e.matmul(
                p_sc[:, 0:8], xs_t[:, 0:128], w[:],
                start=(dt == 0), stop=(dt == ND - 1), skip_group_check=True),
                reads=["r_x", rwt[dt][1]], writes=[k_sc])
        P.op("scalar", lambda e: e.activation(sc[:], p_sc[:, 0:8], AF.Sigmoid),
             reads=[k_sc], writes=["r_sc"])
        P.op("vector", lambda e: e.tensor_tensor(sel[:], sc[:], ebbc[:], op=ALU.add),
             reads=["r_sc", "r_ebbc"], writes=["r_sel"])
        P.op("vector", lambda e: e.reduce_max(m1[:], sel[:], axis=AX.X),
             reads=["r_sel"], writes=["r_m1"])
        P.op("vector", lambda e: e.tensor_scalar(cand[:], sel[:], m1[:], 1e9,
                                                 op0=ALU.is_lt, op1=ALU.mult),
             reads=["r_sel", "r_m1"], writes=["r_cand"])
        P.op("vector", lambda e: e.tensor_tensor(cand[:], cand[:], iota8[:], op=ALU.add),
             reads=["r_cand", "r_iota8"], writes=["r_cand"])
        P.op("vector", lambda e: e.tensor_reduce(i1[:], cand[:], axis=AX.X, op=ALU.min),
             reads=["r_cand"], writes=["r_i1"])
        P.op("vector", lambda e: e.tensor_scalar(oh1[:], iota8[:], i1[:], None,
                                                 op0=ALU.is_equal),
             reads=["r_iota8", "r_i1"], writes=["r_oh1"])
        P.op("vector", lambda e: e.tensor_tensor(tmp8[:], sc[:], oh1[:], op=ALU.mult),
             reads=["r_sc", "r_oh1"], writes=["r_tmp8"])
        P.op("vector", lambda e: e.reduce_sum(s12[:, 0:1], tmp8[:], axis=AX.X),
             reads=["r_tmp8"], writes=["r_s12"])
        P.op("vector", lambda e: e.tensor_scalar(comb[:], oh1[:], s12[:, 0:1], None,
                                                 op0=ALU.mult),
             reads=["r_oh1", "r_s12"], writes=["r_comb"])
        P.op("vector", lambda e: e.scalar_tensor_tensor(
            sel[:], oh1[:], -1e9, sel[:], op0=ALU.mult, op1=ALU.add),
            reads=["r_oh1", "r_sel"], writes=["r_sel"])
        P.op("vector", lambda e: e.reduce_max(m1[:], sel[:], axis=AX.X),
             reads=["r_sel"], writes=["r_m1"])
        P.op("vector", lambda e: e.tensor_scalar(cand[:], sel[:], m1[:], 1e9,
                                                 op0=ALU.is_lt, op1=ALU.mult),
             reads=["r_sel", "r_m1"], writes=["r_cand"])
        P.op("vector", lambda e: e.tensor_tensor(cand[:], cand[:], iota8[:], op=ALU.add),
             reads=["r_cand", "r_iota8"], writes=["r_cand"])
        P.op("vector", lambda e: e.tensor_reduce(i1[:], cand[:], axis=AX.X, op=ALU.min),
             reads=["r_cand"], writes=["r_i1"])
        P.op("vector", lambda e: e.tensor_scalar(oh1[:], iota8[:], i1[:], None,
                                                 op0=ALU.is_equal),
             reads=["r_iota8", "r_i1"], writes=["r_oh1"])
        P.op("vector", lambda e: e.tensor_tensor(tmp8[:], sc[:], oh1[:], op=ALU.mult),
             reads=["r_sc", "r_oh1"], writes=["r_tmp8"])
        P.op("vector", lambda e: e.reduce_sum(s12[:, 1:2], tmp8[:], axis=AX.X),
             reads=["r_tmp8"], writes=["r_s12"])
        P.op("vector", lambda e: e.scalar_tensor_tensor(
            comb[:], oh1[:], s12[:, 1:2], comb[:], op0=ALU.mult, op1=ALU.add),
            reads=["r_oh1", "r_s12", "r_comb"], writes=["r_comb"])
        P.op("vector", lambda e: e.tensor_tensor(s12[:, 2:3], s12[:, 0:1], s12[:, 1:2],
                                                 op=ALU.add),
             reads=["r_s12"], writes=["r_s12"])
        P.op("vector", lambda e: e.tensor_scalar(s12[:, 2:3], s12[:, 2:3], 1e-9, None,
                                                 op0=ALU.add),
             reads=["r_s12"], writes=["r_s12"])
        P.op("vector", lambda e: e.reciprocal(s12[:, 3:4], s12[:, 2:3]),
             reads=["r_s12"], writes=["r_s12"])
        P.op("vector", lambda e: e.tensor_scalar(comb[:], comb[:], s12[:, 3:4], None,
                                                 op0=ALU.mult),
             reads=["r_comb", "r_s12"], writes=["r_comb"])
        P.op("tensor", lambda e: e.transpose(p_tr[0:8, 0:128], comb[:], identf[:]),
             reads=["r_comb", "c_identf"], writes=[k_tr])
        P.op("vector", lambda e, ts=ts: e.tensor_copy(
            combT[:, ts * 128:(ts + 1) * 128], p_tr[0:8, 0:128]),
            reads=[k_tr], writes=["r_combT"])

    mxl = [ws.get(f"lnst_o{i}", [128, 512], BF16) for i in range(ND)]
    hc = [ws.get(f"hc{i}", [128, 512], BF16) for i in range(8)]
    t1 = ws.get("sh_t1", [128, 512], BF16)
    x1t = ws.get("sh_x1t", [128, 512], F32)
    x2t = ws.get("sh_x2t", [128, 512], F32)
    for hgrp in range(16):
        w1t = load_wblock(
            P, ws, "gw0", 0,
            io["sw1T"][0:DM, hgrp * 512:(hgrp + 1) * 512].rearrange(
                "(t p) c -> p t c", p=128),
            "sw1T", ND, True, P.uname("sw1"))
        w2t = load_wblock(
            P, ws, "gw0", 16,
            io["sw2T"][0:DM, hgrp * 512:(hgrp + 1) * 512].rearrange(
                "(t p) c -> p t c", p=128),
            "sw2T", ND, True, P.uname("sw2"))
        oblk = ws.get("lnstOT", [128, 16 * 512], BF16)
        for sg in range(4):
            P.pump(6)
            for b0 in (0, 4, 8, 12):
                P.dma("gpsimd", lambda e, b0=b0, sg=sg: e.dma_start(
                    oblk[0:128, b0 * 512:(b0 + 4) * 512].rearrange(
                        "p (t c) -> p t c", t=4),
                    xln[b0 * 128:(b0 + 4) * 128,
                        sg * 512:(sg + 1) * 512].rearrange("(t p) c -> p t c", p=128)),
                    reads=[f"moe_xln.{sg}.{b0 // 4}"],
                    writes=[f"lnst_o{i}" for i in range(b0, b0 + 4)])
            for ht in range(4):
                pg, kg = ws.psa(ht)
                pu, ku = ws.psa(4 + ht)
                for dt in range(ND):
                    P.op("tensor", lambda e, dt=dt, ht=ht, pg=pg, w=w1t[dt][0]: e.matmul(
                        pg[:], w[:, ht * 128:(ht + 1) * 128], mxl[dt][:],
                        start=(dt == 0), stop=(dt == ND - 1), skip_group_check=True),
                        reads=[w1t[dt][1], f"lnst_o{dt}"], writes=[kg],
                        quiet=(dt % 4 != 3))
                    P.op("tensor", lambda e, dt=dt, ht=ht, pu=pu, w=w2t[dt][0]: e.matmul(
                        pu[:], w[:, ht * 128:(ht + 1) * 128], mxl[dt][:],
                        start=(dt == 0), stop=(dt == ND - 1), skip_group_check=True),
                        reads=[w2t[dt][1], f"lnst_o{dt}"], writes=[ku],
                        quiet=(dt % 4 != 3))
                P.op("scalar", lambda e, pg=pg: e.activation(t1[:], pg[:], AF.Silu),
                     reads=[kg], writes=["sh_t1"])
                P.op("vector", lambda e, pu=pu, ht=ht: e.tensor_tensor(
                    hc[ht][:], t1[:], pu[:], op=ALU.mult),
                    reads=["sh_t1", ku], writes=[f"hc{ht}"])
                P.dma("gpsimd", lambda e, hgrp=hgrp, ht=ht, sg=sg: e.dma_start(
                    hsh[hgrp * 512 + ht * 128:hgrp * 512 + (ht + 1) * 128,
                        sg * 512:(sg + 1) * 512], hc[ht][:]),
                    reads=[f"hc{ht}"], writes=["moe_hsh"], cls="st")
    for og in range(4):
        w3t = load_wblock(
            P, ws, "gwC", 0,
            io["sw3T"][0:32 * 128, og * 512:(og + 1) * 512].rearrange(
                "(t p) c -> p t c", p=128),
            "sw3T", 32, True, P.uname("sw3a"))
        w3t += load_wblock(
            P, ws, "gw0", 0,
            io["sw3T"][32 * 128:64 * 128, og * 512:(og + 1) * 512].rearrange(
                "(t p) c -> p t c", p=128),
            "sw3T", 32, True, P.uname("sw3b"))
        hshb = [ws.get(f"hc{i}", [128, 512], BF16) for i in range(8)]
        hcblk = ws.get("hcT", [128, 8 * 512], BF16)
        for sg in range(4):
            for rt in range(64):
                hb, hk = hshb[rt % 8], f"hc{rt % 8}"
                if rt % 4 == 0:
                    r8 = rt % 8
                    P.dma("scalar", lambda e, rt=rt, sg=sg, r8=r8: e.dma_start(
                        hcblk[0:128, r8 * 512:(r8 + 4) * 512].rearrange(
                            "p (t c) -> p t c", t=4),
                        hsh[rt * 128:(rt + 4) * 128,
                            sg * 512:(sg + 1) * 512].rearrange(
                            "(t p) c -> p t c", p=128)),
                        reads=["moe_hsh"],
                        writes=[f"hc{r8 + i}" for i in range(4)])
                for o in range(4):
                    po, ko = ws.psa(o)
                    P.op("tensor", lambda e, o=o, po=po, hb=hb, w=w3t[rt][0], rt=rt:
                         e.matmul(po[:], w[:, o * 128:(o + 1) * 128], hb[:],
                                  start=(rt == 0), stop=(rt == 63),
                                  skip_group_check=True),
                         reads=[w3t[rt][1], hk], writes=[ko],
                         quiet=not (o == 3))
            for o in range(4):
                po, ko = ws.psa(o)
                orow = og * 512 + o * 128
                P.dma("sync", lambda e, orow=orow, sg=sg: e.dma_start(
                    x1t[:], io["x1"][orow:orow + 128, sg * 512:(sg + 1) * 512]),
                    reads=["x1"], writes=["sh_x1t"])
                P.op("vector", lambda e, po=po: e.tensor_tensor(
                    x2t[:], x1t[:], po[:], op=ALU.add),
                    reads=["sh_x1t", ko], writes=["sh_x2t"])
                P.dma("sync", lambda e, orow=orow, sg=sg: e.dma_start(
                    x2p[orow:orow + 128, sg * 512:(sg + 1) * 512], x2t[:]),
                    reads=["sh_x2t"], writes=["moe_x2p"], cls="st")
    return dict(xln=xln, x2p=x2p, w12b=w12b, w3b=w3b, combT=combT,
                cached=(cache is not None))


def emit_moe_routed(P, ws, io, xln, x2p, w12b, w3b, combT, cache=True):
    ones1 = ws.get("c_ones1", [1, 128], BF16)
    exsel = ws.get("r_exsel", [8, 1024], BF16)
    mxl = [ws.get(f"lnst_o{i}", [128, 512], BF16) for i in range(ND)]
    yr = [ws.get(f"lnst_x{i}", [128, 512], F32) for i in range(16)]
    hcb = [ws.get(f"hc{i}", [128, 512], BF16) for i in range(8)]
    t1b = ws.get("sh_t1", [128, 512], BF16)
    cbc = ws.get("r_x", [128, 512], BF16)
    x2pt = ws.get("sh_x1t", [128, 512], F32)
    x2t = ws.get("sh_x2t", [128, 512], F32)
    oblk = ws.get("lnstOT", [128, 16 * 512], BF16)
    for sg in range(4):
        for b0 in (0, 4, 8, 12):
            P.dma("gpsimd", lambda e, b0=b0, sg=sg: e.dma_start(
                oblk[0:128, b0 * 512:(b0 + 4) * 512].rearrange(
                    "p (t c) -> p t c", t=4),
                xln[b0 * 128:(b0 + 4) * 128,
                    sg * 512:(sg + 1) * 512].rearrange("(t p) c -> p t c", p=128)),
                reads=[f"moe_xln.{sg}.{b0 // 4}"],
                writes=[f"lnst_o{i}" for i in range(b0, b0 + 4)])
        for ex in range(8):
            wcast = not cache
            def ldq(cc, ex=ex):
                return load_wblock(
                    P, ws, "gwC", (cc % 2) * 16,
                    w12b[ex][0:DM, cc * 512:(cc + 1) * 512].rearrange(
                        "(t p) c -> p t c", p=128),
                    "moe_w12b", 16, wcast, P.uname("w12q"))
            w3t = []
            for hcx in range(8):
                w3t += load_wblock(
                    P, ws, "gw0", hcx * 4,
                    w3b[ex][hcx * 128:(hcx + 1) * 128, 0:2048].rearrange(
                        "p (g c) -> p g c", g=4),
                    "moe_w3b", 4, wcast, P.uname("w3l"))
            p_cb, k_cb = ws.psa(4)
            P.op("tensor", lambda e, ex=ex, sg=sg: e.matmul(
                p_cb[:], exsel[:, ex * 128:(ex + 1) * 128],
                combT[:, sg * 512:(sg + 1) * 512],
                start=True, stop=True, skip_group_check=True),
                reads=["r_combT", "r_exsel"], writes=[k_cb])
            P.op("vector", lambda e, p_cb=p_cb: e.tensor_copy(cbc[:], p_cb[:]),
                 reads=[k_cb], writes=["r_x"])
            sh1 = [ws.get(f"sh1_{i}", [128, 512], BF16) for i in range(8)]
            for cc in range(4):
                wq4 = ldq(cc)
                for hq in range(4):
                    hp = (cc % 2) * 4 + hq
                    po, ko = ws.psa(hq)
                    for dt in range(ND):
                        P.op("tensor", lambda e, po=po, w=wq4[dt][0],
                             of=hq * 128, dt=dt: e.matmul(
                                 po[:], w[:, of:of + 128], mxl[dt][:],
                                 start=(dt == 0), stop=(dt == ND - 1),
                                 skip_group_check=True),
                             reads=[wq4[dt][1], f"lnst_o{dt}"],
                             writes=[ko], quiet=(dt % 4 != 3))
                    if cc < 2:
                        P.op("scalar", lambda e, po=po, hp=hp: e.activation(
                            sh1[hp][:], po[:], AF.Silu),
                            reads=[ko], writes=[f"sh1_{hp}"])
                    else:
                        P.op("vector", lambda e, po=po, hp=hp: e.tensor_tensor(
                            hcb[hp][:], sh1[hp][:], po[:], op=ALU.mult),
                            reads=[f"sh1_{hp}", ko], writes=[f"hc{hp}"])
                        P.op("gpsimd", lambda e, hp=hp: e.tensor_tensor(
                            hcb[hp][:], hcb[hp][:], cbc[:], op=ALU.mult),
                            reads=[f"hc{hp}", "r_x"], writes=[f"hc{hp}"])
            for o16 in range(16):
                po, ko = ws.psa(4 + o16 % 4)
                cc, of = o16 // 4, (o16 % 4) * 128
                for hcx in range(8):
                    P.op("tensor", lambda e, po=po, w=w3t[hcx * 4 + cc][0],
                         of=of, hcx=hcx: e.matmul(
                             po[:], w[:, of:of + 128], hcb[hcx][:],
                             start=(hcx == 0), stop=(hcx == 7),
                             skip_group_check=True),
                         reads=[w3t[hcx * 4 + cc][1], f"hc{hcx}"],
                         writes=[ko], quiet=(hcx % 4 != 3))
                if ex == 0:
                    P.op("vector", lambda e, po=po, o16=o16: e.tensor_copy(
                        yr[o16][:], po[:]),
                        reads=[ko], writes=[f"lnst_x{o16}"])
                else:
                    P.op("vector", lambda e, po=po, o16=o16: e.tensor_tensor(
                        yr[o16][:], yr[o16][:], po[:], op=ALU.add),
                        reads=[ko, f"lnst_x{o16}"], writes=[f"lnst_x{o16}"])
        for o16 in range(16):
            P.dma("sync", lambda e, o16=o16, sg=sg: e.dma_start(
                x2pt[:], x2p[o16 * 128:(o16 + 1) * 128, sg * 512:(sg + 1) * 512]),
                reads=["moe_x2p"], writes=["sh_x1t"])
            P.op("vector", lambda e, o16=o16: e.tensor_tensor(
                x2t[:], x2pt[:], yr[o16][:], op=ALU.add),
                reads=["sh_x1t", f"lnst_x{o16}"], writes=["sh_x2t"])
            P.dma("sync", lambda e, o16=o16, sg=sg: e.dma_start(
                io["x2"][o16 * 128:(o16 + 1) * 128, sg * 512:(sg + 1) * 512], x2t[:]),
                reads=["sh_x2t"], writes=["x2"], cls="st")


def build_moe_phase():
    P = Prog()
    ws = WS(P)
    io = dict(
        x1=P.dram("x1", [DM, T], F32, kind="ExternalInput"),
        w12T=P.dram("w12T", [8, DM, 2048], F32, kind="ExternalInput"),
        w3T=P.dram("w3T", [8, 1024, 2048], F32, kind="ExternalInput"),
        rwT=P.dram("rwT", [DM, 8], F32, kind="ExternalInput"),
        ebias=P.dram("ebias", [8], F32, kind="ExternalInput"),
        sw1T=P.dram("sw1T", [DM, 8192], F32, kind="ExternalInput"),
        sw2T=P.dram("sw2T", [DM, 8192], F32, kind="ExternalInput"),
        sw3T=P.dram("sw3T", [8192, DM], F32, kind="ExternalInput"),
        x2=P.dram("x2", [DM, T], F32, kind="ExternalOutput"),
    )
    emit_consts(P, ws)
    d = emit_moe(P, ws, io)
    emit_moe_routed(P, ws, io, d["xln"], d["x2p"], d["w12b"], d["w3b"], d["combT"],
                    cache=d["cached"])
    return P.finalize()


def emit_gla(P, ws, io, x1p, x1p_key, sfin, qhat):
    SCALE = 128.0 ** -0.5
    ones1 = ws.get("c_ones1", [1, 128], BF16)
    xt = [ws.get(f"lnst_x{i}", [128, 512], F32) for i in range(ND)]
    xln = [ws.get(f"lnst_o{i}", [128, 512], BF16) for i in range(ND)]
    xkeys = [f"lnst_x{i}" for i in range(ND)]
    xlnkeys = [f"lnst_o{i}" for i in range(ND)]
    wk = load_wblock(P, ws, "gw0", 0,
                     io["gWkT"][0:DM, 0:512].rearrange("(t p) c -> p t c", p=128),
                     "gWkT", 16, True, P.uname("gwk"))
    wv = load_wblock(P, ws, "gw0", 16,
                     io["gWvT"][0:DM, 0:512].rearrange("(t p) c -> p t c", p=128),
                     "gWvT", 16, True, P.uname("gwv"))
    bqs = []
    for ot in range(16):
        t = ws.get(f"gbq{ot}", [128, 1], F32)
        P.dma("sync", lambda e, ot=ot, t=t: e.dma_start(
            t[:], io["gbq"][ot * 128:(ot + 1) * 128, None]),
            reads=["gbq"], writes=[f"gbq{ot}"])
        P.op("vector", lambda e, t=t: e.tensor_scalar(
            t[:], t[:], SCALE, None, op0=ALU.mult),
            reads=[f"gbq{ot}"], writes=[f"gbq{ot}"])
        bqs.append(t)
    bks = []
    for kt in range(4):
        t = ws.get(f"gbk{kt}", [128, 1], F32)
        P.dma("sync", lambda e, kt=kt, t=t: e.dma_start(
            t[:], io["gbk"][kt * 128:(kt + 1) * 128, None]),
            reads=["gbk"], writes=[f"gbk{kt}"])
        bks.append(t)
    bvrow = ws.get("gbvrow", [1, 512], BF16)
    P.dma("gpsimd", lambda e: e.dma_start(bvrow[:], io["gbv"][None, :]),
          reads=["gbv"], writes=["gbvrow"])
    p_bv, k_bv = ws.psa(4)
    bvbc = ws.get("gbvbc", [128, 512], F32)
    P.op("tensor", lambda e: e.matmul(p_bv[:], ones1[:], bvrow[:],
                                      start=True, stop=True, skip_group_check=True),
         reads=["gbvrow", "c_ones1"], writes=[k_bv])
    P.op("vector", lambda e: e.tensor_copy(bvbc[:], p_bv[:]),
         reads=[k_bv], writes=["gbvbc"])
    maskT = ws.get("c_maskT", [32, 128], BF16)
    P.op("gpsimd", lambda e: e.memset(maskT[:], 0.0), writes=["c_maskT"])
    P.op("gpsimd", lambda e: e.affine_select(
        out=maskT.ap().rearrange("p (j t) -> p j t", j=4),
        in_=maskT.ap().rearrange("p (j t) -> p j t", j=4),
        compare_op=ALU.is_ge, fill=0.0, base=0,
        channel_multiplier=-1, pattern=[[0, 4], [1, 32]]),
        reads=["c_maskT"], writes=["c_maskT"])
    P.op("gpsimd", lambda e: e.memset(maskT[:], 1.0), writes=["c_maskT"])
    P.op("gpsimd", lambda e: e.affine_select(
        out=maskT.ap().rearrange("p (j t) -> p j t", j=4),
        in_=maskT.ap().rearrange("p (j t) -> p j t", j=4),
        compare_op=ALU.is_ge, fill=0.0, base=0,
        channel_multiplier=-1, pattern=[[0, 4], [1, 32]]),
        reads=["c_maskT"], writes=["c_maskT"])
    zero32 = ws.get("c_zero32", [128, 32], F32)
    P.op("vector", lambda e: e.memset(zero32[:], 0.0), writes=["c_zero32"])
    gS = [ws.get(f"gS{kt}", [128, 128], F32) for kt in range(4)]
    gSb = [ws.get(f"gSb{kt}", [128, 128], BF16) for kt in range(4)]
    gGr = [ws.get(f"gGr{kt}", [128, 1], F32) for kt in range(4)]
    for kt in range(4):
        P.op("vector", lambda e, kt=kt: e.memset(gS[kt][:], 0.0), writes=[f"gS{kt}"])
        P.op("vector", lambda e, kt=kt: e.memset(gSb[kt][:], 0.0), writes=[f"gSb{kt}"])
        P.op("vector", lambda e, kt=kt: e.memset(gGr[kt][:], 1.0), writes=[f"gGr{kt}"])
    gqk = [ws.get(f"gqk{kt}", [128, 2048], BF16) for kt in range(4)]
    gkt_ = [ws.get(f"gkt{kt}", [128, 512], BF16) for kt in range(4)]
    gkf = ws.get("gkf", [128, 512], F32)
    gg = ws.get("gg", [128, 512], F32)
    gB = ws.get("gB", [128, 512], F32)
    gBr = ws.get("gBr", [128, 512], F32)
    geP = ws.get("geP", [128, 512], BF16)
    geN = ws.get("geN", [128, 512], BF16)
    gKh = ws.get("gKh", [128, 512], F32)
    g15 = [ws.get(f"g15_{kt}", [128, 16], F32) for kt in range(4)]
    ge31 = ws.get("ge31", [128, 16], F32)
    gdec = [ws.get(f"gdec{kt}", [128, 16], F32) for kt in range(4)]
    gva = ws.get("gva", [64, 2048], BF16)
    gvb = ws.get("gvb", [64, 2048], BF16)
    gka = ws.get("gka", [64, 2048], BF16)
    gkb = ws.get("gkb", [64, 2048], BF16)
    gqG_ = [ws.get(f"gqG{k}", [128, 128], BF16) for k in range(4)]
    gPt_ = [ws.get(f"gPt{k}", [64, 128], BF16) for k in range(4)]
    gGm_ = [ws.get(f"gGm{k}", [128, 1], F32) for k in range(4)]
    gQc = [ws.get(f"gQc{i}", [128, 128], BF16) for i in range(4)]
    identf = ws.get("c_identf", [128, 128], F32)
    P.op("gpsimd", lambda e: e.memset(identf[:], 0.0), writes=["c_identf"])
    P.op("gpsimd", lambda e: e.affine_select(
        out=identf[:], in_=identf[:], compare_op=ALU.not_equal, fill=1.0,
        base=0, pattern=[[-1, 128]], channel_multiplier=1),
        reads=["c_identf"], writes=["c_identf"])

    xblk = ws.get("lnstXT", [128, 16 * 512], F32)
    for sg in range(4):
        for b0 in (0, 4, 8, 12):
            P.dma("sync", lambda e, b0=b0, sg=sg: e.dma_start(
                xblk[0:128, b0 * 512:(b0 + 4) * 512].rearrange(
                    "p (t c) -> p t c", t=4),
                io["x0"][b0 * 128:(b0 + 4) * 128,
                         sg * 512:(sg + 1) * 512].rearrange("(t p) c -> p t c", p=128)),
                reads=["x0"], writes=xkeys[b0:b0 + 4])
        emit_ln(P, ws, xt, xkeys, xln, xlnkeys, 512)
        for half in range(2):
            wq = load_wblock(
                P, ws, "gwC", 0,
                io["gWqT"][0:DM, half * 1024:(half + 1) * 1024].rearrange(
                    "(t p) c -> p t c", p=128),
                "gWqT", 32, True, P.uname("gwq"))
            for ot in range(half * 8, half * 8 + 8):
                po, ko = ws.psa(ot % 4)
                ccw = (ot // 4) % 2
                for dt in range(ND):
                    P.op("tensor", lambda e, po=po, w=wq[dt * 2 + ccw][0],
                         of=(ot % 4) * 128, dt=dt: e.matmul(
                             po[:], w[:, of:of + 128], xln[dt][:],
                             start=(dt == 0), stop=(dt == ND - 1),
                             skip_group_check=True),
                         reads=[wq[dt * 2 + ccw][1], xlnkeys[dt]], writes=[ko],
                         quiet=(dt % 4 != 3))
                kt, j = ot // 4, ot % 4
                P.op("scalar", lambda e, po=po, kt=kt, j=j, b=bqs[ot]: e.activation(
                    gqk[kt][:, j * 512:(j + 1) * 512], po[:], AF.Relu,
                    bias=b[:], scale=SCALE),
                    reads=[ko, f"gbq{ot}"], writes=[f"gqk{kt}"])
                P.pump(8)
        for kt in range(4):
            po, ko = ws.psa(4 + kt % 4)
            for dt in range(ND):
                P.op("tensor", lambda e, po=po, w=wk[dt][0], of=kt * 128, dt=dt:
                     e.matmul(po[:], w[:, of:of + 128], xln[dt][:],
                              start=(dt == 0), stop=(dt == ND - 1),
                              skip_group_check=True),
                     reads=[wk[dt][1], xlnkeys[dt]], writes=[ko],
                     quiet=(dt % 4 != 3))
            P.op("scalar", lambda e, po=po, b=bks[kt]: e.activation(
                gkf[:], po[:], AF.Sigmoid, bias=b[:]),
                reads=[ko, f"gbk{kt}"], writes=["gkf"])
            P.op("vector", lambda e: e.tensor_scalar(gkf[:], gkf[:], 0.95, None,
                                                     op0=ALU.min),
                 reads=["gkf"], writes=["gkf"])
            P.op("vector", lambda e: e.tensor_scalar(gg[:], gkf[:], -1.0, 1.0,
                                                     op0=ALU.mult, op1=ALU.add),
                 reads=["gkf"], writes=["gg"])
            P.op("scalar", lambda e: e.activation(gg[:], gg[:], AF.Ln),
                 reads=["gg"], writes=["gg"])
            for c in range(16):
                P.op("vector", lambda e, c=c: e.tensor_tensor_scan(
                    gB[:, c * 32:(c + 1) * 32], gg[:, c * 32:(c + 1) * 32],
                    zero32[:], 0.0, op0=ALU.add, op1=ALU.add),
                    reads=["gg", "c_zero32"], writes=["gB"])
            for c in range(16):
                P.op("vector", lambda e, c=c: e.tensor_scalar(
                    gBr[:, c * 32:(c + 1) * 32], gB[:, c * 32:(c + 1) * 32],
                    gB[:, c * 32 + 15:c * 32 + 16], None, op0=ALU.subtract),
                    reads=["gB"], writes=["gBr"])
            P.op("scalar", lambda e: e.activation(geP[:], gBr[:], AF.Exp),
                 reads=["gBr"], writes=["geP"])
            P.op("scalar", lambda e: e.activation(geN[:], gBr[:], AF.Exp, scale=-1.0),
                 reads=["gBr"], writes=["geN"])
            P.op("scalar", lambda e, kt=kt: e.activation(
                g15[kt][:], gB[:, 15::32], AF.Exp),
                reads=["gB"], writes=[f"g15_{kt}"])
            P.op("scalar", lambda e, kt=kt: e.activation(
                ge31[:], gBr[:, 31::32], AF.Exp),
                reads=["gBr"], writes=["ge31"])
            P.op("vector", lambda e, kt=kt: e.tensor_tensor(
                gdec[kt][:], ge31[:], g15[kt][:], op=ALU.mult),
                reads=["ge31", f"g15_{kt}"], writes=[f"gdec{kt}"])
            P.op("vector", lambda e, kt=kt: e.tensor_tensor(
                gkt_[kt][:], gkf[:], geN[:], op=ALU.mult),
                reads=["gkf", "geN"], writes=[f"gkt{kt}"])
            for c in range(16):
                P.op("vector", lambda e, kt=kt, c=c: e.tensor_scalar(
                    gKh[:, c * 32:(c + 1) * 32], gkt_[kt][:, c * 32:(c + 1) * 32],
                    ge31[:, c:c + 1], None, op0=ALU.mult),
                    reads=[f"gkt{kt}", "ge31"], writes=["gKh"])
            for j in range(4):
                P.op("vector", lambda e, kt=kt, j=j: e.tensor_tensor(
                    gqk[kt][:, j * 512:(j + 1) * 512],
                    gqk[kt][:, j * 512:(j + 1) * 512], geP[:], op=ALU.mult),
                    reads=[f"gqk{kt}", "geP"], writes=[f"gqk{kt}"])
            for tb in range(4):
                ptr, ktr = ws.psa(kt % 4)
                P.op("tensor", lambda e, ptr=ptr, tb=tb: e.transpose(
                    ptr[:, 0:128], gKh[:, tb * 128:(tb + 1) * 128], identf[:]),
                    reads=["gKh", "c_identf"], writes=[ktr])
                dst_a = gka if True else None
                P.op("vector", lambda e, ptr=ptr, tb=tb, kt=kt: e.tensor_copy(
                    gka[0:64, tb * 512 + kt * 128:tb * 512 + (kt + 1) * 128],
                    ptr[0:64, 0:128]),
                    reads=[ktr], writes=["gka"])
                P.op("vector", lambda e, ptr=ptr, tb=tb, kt=kt: e.tensor_copy(
                    gkb[0:64, tb * 512 + kt * 128:tb * 512 + (kt + 1) * 128],
                    ptr[64:128, 0:128]),
                    reads=[ktr], writes=["gkb"])
        for tb in range(4):
            po, ko = ws.psa(4 + tb % 4)
            for dt in range(ND):
                P.op("tensor", lambda e, po=po, tb=tb, dt=dt, w=wv[dt][0]: e.matmul(
                    po[:], xln[dt][:, tb * 128:(tb + 1) * 128], w[:],
                    start=(dt == 0), stop=(dt == ND - 1), skip_group_check=True),
                    reads=[wv[dt][1], xlnkeys[dt]], writes=[ko],
                    quiet=(dt % 4 != 3))
            P.op("vector", lambda e, po=po, tb=tb: e.scalar_tensor_tensor(
                gva[0:64, tb * 512:(tb + 1) * 512], po[0:64, :], 1.0, bvbc[0:64, :],
                op0=ALU.mult, op1=ALU.add),
                reads=[ko, "gbvbc"], writes=["gva"])
            P.op("vector", lambda e, po=po, tb=tb: e.scalar_tensor_tensor(
                gvb[0:64, tb * 512:(tb + 1) * 512], po[64:128, :], 1.0, bvbc[64:128, :],
                op0=ALU.mult, op1=ALU.add),
                reads=[ko, "gbvbc"], writes=["gvb"])
        for c in range(16):
            P.pump(6)
            gc = sg * 16 + c
            tb, q = c // 4, c % 4
            vsrc, vkey = (gva, "gva") if q < 2 else (gvb, "gvb")
            ksrc, kkey = (gka, "gka") if q < 2 else (gkb, "gkb")
            roff = (q % 2) * 32
            for kt in range(4):
                qstr = gqk[kt].ap().rearrange("p (j t) -> p j t", t=512)[
                    :, :, c * 32:(c + 1) * 32]
                pP, kP = ws.psa(5)
                pO, kO = ws.psa(6)
                pD, kD = ws.psa(7)
                P.op("tensor", lambda e, pP=pP, kt=kt, c=c, qstr=qstr: e.matmul(
                    pP[0:32, 0:128], gkt_[kt][:, c * 32:(c + 1) * 32], qstr,
                    start=True, stop=True, skip_group_check=True),
                    reads=[f"gkt{kt}", f"gqk{kt}"], writes=[kP])
                P.op("vector", lambda e, pP=pP, roff=roff, kt=kt: e.tensor_tensor(
                    gPt_[kt][roff:roff + 32, :], pP[0:32, 0:128], maskT[:], op=ALU.mult),
                    reads=[kP, "c_maskT"], writes=[f"gPt{kt}"])
                P.op("vector", lambda e, kt=kt, c=c, qstr=qstr: e.tensor_scalar(
                    gqG_[kt].ap().rearrange("p (j t) -> p j t", t=32), qstr,
                    g15[kt][:, c:c + 1], None, op0=ALU.mult),
                    reads=[f"gqk{kt}", f"g15_{kt}"], writes=[f"gqG{kt}"])
                P.op("tensor", lambda e, pO=pO, kt=kt: e.matmul(
                    pO[:, 0:128], gSb[kt][:], gqG_[kt][:],
                    start=True, stop=False, skip_group_check=True),
                    reads=[f"gSb{kt}", f"gqG{kt}"], writes=[kO], quiet=True)
                P.op("tensor", lambda e, pO=pO, kt=kt, vsrc=vsrc, roff=roff,
                     tb=tb: e.matmul(
                         pO[:, 0:128],
                         vsrc[roff:roff + 32, tb * 512 + kt * 128:tb * 512 + (kt + 1) * 128],
                         gPt_[kt][roff:roff + 32, :], start=False, stop=True,
                         skip_group_check=True),
                     reads=[vkey, f"gPt{kt}"], writes=[kO])
                P.op("tensor", lambda e, pD=pD, kt=kt, ksrc=ksrc, vsrc=vsrc,
                     roff=roff, tb=tb: e.matmul(
                         pD[:, 0:128],
                         ksrc[roff:roff + 32, tb * 512 + kt * 128:tb * 512 + (kt + 1) * 128],
                         vsrc[roff:roff + 32, tb * 512 + kt * 128:tb * 512 + (kt + 1) * 128],
                         start=True, stop=True, skip_group_check=True),
                     reads=[kkey, vkey], writes=[kD])
                P.op("vector", lambda e, kt=kt, c=c, pD=pD: e.scalar_tensor_tensor(
                    gS[kt][:], gS[kt][:], gdec[kt][:, c:c + 1], pD[:, 0:128],
                    op0=ALU.mult, op1=ALU.add),
                    reads=[f"gS{kt}", f"gdec{kt}", kD], writes=[f"gS{kt}"])
                P.op("vector", lambda e, kt=kt: e.tensor_copy(gSb[kt][:], gS[kt][:]),
                     reads=[f"gS{kt}"], writes=[f"gSb{kt}"])
                P.op("vector", lambda e, kt=kt, c=c: e.tensor_tensor(
                    gGm_[kt][:], gGr[kt][:], g15[kt][:, c:c + 1], op=ALU.mult),
                    reads=[f"gGr{kt}", f"g15_{kt}"], writes=[f"gGm{kt}"])
                qc, qck = gQc[kt], f"gQc{kt}"
                P.op("vector", lambda e, kt=kt, c=c, qstr=qstr, qc=qc: e.tensor_scalar(
                    qc.ap().rearrange("p (j t) -> p j t", t=32), qstr,
                    gGm_[kt][:], None, op0=ALU.mult),
                    reads=[f"gqk{kt}", f"gGm{kt}"], writes=[qck])
                P.dma("sync", lambda e, kt=kt, gc=gc, qc=qc: e.dma_start(
                    qhat[kt * 128:(kt + 1) * 128, :].rearrange(
                        "p (j t) -> p j t", j=4)[:, :, gc * 32:(gc + 1) * 32],
                    qc.ap().rearrange("p (j t) -> p j t", t=32)),
                    reads=[qck], writes=["qhat"], cls="st")
                P.op("vector", lambda e, kt=kt, c=c: e.tensor_scalar(
                    gGr[kt][:], gGr[kt][:], gdec[kt][:, c:c + 1], None, op0=ALU.mult),
                    reads=[f"gGr{kt}", f"gdec{kt}"], writes=[f"gGr{kt}"])
                for j in range(4):
                    h = kt * 4 + j
                    P.op("vector", lambda e, pO=pO, h=h, j=j, c=c: e.tensor_tensor(
                        xt[h][:, c * 32:(c + 1) * 32], xt[h][:, c * 32:(c + 1) * 32],
                        pO[:, j * 32:(j + 1) * 32], op=ALU.add),
                        reads=[kO, xkeys[h]], writes=[xkeys[h]])
        for i in range(ND):
            P.dma("sync", lambda e, i=i, sg=sg: e.dma_start(
                x1p[i * 128:(i + 1) * 128, sg * 512:(sg + 1) * 512], xt[i][:]),
                reads=[xkeys[i]], writes=[x1p_key])
    for kt in range(4):
        P.dma("sync", lambda e, kt=kt: e.dma_start(
            sfin[kt * 128:(kt + 1) * 128, :], gS[kt][:]),
            reads=[f"gS{kt}"], writes=["sfin"], cls="st")


def emit_glacorr(P, ws, io, x1p, x1p_key, qhat, qhat_key, xout, xout_key):
    sin = [ws.get(f"ci_S{kt}", [128, 128], F32) for kt in range(4)]
    sinb = [ws.get(f"ci_Sb{kt}", [128, 128], BF16) for kt in range(4)]
    hm = ws.get("ci_hm", [128, 1], F32)
    qh = [ws.get(f"ci_qh{i}", [128, 512], BF16) for i in range(4)]
    xg = [ws.get(f"ci_x{i}", [128, 512], F32) for i in range(2)]
    xo = [ws.get(f"ci_xo{i}", [128, 512], F32) for i in range(2)]
    P.dma("sync", lambda e: e.dma_start(hm[:], io["hmask"]),
          reads=["hmask"], writes=["ci_hm"])
    for kt in range(4):
        P.dma("sync", lambda e, kt=kt: e.dma_start(
            sin[kt][:], io["sin"][kt * 128:(kt + 1) * 128, :]),
            reads=["sin"], writes=[f"ci_S{kt}"])
        P.op("vector", lambda e, kt=kt: e.tensor_scalar(
            sin[kt][:], sin[kt][:], hm[:], None, op0=ALU.mult),
            reads=[f"ci_S{kt}", "ci_hm"], writes=[f"ci_S{kt}"])
        P.op("vector", lambda e, kt=kt: e.tensor_copy(sinb[kt][:], sin[kt][:]),
             reads=[f"ci_S{kt}"], writes=[f"ci_Sb{kt}"])
    for kt in range(4):
        for j in range(4):
            h = kt * 4 + j
            for tc in range(4):
                qb, qk_ = qh[tc % 4], f"ci_qh{tc % 4}"
                P.dma("gpsimd", lambda e, kt=kt, j=j, tc=tc, qb=qb: e.dma_start(
                    qb[:], qhat[kt * 128:(kt + 1) * 128,
                                j * 2048 + tc * 512:j * 2048 + (tc + 1) * 512]),
                    reads=[qhat_key], writes=[qk_])
                po, ko = ws.psa(tc % 4)
                P.op("tensor", lambda e, po=po, kt=kt, qb=qb: e.matmul(
                    po[:], sinb[kt][:], qb[:],
                    start=True, stop=True, skip_group_check=True),
                    reads=[f"ci_Sb{kt}", qk_], writes=[ko])
                xa, xak = xg[tc % 2], f"ci_x{tc % 2}"
                xb_, xbk = xo[tc % 2], f"ci_xo{tc % 2}"
                P.dma("sync", lambda e, h=h, tc=tc, xa=xa: e.dma_start(
                    xa[:], x1p[h * 128:(h + 1) * 128, tc * 512:(tc + 1) * 512]),
                    reads=[x1p_key], writes=[xak])
                P.op("vector", lambda e, po=po, xa=xa, xb_=xb_: e.tensor_tensor(
                    xb_[:], xa[:], po[:], op=ALU.add),
                    reads=[ko, xak], writes=[xbk])
                P.dma("sync", lambda e, h=h, tc=tc, xb_=xb_: e.dma_start(
                    xout[h * 128:(h + 1) * 128, tc * 512:(tc + 1) * 512], xb_[:]),
                    reads=[xbk], writes=[xout_key], cls="st")


def build_gla_phase():
    P = Prog()
    ws = WS(P)
    io = dict(
        x0=P.dram("x0", [DM, T], F32, kind="ExternalInput"),
        gWqT=P.dram("gWqT", [DM, DM], F32, kind="ExternalInput"),
        gbq=P.dram("gbq", [DM], F32, kind="ExternalInput"),
        gWkT=P.dram("gWkT", [DM, 512], F32, kind="ExternalInput"),
        gbk=P.dram("gbk", [512], F32, kind="ExternalInput"),
        gWvT=P.dram("gWvT", [DM, 512], F32, kind="ExternalInput"),
        gbv=P.dram("gbv", [512], F32, kind="ExternalInput"),
    )
    x1p = P.dram("x1p", [DM, T], F32, kind="ExternalOutput")
    sfin = P.dram("sfin", [512, 128], F32, kind="ExternalOutput")
    qhat = P.dram("qhat", [512, 8192], BF16, kind="ExternalOutput")
    emit_consts(P, ws)
    emit_gla(P, ws, io, x1p, "x1p", sfin, qhat)
    return P.finalize()


def build_glacorr_phase():
    P = Prog()
    ws = WS(P)
    io = dict(
        hmask=P.dram("hmask", [128, 1], F32, kind="ExternalInput"),
        sin=P.dram("sin", [512, 128], F32, kind="ExternalInput"),
    )
    x1p = P.dram("x1p", [DM, T], F32, kind="ExternalInput")
    qhat = P.dram("qhat", [512, 8192], BF16, kind="ExternalInput")
    x1 = P.dram("x1", [DM, T], F32, kind="ExternalOutput")
    emit_consts(P, ws)
    emit_glacorr(P, ws, io, x1p, "x1p", qhat, "qhat", x1, "x1")
    return P.finalize()


def emit_attn_proj(P, ws, io, qT, qT_key, qn2d, qn2_key):
    SC = 128 ** -0.5
    ones1 = ws.get("c_ones1", [1, 128], BF16)
    onesK = ws.get("c_onesK", [128, 1], BF16)
    xln = P.dram("attn_xln", [DM, T], BF16)
    emit_ln_stream(P, ws, "attnln", io["x2"], "x2", xln, "attn_xln")

    xlt = [ws.get(f"lnst_o{i}", [128, 512], BF16) for i in range(ND)]
    wk = load_wblock(P, ws, "gw0", 0,
                     io["aWkT"][0:DM, 0:512].rearrange("(t p) c -> p t c", p=128),
                     "aWkT", 16, True, P.uname("awk"))
    wv = load_wblock(P, ws, "gw0", 16,
                     io["aWvT"][0:DM, 0:512].rearrange("(t p) c -> p t c", p=128),
                     "aWvT", 16, True, P.uname("awv"))
    q2b = ws.get("a_q2", [128, 512], BF16)
    aks = ws.get("a_ks", [128, 512], BF16)
    nrow = ws.get("a_nrow", [1, 512], F32)
    oblk = ws.get("lnstOT", [128, 16 * 512], BF16)
    for sg in range(4):
        for b0 in (0, 4, 8, 12):
            P.dma("sync", lambda e, b0=b0, sg=sg: e.dma_start(
                oblk[0:128, b0 * 512:(b0 + 4) * 512].rearrange(
                    "p (t c) -> p t c", t=4),
                xln[b0 * 128:(b0 + 4) * 128,
                    sg * 512:(sg + 1) * 512].rearrange("(t p) c -> p t c", p=128)),
                reads=[f"attn_xln.{sg}.{b0 // 4}"],
                writes=[f"lnst_o{i}" for i in range(b0, b0 + 4)])
        for half in range(2):
            wq = load_wblock(
                P, ws, "gwC", 0,
                io["aWqT"][0:DM, half * 1024:(half + 1) * 1024].rearrange(
                    "(t p) c -> p t c", p=128),
                "aWqT", 32, True, P.uname("awq"))
            for ot in range(half * 8, half * 8 + 8):
                po, ko = ws.psa(ot % 4)
                ccw = (ot // 4) % 2
                for dt in range(ND):
                    P.op("tensor", lambda e, po=po, w=wq[dt * 2 + ccw][0],
                         of=(ot % 4) * 128, dt=dt: e.matmul(
                             po[:], w[:, of:of + 128], xlt[dt][:],
                             start=(dt == 0), stop=(dt == ND - 1),
                             skip_group_check=True),
                         reads=[wq[dt * 2 + ccw][1], f"lnst_o{dt}"], writes=[ko],
                         quiet=(dt % 4 != 3))
                P.op("scalar", lambda e, po=po: e.activation(q2b[:], po[:], AF.Copy,
                                                             scale=SC),
                     reads=[ko], writes=["a_q2"])
                P.dma("sync", lambda e, ot=ot, sg=sg: e.dma_start(
                    qT[ot * 128:(ot + 1) * 128, sg * 512:(sg + 1) * 512], q2b[:]),
                    reads=["a_q2"], writes=[qT_key], cls="st")
                P.op("scalar", lambda e: e.activation(q2b[:], q2b[:], AF.Square),
                     reads=["a_q2"], writes=["a_q2"])
                pn, kn = ws.psa(4 + ot % 2)
                P.op("tensor", lambda e, pn=pn: e.matmul(
                    pn[0:1, :], onesK[:], q2b[:], start=True, stop=True,
                    skip_group_check=True),
                    reads=["a_q2", "c_onesK"], writes=[kn])
                P.op("vector", lambda e, pn=pn: e.tensor_copy(nrow[:], pn[0:1, :]),
                     reads=[kn], writes=["a_nrow"])
                P.dma("sync", lambda e, ot=ot, sg=sg: e.dma_start(
                    qn2d[ot, sg * 512:(sg + 1) * 512][None, :], nrow[:]),
                    reads=["a_nrow"], writes=[qn2_key], cls="st")
        for kt in range(4):
            po, ko = ws.psa(4 + kt % 2)
            for dt in range(ND):
                P.op("tensor", lambda e, po=po, w=wk[dt][0], of=kt * 128, dt=dt:
                     e.matmul(po[:], w[:, of:of + 128], xlt[dt][:],
                              start=(dt == 0), stop=(dt == ND - 1),
                              skip_group_check=True),
                     reads=[wk[dt][1], f"lnst_o{dt}"], writes=[ko],
                     quiet=(dt % 4 != 3))
            P.op("scalar", lambda e, po=po: e.activation(aks[:], po[:], AF.Copy),
                 reads=[ko], writes=["a_ks"])
            P.dma("sync", lambda e, kt=kt, sg=sg: e.dma_start(
                io["kown"][kt * 128:(kt + 1) * 128, sg * 512:(sg + 1) * 512], aks[:]),
                reads=["a_ks"], writes=["kown"], cls="st")
        for tb in range(4):
            po, ko = ws.psa(6 + tb % 2)
            for dt in range(ND):
                P.op("tensor", lambda e, po=po, tb=tb, dt=dt, w=wv[dt][0]: e.matmul(
                    po[:], xlt[dt][:, tb * 128:(tb + 1) * 128], w[:],
                    start=(dt == 0), stop=(dt == ND - 1), skip_group_check=True),
                    reads=[wv[dt][1], f"lnst_o{dt}"], writes=[ko],
                    quiet=(dt % 4 != 3))
            P.op("scalar", lambda e, po=po: e.activation(q2b[:], po[:], AF.Copy),
                 reads=[ko], writes=["a_q2"])
            P.dma("sync", lambda e, tb=tb, sg=sg: e.dma_start(
                io["vown"][sg * 512 + tb * 128:sg * 512 + (tb + 1) * 128, :], q2b[:]),
                reads=["a_q2"], writes=["vown"], cls="st")


def emit_attn_core(P, ws, io, qT, qT_key, qn2d, qn2_key, x3, x3_key):
    SC = 128 ** -0.5
    ones1 = ws.get("c_ones1", [1, 128], BF16)
    onesK = ws.get("c_onesK", [128, 1], BF16)
    onesM = ws.get("c_onesM", [128, 128], BF16)
    oT = P.dram("attn_oT", [DM, T], BF16)
    q2b = ws.get("a_q2", [128, 512], BF16)
    aks = ws.get("a_ks", [128, 512], BF16)
    kn2 = ws.get("a_kn2", [1, 512], F32)
    kmax = ws.get("a_kmax", [1, 2], F32)
    P.op("vector", lambda e: e.memset(kmax[:], 0.0), writes=["a_kmax"])
    for src_d, srk in [(io["kown"], "kown"), (io["kpart"], "kpart")]:
        for kt in range(4):
            for sc in range(4):
                P.dma("gpsimd", lambda e, kt=kt, sc=sc, src_d=src_d: e.dma_start(
                    aks[:], src_d[kt * 128:(kt + 1) * 128, sc * 512:(sc + 1) * 512]),
                    reads=[srk], writes=["a_ks"])
                P.op("scalar", lambda e: e.activation(q2b[:], aks[:], AF.Square),
                     reads=["a_ks"], writes=["a_q2"])
                pn, kn = ws.psa(4)
                P.op("tensor", lambda e, pn=pn: e.matmul(
                    pn[0:1, :], onesK[:], q2b[:], start=True, stop=True,
                    skip_group_check=True),
                    reads=["a_q2", "c_onesK"], writes=[kn])
                P.op("vector", lambda e, pn=pn: e.reduce_max(
                    kmax[:, 1:2], pn[0:1, :], axis=AX.X),
                    reads=[kn], writes=["a_kmax"])
                P.op("vector", lambda e: e.tensor_tensor(
                    kmax[:, 0:1], kmax[:, 0:1], kmax[:, 1:2], op=ALU.max),
                    reads=["a_kmax"], writes=["a_kmax"])
    P.op("scalar", lambda e: e.activation(kmax[:, 0:1], kmax[:, 0:1], AF.Sqrt),
         reads=["a_kmax"], writes=["a_kmax"])
    maskTD = ws.get("a_maskTD", [128, 128], BF16)
    P.op("gpsimd", lambda e: e.memset(maskTD[:], 0.0), writes=["a_maskTD"])
    P.op("gpsimd", lambda e: e.affine_select(
        out=maskTD[:], in_=maskTD[:], compare_op=ALU.is_ge, fill=-1e30,
        base=0, channel_multiplier=-1, pattern=[[1, 128]]),
        reads=["a_maskTD"], writes=["a_maskTD"])
    cmask = ws.get("a_cmask", [128, 1], F32)
    P.dma("sync", lambda e: e.dma_start(cmask[:], io["cmask"]),
          reads=["cmask"], writes=["a_cmask"])
    aqt = ws.get("a_qt", [128, T], BF16)
    mrow = ws.get("a_mrow", [1, T], BF16)
    ambc = ws.get("a_mbc", [128, T], BF16)
    aPt = [ws.get(f"a_Pt{i}", [128, 512], BF16) for i in range(4)]
    akv = ws.get("a_kv", [128, 8192], BF16)
    aTt = [ws.get(f"a_Tt{i}", [128, 512], F32) for i in range(2)]
    linv = ws.get("a_linv", [128, 1024], F32)
    aot = ws.get("a_ot", [128, 512], BF16)
    for h in range(16):
        kv = h // 4
        P.dma("sync", lambda e, h=h: e.dma_start(aqt[:], qT[h * 128:(h + 1) * 128, :]),
              reads=[qT_key], writes=["a_qt"])
        if h % 4 == 0:
            P.dma("sync", lambda e, kv=kv: e.dma_start(
                akv[0:128, 0:2048], io["kown"][kv * 128:(kv + 1) * 128, :]),
                reads=["kown"], writes=["akv_ko"])
            P.dma("gpsimd", lambda e, kv=kv: e.dma_start(
                akv[0:128, 2048:4096], io["kpart"][kv * 128:(kv + 1) * 128, :]),
                reads=["kpart"], writes=["akv_kp"])
            for q4 in range(4):
                P.dma("sync", lambda e, kv=kv, q4=q4: e.dma_start(
                    akv[0:128, 4096 + q4 * 512:4096 + (q4 + 1) * 512].rearrange(
                        "p (t c) -> p t c", t=4),
                    io["vown"][q4 * 512:(q4 + 1) * 512,
                               kv * 128:(kv + 1) * 128].rearrange(
                        "(t p) c -> p t c", p=128)),
                    reads=["vown"], writes=["akv_vo"])
                P.dma("gpsimd", lambda e, kv=kv, q4=q4: e.dma_start(
                    akv[0:128, 6144 + q4 * 512:6144 + (q4 + 1) * 512].rearrange(
                        "p (t c) -> p t c", t=4),
                    io["vpart"][q4 * 512:(q4 + 1) * 512,
                                kv * 128:(kv + 1) * 128].rearrange(
                        "(t p) c -> p t c", p=128)),
                    reads=["vpart"], writes=["akv_vp"])
        P.dma("gpsimd", lambda e, h=h: e.dma_start(mrow[:], qn2d[h, :][None, :]),
              reads=[qn2_key], writes=["a_mrow"])
        P.op("scalar", lambda e: e.activation(mrow[:], mrow[:], AF.Sqrt),
             reads=["a_mrow"], writes=["a_mrow"])
        P.op("vector", lambda e: e.tensor_scalar(
            mrow[:], mrow[:], kmax[:, 0:1], 1.01, op0=ALU.mult, op1=ALU.mult),
            reads=["a_mrow", "a_kmax"], writes=["a_mrow"])
        P.op("vector", lambda e: e.tensor_scalar(
            mrow[:], mrow[:], 1.0, None, op0=ALU.add),
            reads=["a_mrow"], writes=["a_mrow"])
        for qc in range(4):
            pb, kb = ws.psa(6 + qc % 2)
            P.op("tensor", lambda e, pb=pb, qc=qc: e.matmul(
                pb[:], ones1[:], mrow[:, qc * 512:(qc + 1) * 512],
                start=True, stop=True, skip_group_check=True),
                reads=["a_mrow", "c_ones1"], writes=[kb])
            P.op("vector", lambda e, pb=pb, qc=qc: e.tensor_copy(
                ambc[:, qc * 512:(qc + 1) * 512], pb[:]),
                reads=[kb], writes=["a_mbc"])
        for qh in range(2):
            q0 = qh * 1024
            plan = []
            for kt in range(16):
                plan.append(("p", kt, q0, 1024, "cross"))
            for kt in range(qh * 8, qh * 8 + 8):
                kq0 = max(kt * 128, q0)
                plan.append(("o", kt, kq0, 128, "diag"))
                if q0 + 1024 - (kq0 + 128) > 0:
                    plan.append(("o", kt, kq0 + 128, q0 + 1024 - (kq0 + 128), "none"))
            if qh == 1:
                for kt in range(0, 8):
                    plan.append(("o", kt, q0, 1024, "none"))
            def subchunks(qs, qn):
                cq = qs
                while cq < qs + qn:
                    nb = q0 + ((cq - q0) // 512 + 1) * 512
                    cw = min(qs + qn, nb) - cq
                    yield cq, cw
                    cq += cw
            lastmm = {}
            for idx, (src, kt, qs, qn, mk) in enumerate(plan):
                for cq0, cw in subchunks(qs, qn):
                    lastmm[(cq0 - q0) // 512] = (idx, cq0)
            def emit_st(idx, entry):
                src, kt, qs, qn, mk = entry
                koff = (0 if src == "o" else 2048) + kt * 128
                voff = (4096 if src == "o" else 6144) + kt * 128
                akb = akv.t if isinstance(akv, APView) else akv
                ak0 = akv.c0 if isinstance(akv, APView) else 0
                ktb = APView(akb, 128, ak0 + koff, 128)
                ktbk = "akv_ko" if src == "o" else "akv_kp"
                vt = APView(akb, 128, ak0 + voff, 128)
                vtk = "akv_vo" if src == "o" else "akv_vp"
                outs = []
                for ci, (cq0, cw) in enumerate(subchunks(qs, qn)):
                    rot = (2 * idx + ci) % 4
                    pS, kS = ws.psa(4 + rot % 2)
                    P.op("tensor", lambda e, pS=pS, ktb=ktb, cq0=cq0,
                         cw=cw: e.matmul(
                             pS[:, 0:cw], ktb[:], aqt[:, cq0:cq0 + cw],
                             start=True, stop=True, skip_group_check=True),
                         reads=[ktbk, "a_qt"], writes=[kS])
                    pt, ptk = aPt[rot], f"a_Pt{rot}"
                    tt, ttk = aTt[rot % 2], f"a_Tt{rot % 2}"
                    if mk == "cross":
                        P.op("vector", lambda e, pS=pS, cq0=cq0, cw=cw, tt=tt:
                             e.scalar_tensor_tensor(
                                 tt[:, 0:cw], pS[:, 0:cw], cmask[:],
                                 ambc[:, cq0:cq0 + cw],
                                 op0=ALU.add, op1=ALU.subtract),
                             reads=[kS, "a_cmask", "a_mbc"], writes=[ttk])
                    elif mk == "diag":
                        P.op("vector", lambda e, pS=pS, cw=cw, tt=tt: e.tensor_tensor(
                            tt[:, 0:cw], pS[:, 0:cw], maskTD[:, 0:cw], op=ALU.add),
                            reads=[kS, "a_maskTD"], writes=[ttk])
                        P.op("vector", lambda e, cq0=cq0, cw=cw, tt=tt: e.tensor_tensor(
                            tt[:, 0:cw], tt[:, 0:cw], ambc[:, cq0:cq0 + cw],
                            op=ALU.subtract),
                            reads=[ttk, "a_mbc"], writes=[ttk])
                    else:
                        P.op("vector", lambda e, pS=pS, cq0=cq0, cw=cw, tt=tt:
                             e.tensor_tensor(
                                 tt[:, 0:cw], pS[:, 0:cw], ambc[:, cq0:cq0 + cw],
                                 op=ALU.subtract),
                             reads=[kS, "a_mbc"], writes=[ttk])
                    P.op("scalar", lambda e, pt=pt, cw=cw, tt=tt: e.activation(
                        pt[:, 0:cw], tt[:, 0:cw], AF.Exp),
                        reads=[ttk], writes=[ptk])
                    outs.append((cq0, cw, pt, ptk))
                return vt, vtk, outs

            def emit_pv(idx, staged):
                vt, vtk, outs = staged
                for cq0, cw, pt, ptk in outs:
                    bank = (cq0 - q0) // 512
                    pO, kO = ws.psa(bank)
                    ob0 = cq0 - q0 - bank * 512
                    first = bank not in seen_first_o
                    seen_first_o.add(bank)
                    last = (lastmm.get(bank) == (idx, cq0))
                    P.op("tensor", lambda e, pO=pO, vt=vt, pt=pt, ob0=ob0, cw=cw,
                         first=first, last=last: e.matmul(
                             pO[:, ob0:ob0 + cw], vt[:], pt[:, 0:cw],
                             start=first, stop=last, skip_group_check=True),
                         reads=[vtk, ptk], writes=[kO], quiet=not last)
                for cq0, cw, pt, ptk in outs:
                    bank = (cq0 - q0) // 512
                    pL, kL = ws.psa(2 + bank)
                    ob0 = cq0 - q0 - bank * 512
                    first = bank not in seen_first_l
                    seen_first_l.add(bank)
                    last = (lastmm.get(bank) == (idx, cq0))
                    P.op("tensor", lambda e, pL=pL, pt=pt, ob0=ob0, cw=cw,
                         first=first, last=last: e.matmul(
                             pL[:, ob0:ob0 + cw], onesM[:], pt[:, 0:cw],
                             start=first, stop=last, skip_group_check=True),
                         reads=["c_onesM", ptk], writes=[kL], quiet=not last)

            seen_first_o = set()
            seen_first_l = set()
            prev = None
            for idx, entry in enumerate(plan):
                staged = emit_st(idx, entry)
                if prev is not None:
                    emit_pv(prev[0], prev[1])
                prev = (idx, staged)
            emit_pv(prev[0], prev[1])
            for bank in range(2):
                pO, kO = ws.psa(bank)
                pL, kL = ws.psa(2 + bank)
                P.op("vector", lambda e, pL=pL, bank=bank: e.tensor_scalar(
                    linv[:, bank * 512:(bank + 1) * 512], pL[:, 0:512],
                    1e-37, None, op0=ALU.max),
                    reads=[kL], writes=["a_linv"])
                P.op("scalar", lambda e, bank=bank: e.activation(
                    linv[:, bank * 512:(bank + 1) * 512],
                    linv[:, bank * 512:(bank + 1) * 512], AF.Ln),
                    reads=["a_linv"], writes=["a_linv"])
                P.op("scalar", lambda e, bank=bank: e.activation(
                    linv[:, bank * 512:(bank + 1) * 512],
                    linv[:, bank * 512:(bank + 1) * 512], AF.Exp, scale=-1.0),
                    reads=["a_linv"], writes=["a_linv"])
                P.op("vector", lambda e, pO=pO, bank=bank: e.tensor_tensor(
                    aot[:], pO[:, 0:512], linv[:, bank * 512:(bank + 1) * 512],
                    op=ALU.mult),
                    reads=[kO, "a_linv"], writes=["a_ot"])
                P.dma("sync", lambda e, h=h, q0=q0, bank=bank: e.dma_start(
                    oT[h * 128:(h + 1) * 128,
                       q0 + bank * 512:q0 + (bank + 1) * 512], aot[:]),
                    reads=["a_ot"],
                    writes=["attn_oT",
                            f"attn_oT.{(q0 + bank * 512) // 512}.{h // 4}"], cls="st")
    P.barrier()
    ws.new_epoch()
    emit_dense_fm(P, ws, "awp", io["aWpT"], "aWpT", io["abp"], "abp",
                  oT, "attn_oT", x3, x3_key, DM, DM, None,
                  res_dram=io["x2"], res_key="x2", out_dtype=F32,
                  in_granular=True)


def build_attnproj_phase():
    P = Prog()
    ws = WS(P)
    io = dict(
        x2=P.dram("x2", [DM, T], F32, kind="ExternalInput"),
        aWqT=P.dram("aWqT", [DM, DM], F32, kind="ExternalInput"),
        aWkT=P.dram("aWkT", [DM, 512], F32, kind="ExternalInput"),
        aWvT=P.dram("aWvT", [DM, 512], F32, kind="ExternalInput"),
        kown=P.dram("kown", [512, T], BF16, kind="ExternalOutput"),
        vown=P.dram("vown", [T, 512], BF16, kind="ExternalOutput"),
    )
    qT = P.dram("qTo", [DM, T], BF16, kind="ExternalOutput")
    qn2d = P.dram("qn2o", [16, T], F32, kind="ExternalOutput")
    emit_consts(P, ws)
    emit_attn_proj(P, ws, io, qT, "qTo", qn2d, "qn2o")
    return P.finalize()


def build_attncore_phase():
    P = Prog()
    ws = WS(P)
    io = dict(
        x2=P.dram("x2", [DM, T], F32, kind="ExternalInput"),
        aWpT=P.dram("aWpT", [DM, DM], F32, kind="ExternalInput"),
        abp=P.dram("abp", [DM], F32, kind="ExternalInput"),
        kown=P.dram("kown", [512, T], BF16, kind="ExternalInput"),
        vown=P.dram("vown", [T, 512], BF16, kind="ExternalInput"),
        kpart=P.dram("kpart", [512, T], BF16, kind="ExternalInput"),
        vpart=P.dram("vpart", [T, 512], BF16, kind="ExternalInput"),
        cmask=P.dram("cmask", [128, 1], F32, kind="ExternalInput"),
    )
    qT = P.dram("qTo", [DM, T], BF16, kind="ExternalInput")
    qn2d = P.dram("qn2o", [16, T], F32, kind="ExternalInput")
    x3 = P.dram("x3", [DM, T], F32, kind="ExternalOutput")
    emit_consts(P, ws)
    emit_attn_core(P, ws, io, qT, "qTo", qn2d, "qn2o", x3, "x3")
    return P.finalize()


_BUILD_CACHE = {}


def _get(name, builder):
    if name not in _BUILD_CACHE:
        _BUILD_CACHE[name] = builder()
    return _BUILD_CACHE[name]


def kernel(**inputs):
    return _kernel_fused(**inputs)


def _kernel_fused(L0_attn_hidden_states, L0_attn_Wq, L0_attn_bq, L0_attn_Wk,
                  L0_attn_bk, L0_attn_Wv, L0_attn_bv, L0_ffn_w12, L0_ffn_w3,
                  L0_ffn_router_w, L0_ffn_expert_bias, L0_ffn_sw1, L0_ffn_sw2,
                  L0_ffn_sw3, L1_attn_Wq, L1_attn_Wk, L1_attn_Wv, L1_attn_Wp,
                  L1_attn_bp, L1_ffn_W1, L1_ffn_b1, L1_ffn_W2, L1_ffn_b2,
                  L1_ffn_W3, L1_ffn_b3):
    C = list(range(NCORE))
    x = np.asarray(L0_attn_hidden_states)
    shards = [np.ascontiguousarray(x[c // 2, (c % 2) * T:(c % 2 + 1) * T, :].T)
              for c in C]
    tr = lambda a: np.ascontiguousarray(np.asarray(a).T)
    f32 = lambda a: np.ascontiguousarray(np.asarray(a), dtype=np.float32)
    Wk_sl = np.concatenate([np.asarray(L1_attn_Wk)[g * 512:g * 512 + 128]
                            for g in range(4)], axis=0)
    Wv_sl = np.concatenate([np.asarray(L1_attn_Wv)[g * 512:g * 512 + 128]
                            for g in range(4)], axis=0)
    wshared = dict(
        gWqT=tr(L0_attn_Wq), gbq=f32(L0_attn_bq),
        gWkT=tr(L0_attn_Wk), gbk=f32(L0_attn_bk),
        gWvT=tr(L0_attn_Wv), gbv=f32(L0_attn_bv),
        w12T=np.ascontiguousarray(np.asarray(L0_ffn_w12).transpose(0, 2, 1)),
        w3T=np.ascontiguousarray(np.asarray(L0_ffn_w3).transpose(0, 2, 1)),
        rwT=tr(L0_ffn_router_w), ebias=f32(L0_ffn_expert_bias),
        sw1T=tr(L0_ffn_sw1), sw2T=tr(L0_ffn_sw2), sw3T=tr(L0_ffn_sw3),
        aWqT=tr(L1_attn_Wq), aWkT=tr(Wk_sl), aWvT=tr(Wv_sl),
        aWpT=tr(L1_attn_Wp), abp=f32(L1_attn_bp),
        W1T=tr(L1_ffn_W1), b1=f32(L1_ffn_b1),
        W2T=tr(L1_ffn_W2), b2=f32(L1_ffn_b2),
        W3T=tr(L1_ffn_W3), b3=f32(L1_ffn_b3))
    in_maps = []
    for c in C:
        h = c % 2
        in_maps.append(dict(
            x0=shards[c],
            hmask=np.full((128, 1), float(h), np.float32),
            cmask=np.full((128, 1), 0.0 if h == 1 else -1e30, np.float32),
            **wshared))
    nc = _get("fused", build_fused)
    res = run_bass_kernel_spmd(nc, in_maps, core_ids=C)
    out = np.empty((B, S, DM), np.float32)
    for c in C:
        out[c // 2, (c % 2) * T:(c % 2 + 1) * T, :] =             res.results[c]["x4"].reshape(DM, T).T
    return out


def _kernel_unfused(L0_attn_hidden_states, L0_attn_Wq, L0_attn_bq, L0_attn_Wk, L0_attn_bk,
           L0_attn_Wv, L0_attn_bv, L0_ffn_w12, L0_ffn_w3, L0_ffn_router_w,
           L0_ffn_expert_bias, L0_ffn_sw1, L0_ffn_sw2, L0_ffn_sw3,
           L1_attn_Wq, L1_attn_Wk, L1_attn_Wv, L1_attn_Wp, L1_attn_bp,
           L1_ffn_W1, L1_ffn_b1, L1_ffn_W2, L1_ffn_b2, L1_ffn_W3, L1_ffn_b3):
    C = list(range(NCORE))
    x = np.asarray(L0_attn_hidden_states)
    shards = [np.ascontiguousarray(x[c // 2, (c % 2) * T:(c % 2 + 1) * T, :].T)
              for c in C]

    tr = lambda a: np.ascontiguousarray(np.asarray(a).T)
    f32 = lambda a: np.ascontiguousarray(np.asarray(a), dtype=np.float32)

    gla_w = dict(gWqT=tr(L0_attn_Wq), gbq=f32(L0_attn_bq),
                 gWkT=tr(L0_attn_Wk), gbk=f32(L0_attn_bk),
                 gWvT=tr(L0_attn_Wv), gbv=f32(L0_attn_bv))
    nc = _get("gla", build_gla_phase)
    r1 = run_bass_kernel_spmd(nc, [dict(x0=shards[c], **gla_w) for c in C],
                              core_ids=C)
    nc = _get("glacorr", build_glacorr_phase)
    in2 = []
    for c in C:
        h = c % 2
        sin = (r1.results[c - 1]["sfin"] if h == 1
               else np.zeros((512, 128), np.float32))
        in2.append(dict(
            hmask=np.full((128, 1), float(h), np.float32),
            sin=np.ascontiguousarray(sin.reshape(512, 128)),
            x1p=r1.results[c]["x1p"].reshape(DM, T),
            qhat=r1.results[c]["qhat"].reshape(512, 8192)))
    r2 = run_bass_kernel_spmd(nc, in2, core_ids=C)
    moe_w = dict(
        w12T=np.ascontiguousarray(np.asarray(L0_ffn_w12).transpose(0, 2, 1)),
        w3T=np.ascontiguousarray(np.asarray(L0_ffn_w3).transpose(0, 2, 1)),
        rwT=tr(L0_ffn_router_w), ebias=f32(L0_ffn_expert_bias),
        sw1T=tr(L0_ffn_sw1), sw2T=tr(L0_ffn_sw2), sw3T=tr(L0_ffn_sw3))
    nc = _get("moe", build_moe_phase)
    r3 = run_bass_kernel_spmd(
        nc, [dict(x1=r2.results[c]["x1"].reshape(DM, T), **moe_w) for c in C],
        core_ids=C)
    Wk_sl = np.concatenate([np.asarray(L1_attn_Wk)[g * 512:g * 512 + 128]
                            for g in range(4)], axis=0)
    Wv_sl = np.concatenate([np.asarray(L1_attn_Wv)[g * 512:g * 512 + 128]
                            for g in range(4)], axis=0)
    ap_w = dict(aWqT=tr(L1_attn_Wq), aWkT=tr(Wk_sl), aWvT=tr(Wv_sl))
    nc = _get("attnproj", build_attnproj_phase)
    r4 = run_bass_kernel_spmd(
        nc, [dict(x2=r3.results[c]["x2"].reshape(DM, T), **ap_w) for c in C],
        core_ids=C)
    ac_w = dict(aWpT=tr(L1_attn_Wp), abp=f32(L1_attn_bp))
    nc = _get("attncore", build_attncore_phase)
    in5 = []
    for c in C:
        h = c % 2
        part = c - 1 if h == 1 else c
        in5.append(dict(
            x2=r3.results[c]["x2"].reshape(DM, T),
            kown=r4.results[c]["kown"].reshape(512, T),
            vown=r4.results[c]["vown"].reshape(T, 512),
            kpart=r4.results[part]["kown"].reshape(512, T),
            vpart=r4.results[part]["vown"].reshape(T, 512),
            qTo=r4.results[c]["qTo"].reshape(DM, T),
            qn2o=r4.results[c]["qn2o"].reshape(16, T),
            cmask=np.full((128, 1), 0.0 if h == 1 else -1e30, np.float32),
            **ac_w))
    r5 = run_bass_kernel_spmd(nc, in5, core_ids=C)
    mlp_w = dict(W1T=tr(L1_ffn_W1), b1=f32(L1_ffn_b1),
                 W2T=tr(L1_ffn_W2), b2=f32(L1_ffn_b2),
                 W3T=tr(L1_ffn_W3), b3=f32(L1_ffn_b3))
    nc = _get("mlp", build_mlp_phase)
    r6 = run_bass_kernel_spmd(
        nc, [dict(x3=r5.results[c]["x3"].reshape(DM, T), **mlp_w) for c in C],
        core_ids=C)
    out = np.empty((B, S, DM), np.float32)
    for c in C:
        out[c // 2, (c % 2) * T:(c % 2 + 1) * T, :] = \
            r6.results[c]["x4"].reshape(DM, T).T
    return out


def build_fused():
    P = Prog()
    ws = WS(P, pooled=True)
    io = dict(
        x0=P.dram("x0", [DM, T], F32, kind="ExternalInput"),
        gWqT=P.dram("gWqT", [DM, DM], F32, kind="ExternalInput"),
        gbq=P.dram("gbq", [DM], F32, kind="ExternalInput"),
        gWkT=P.dram("gWkT", [DM, 512], F32, kind="ExternalInput"),
        gbk=P.dram("gbk", [512], F32, kind="ExternalInput"),
        gWvT=P.dram("gWvT", [DM, 512], F32, kind="ExternalInput"),
        gbv=P.dram("gbv", [512], F32, kind="ExternalInput"),
        hmask=P.dram("hmask", [128, 1], F32, kind="ExternalInput"),
        cmask=P.dram("cmask", [128, 1], F32, kind="ExternalInput"),
        w12T=P.dram("w12T", [8, DM, 2048], F32, kind="ExternalInput"),
        w3T=P.dram("w3T", [8, 1024, 2048], F32, kind="ExternalInput"),
        rwT=P.dram("rwT", [DM, 8], F32, kind="ExternalInput"),
        ebias=P.dram("ebias", [8], F32, kind="ExternalInput"),
        sw1T=P.dram("sw1T", [DM, 8192], F32, kind="ExternalInput"),
        sw2T=P.dram("sw2T", [DM, 8192], F32, kind="ExternalInput"),
        sw3T=P.dram("sw3T", [8192, DM], F32, kind="ExternalInput"),
        aWqT=P.dram("aWqT", [DM, DM], F32, kind="ExternalInput"),
        aWkT=P.dram("aWkT", [DM, 512], F32, kind="ExternalInput"),
        aWvT=P.dram("aWvT", [DM, 512], F32, kind="ExternalInput"),
        aWpT=P.dram("aWpT", [DM, DM], F32, kind="ExternalInput"),
        abp=P.dram("abp", [DM], F32, kind="ExternalInput"),
        W1T=P.dram("W1T", [DM, 4096], F32, kind="ExternalInput"),
        b1=P.dram("b1", [4096], F32, kind="ExternalInput"),
        W2T=P.dram("W2T", [4096, 4096], F32, kind="ExternalInput"),
        b2=P.dram("b2", [4096], F32, kind="ExternalInput"),
        W3T=P.dram("W3T", [4096, DM], F32, kind="ExternalInput"),
        b3=P.dram("b3", [DM], F32, kind="ExternalInput"),
    )
    x1p = P.dram("x1p", [DM, T], F32)
    qhat = P.dram("qhat", [512, 8192], BF16)
    sfin = P.dram("sfin", [512, 128], F32)
    x1 = P.dram("x1", [DM, T], F32)
    x2 = P.dram("x2", [DM, T], F32)
    x3 = P.dram("x3", [DM, T], F32)
    kown = P.dram("kown", [512, T], BF16)
    vown = P.dram("vown", [T, 512], BF16)
    qT = P.dram("qTo", [DM, T], BF16)
    qn2d = P.dram("qn2o", [16, T], F32)
    cc1out = P.dram("cc1out", [1024, 128], F32)
    cc2k = P.dram("cc2k", [1024, T], BF16)
    cc2v = P.dram("cc2v", [2 * T, 512], BF16)
    x4 = P.dram("x4", [DM, T], F32, kind="ExternalOutput")
    io["x1"] = x1
    io["x2"] = x2
    io["kown"] = kown
    io["vown"] = vown
    io["sin"] = cc1out[0:512, :]
    io["kpart"] = cc2k[0:512, :]
    io["vpart"] = cc2v[0:T, :]

    emit_consts(P, ws)
    w12b = P.dram("moe_w12b", [8, DM, 2048], BF16)
    w3b = P.dram("moe_w3b", [8, 1024, 2048], BF16)
    for ex in range(8):
        emit_bf16_cache(P, ws, io["w12T"][ex], "w12T", w12b[ex], "moe_w12b",
                        DM, 2048, "cw12", bg=True)
        emit_bf16_cache(P, ws, io["w3T"][ex], "w3T", w3b[ex], "moe_w3b",
                        1024, 2048, "cw12", bg=True)
    emit_gla(P, ws, io, x1p, "x1p", sfin, qhat)
    P.barrier()
    ws.new_epoch()
    P.op("gpsimd", lambda e: e.collective_compute(
        "AllGather", ALU.bypass,
        replica_groups=[[0, 1], [2, 3], [4, 5], [6, 7]],
        ins=[sfin.opt()], outs=[cc1out.opt()]),
        reads=["sfin"], writes=["sin"])
    emit_glacorr(P, ws, io, x1p, "x1p", qhat, "qhat", x1, "x1")
    P.barrier()
    ws.new_epoch()
    d = emit_moe(P, ws, io, cache=(w12b, w3b))
    P.pump_all()
    emit_moe_routed(P, ws, io, d["xln"], d["x2p"], d["w12b"], d["w3b"], d["combT"],
                    cache=d["cached"])
    P.barrier()
    ws.new_epoch()
    emit_attn_proj(P, ws, io, qT, "qTo", qn2d, "qn2o")
    P.op("gpsimd", lambda e: e.collective_compute(
        "AllGather", ALU.bypass,
        replica_groups=[[0, 1], [2, 3], [4, 5], [6, 7]],
        ins=[kown.opt()], outs=[cc2k.opt()]),
        reads=["kown"], writes=["kpart"])
    P.op("gpsimd", lambda e: e.collective_compute(
        "AllGather", ALU.bypass,
        replica_groups=[[0, 1], [2, 3], [4, 5], [6, 7]],
        ins=[vown.opt()], outs=[cc2v.opt()]),
        reads=["vown"], writes=["vpart"])
    emit_attn_core(P, ws, io, qT, "qTo", qn2d, "qn2o", x3, "x3")
    P.barrier()
    ws.new_epoch()
    emit_mlp(P, ws, dict(x3=x3, W1T=io["W1T"], b1=io["b1"], W2T=io["W2T"],
                         b2=io["b2"], W3T=io["W3T"], b3=io["b3"], x4=x4))
    return P.finalize()


def kernel_fused(inputs_by_core):
    nc = _get("fused", build_fused)
    res = run_bass_kernel_spmd(nc, inputs_by_core, core_ids=list(range(NCORE)))
    return res


# version 19
# speedup vs baseline: 1.0298x; 1.0298x over previous
import numpy as np
from contextlib import ExitStack

import concourse.bass as bass
import concourse.mybir as mybir
from concourse.bass_utils import run_bass_kernel_spmd

F32 = mybir.dt.float32
BF16 = mybir.dt.bfloat16
AF = mybir.ActivationFunctionType
ALU = mybir.AluOpType
AX = mybir.AxisListType

ENGINES = ("sync", "scalar", "vector", "tensor", "gpsimd")

B, S, DM = 4, 4096, 2048
NCORE = 8
T = 2048
NT = T // 128
ND = DM // 128


class Prog:

    def __init__(self):
        self.nc = bass.Bass("TRN2", target_bir_lowering=False)
        self.stack = ExitStack()
        self.ops = {e: [] for e in ENGINES}
        self.exec_cnt = {e: 0 for e in ENGINES}
        self.NSLOT = 10
        self.dma_seq = {}
        self.dma_cnt = {}
        self.res = {}
        self.pending_quiet = {e: [] for e in ENGINES}
        self._name_i = 0
        self.bg = []

    def pump(self, n):
        for _ in range(min(n, len(self.bg))):
            self.bg.pop(0)()

    def pump_all(self):
        self.pump(len(self.bg))

    def uname(self, prefix):
        self._name_i += 1
        return f"{prefix}_{self._name_i}"

    def sbuf(self, name, shape, dtype):
        return self.stack.enter_context(self.nc.sbuf_tensor(name, list(shape), dtype))

    def psum(self, name, shape, dtype=F32):
        return self.stack.enter_context(self.nc.psum_tensor(name, list(shape), dtype))

    def dram(self, name, shape, dtype, kind="Internal", **kw):
        return self.nc.dram_tensor(name, list(shape), dtype, kind=kind, **kw).ap()

    def sem(self, name):
        return self.stack.enter_context(self.nc.semaphore(name))

    def _dep(self, reads, writes):
        waits = {}

        def add(ent):
            if ent is None:
                return
            eng, kind, cnt = ent
            key = (eng, kind)
            if waits.get(key, -1) < cnt:
                waits[key] = cnt

        for r in reads:
            ent = self.res.get(r)
            if ent:
                add(ent.get("w"))
        for w in writes:
            ent = self.res.get(w)
            if ent:
                add(ent.get("w"))
                for rd in ent.get("r", ()):
                    add(rd)
        return waits

    def _register(self, eng, kind, cnt, reads, writes):
        for w in writes:
            self.res[w] = {"w": (eng, kind, cnt), "r": []}
        for r in reads:
            ent = self.res.setdefault(r, {"w": None, "r": []})
            ent["r"].append((eng, kind, cnt))

    def op(self, eng, fn, reads=(), writes=(), quiet=False):
        waits = self._dep(reads, writes)
        if quiet:
            assert eng == "tensor", "quiet ops only on the PE"
            fwd = self.exec_cnt[eng] + 1
            self.pending_quiet[eng] = [True]
            self._register(eng, "exec", fwd, reads, writes)
            self.ops[eng].append((waits, fn, None))
        else:
            self.exec_cnt[eng] += 1
            cnt = self.exec_cnt[eng]
            self.pending_quiet[eng] = []
            self._register(eng, "exec", cnt, reads, writes)
            self.ops[eng].append((waits, fn, ("exec", cnt)))

    def dma(self, eng, fn, reads=(), writes=(), cls="ld"):
        waits = self._dep(reads, writes)
        if cls == "ld":
            slots = (0, 1, 2, 3, 4, 5)
        elif cls == "cvt":
            slots = (8, 9)
        else:
            slots = (6, 7)
        key2 = (eng, cls)
        i = self.dma_seq.get(key2, 0)
        self.dma_seq[key2] = i + 1
        slot = slots[i % len(slots)]
        seq = i // len(slots) + 1
        kind = f"dma{slot}"
        if seq > 1:
            key = (eng, kind)
            if waits.get(key, -1) < 16 * (seq - 1):
                waits[key] = 16 * (seq - 1)
        self.dma_cnt[(eng, slot)] = 16 * seq
        self._register(eng, kind, 16 * seq, reads, writes)
        self.ops[eng].append((waits, fn, (kind, 16 * seq)))

    def barrier(self):
        snapshot = {}
        for e in ENGINES:
            if self.exec_cnt[e]:
                snapshot[(e, "exec")] = self.exec_cnt[e]
        for (e, slot), cnt in self.dma_cnt.items():
            snapshot[(e, f"dma{slot}")] = cnt
        for e in ENGINES:
            self.ops[e].append((dict(snapshot), None, None))
        self.res = {}
        self.pending_quiet = {e: [] for e in ENGINES}

    def finalize(self):
        assert all(not v for v in self.pending_quiet.values()), \
            "quiet ops not covered by a later inc"
        nc = self.nc
        sems = {}
        for e in ENGINES:
            sems[(e, "exec")] = self.stack.enter_context(nc.semaphore(f"sem_x_{e}"))
            for s in range(self.NSLOT):
                if (e, s) in self.dma_cnt:
                    sems[(e, f"dma{s}")] = self.stack.enter_context(
                        nc.semaphore(f"sem_d{s}_{e}"))
        block = self.stack.enter_context(nc.Block())

        final_waits = {}
        for e in ENGINES:
            if self.exec_cnt[e]:
                final_waits[(e, "exec")] = self.exec_cnt[e]
        for (e, slot), cnt in self.dma_cnt.items():
            final_waits[(e, f"dma{slot}")] = cnt

        needed = {}
        for e in ENGINES:
            s = set()
            for e2 in ENGINES:
                for waits, _, _ in self.ops[e2]:
                    c = waits.get((e, "exec"))
                    if c is not None:
                        s.add(c)
            if self.exec_cnt[e]:
                s.add(self.exec_cnt[e])
            needed[e] = sorted(c for c in s if c <= self.exec_cnt[e])

        import bisect

        def remap(e, c):
            i = bisect.bisect_left(needed[e], c)
            assert i < len(needed[e]) and needed[e][i] == c, (e, c)
            return i + 1

        def mk(ename):
            def body(eng):
                watermark = {}
                own_incs = 0
                for waits, fn, inc in self.ops[ename]:
                    for key, cnt in sorted(waits.items()):
                        if key == (ename, "exec") and cnt > own_incs:
                            continue
                        w = remap(key[0], cnt) if key[1] == "exec" else cnt
                        if watermark.get(key, -1) >= w:
                            continue
                        eng.wait_ge(sems[key], w)
                        watermark[key] = w
                    if inc is not None and inc[0] == "exec":
                        own_incs = inc[1]
                    if fn is None:
                        continue
                    inst = fn(eng)
                    if inc is not None:
                        kind, cnt = inc
                        assert inst is not None, "inc op returned no instruction"
                        if kind.startswith("dma"):
                            inst.then_inc(sems[(ename, kind)], 16)
                        elif cnt in needed_set[ename]:
                            inst.then_inc(sems[(ename, kind)], 1)
                if ename == "sync":
                    for key, cnt in sorted(final_waits.items()):
                        w = remap(key[0], cnt) if key[1] == "exec" else cnt
                        eng.wait_ge(sems[key], w)
            return body

        needed_set = {e: set(needed[e]) for e in ENGINES}

        for e in ENGINES:
            getattr(block, e)(mk(e))
        self.stack.close()
        return nc


class APView:

    def __init__(self, t, parts, c0, cols):
        self.t, self.parts, self.c0, self.cols = t, parts, c0, cols
        self.shape = (parts, cols)

    def ap(self):
        return self.t[0:self.parts, self.c0:self.c0 + self.cols]

    def __getitem__(self, key):
        if not isinstance(key, tuple):
            key = (key, slice(None))
        ps, cs = key
        if isinstance(ps, slice):
            p0, p1, pst = ps.indices(self.parts)
            assert pst == 1
        else:
            raise TypeError("int partition index not supported on APView")
        c0, c1, cst = cs.indices(self.cols)
        return self.t[p0:p1, self.c0 + c0:self.c0 + c1:cst] if cst != 1 else             self.t[p0:p1, self.c0 + c0:self.c0 + c1]


_FIXED_PREFIXES = ("lnst", "ln_", "gw0", "gwC", "cw12_", "c_ones", "c_identf")


class WS:
    POOLB_COLS = 25600
    POOLF_COLS = 5120

    def __init__(self, P, pooled=False):
        self.P = P
        self.bufs = {}
        self.pooled = pooled
        self.pool_t = {}
        self.pool_off = {BF16: 0, F32: 0}
        self.epoch_cache = {}

    def _pool(self, dtype):
        if dtype not in self.pool_t:
            if dtype == BF16:
                self.pool_t[dtype] = self.P.sbuf("POOLB", [128, self.POOLB_COLS], BF16)
            else:
                self.pool_t[dtype] = self.P.sbuf("POOLF", [128, self.POOLF_COLS], F32)
        return self.pool_t[dtype]

    def new_epoch(self):
        self.pool_off = {BF16: 0, F32: 0}
        self.epoch_cache = {}

    BLOCKS = {"gw0_b": ("gw0T", 32, BF16), "gwC_b": ("gwCT", 32, BF16),
              "lnst_x": ("lnstXT", 16, F32), "lnst_o": ("lnstOT", 16, BF16),
              "hc": ("hcT", 8, BF16)}

    def get(self, name, shape, dtype, psum=False):
        for pref, (bname, n, bdt) in self.BLOCKS.items():
            if name.startswith(pref) and name[len(pref):].isdigit():
                i = int(name[len(pref):])
                assert i < n and dtype == bdt and tuple(shape) == (128, 512), name
                bt = self.get(bname, [128, n * 512], bdt)
                return APView(bt, 128, i * 512, 512)
        key = name
        if psum or not self.pooled or any(name.startswith(p) for p in _FIXED_PREFIXES):
            if key in self.bufs:
                t, sh, dt = self.bufs[key]
                assert tuple(sh) == tuple(shape) and dt == dtype,                     f"ws {name} shape clash"
                return t
            t = self.P.psum(name, shape, dtype) if psum else                 self.P.sbuf(name, shape, dtype)
            self.bufs[key] = (t, tuple(shape), dtype)
            return t
        if key in self.epoch_cache:
            v, sh, dt = self.epoch_cache[key]
            assert tuple(sh) == tuple(shape) and dt == dtype, f"pool {name} clash"
            return v
        t = self._pool(dtype)
        cols = shape[1]
        off = self.pool_off[dtype]
        cap = self.POOLB_COLS if dtype == BF16 else self.POOLF_COLS
        assert off + cols <= cap,             f"pool {dtype} overflow at {name}: {off}+{cols} > {cap}"
        self.pool_off[dtype] = off + cols
        v = APView(t, shape[0], off, cols)
        self.epoch_cache[key] = (v, tuple(shape), dtype)
        return v

    def get_block(self, name, n_tiles, dtype, cols=512):
        blk = self.get(name, [128, n_tiles * cols], dtype)
        views = [APView(blk.t if isinstance(blk, APView) else blk, 128,
                        (blk.c0 if isinstance(blk, APView) else 0) + i * cols, cols)
                 for i in range(n_tiles)]
        return blk, views

    def psa(self, i):
        return self.get(f"PSA{i}", [128, 512], F32, psum=True), f"PSA{i}"


def emit_consts(P, ws):
    ones1 = ws.get("c_ones1", [1, 128], BF16)
    onesK = ws.get("c_onesK", [128, 1], BF16)
    onesM = ws.get("c_onesM", [128, 128], BF16)
    P.op("vector", lambda e: e.memset(ones1[:], 1.0), writes=["c_ones1"])
    P.op("vector", lambda e: e.memset(onesK[:], 1.0), writes=["c_onesK"])
    P.op("vector", lambda e: e.memset(onesM[:], 1.0), writes=["c_onesM"])


def emit_ln(P, ws, xt, xkeys, xlnt, xlnkeys, seg):
    onesM = ws.get("c_onesM", [128, 128], BF16)
    xb = [ws.get(f"ln_xb{i}", [128, 512], BF16) for i in range(2)]
    sq = [ws.get(f"ln_sq{i}", [128, 512], BF16) for i in range(2)]
    p_sum, k_sum = ws.psa(4)
    p_sq, k_sq = ws.psa(5)
    mbc = ws.get("ln_mbc", [128, 512], F32)
    rbc = ws.get("ln_rbc", [128, 512], F32)
    tmp = [ws.get(f"ln_tmp{i}", [128, 512], F32) for i in range(2)]

    sl = slice(0, seg)
    for i in range(ND):
        b, bk = xb[i % 2], f"ln_xb{i % 2}"
        s, sk2 = sq[i % 2], f"ln_sq{i % 2}"
        P.op("scalar", lambda e, b=b, i=i: e.copy(b[:, sl], xt[i][:, sl]),
             reads=[xkeys[i]], writes=[bk])
        P.op("tensor", lambda e, b=b, i=i: e.matmul(
            p_sum[:, sl], onesM[:], b[:, sl], start=(i == 0), stop=(i == ND - 1),
            skip_group_check=True),
            reads=[bk, "c_onesM"], writes=[k_sum])
        P.op("scalar", lambda e, s=s, i=i: e.activation(s[:, sl], xt[i][:, sl], AF.Square),
             reads=[xkeys[i]], writes=[sk2])
        P.op("tensor", lambda e, s=s, i=i: e.matmul(
            p_sq[:, sl], onesM[:], s[:, sl], start=(i == 0), stop=(i == ND - 1),
            skip_group_check=True),
            reads=[sk2, "c_onesM"], writes=[k_sq])
    P.op("vector", lambda e: e.tensor_scalar(mbc[:, sl], p_sum[:, sl], 1.0 / DM, None,
                                             op0=ALU.mult),
         reads=[k_sum], writes=["ln_mbc"])
    P.op("vector", lambda e: e.tensor_tensor(rbc[:, sl], mbc[:, sl], mbc[:, sl],
                                             op=ALU.mult),
         reads=["ln_mbc"], writes=["ln_rbc"])
    P.op("vector", lambda e: e.scalar_tensor_tensor(
        rbc[:, sl], p_sq[:, sl], 1.0 / DM, rbc[:, sl], op0=ALU.mult, op1=ALU.subtract),
        reads=[k_sq, "ln_rbc"], writes=["ln_rbc"])
    P.op("vector", lambda e: e.tensor_scalar(rbc[:, sl], rbc[:, sl], 1e-5, None,
                                             op0=ALU.add),
         reads=["ln_rbc"], writes=["ln_rbc"])
    P.op("scalar", lambda e: e.activation(rbc[:, sl], rbc[:, sl], AF.Ln),
         reads=["ln_rbc"], writes=["ln_rbc"])
    P.op("scalar", lambda e: e.activation(rbc[:, sl], rbc[:, sl], AF.Exp, scale=-0.5),
         reads=["ln_rbc"], writes=["ln_rbc"])
    for i in range(ND):
        t, tk = tmp[i % 2], f"ln_tmp{i % 2}"
        P.op("gpsimd", lambda e, i=i, t=t: e.tensor_tensor(
            t[:, sl], xt[i][:, sl], mbc[:, sl], op=ALU.subtract),
            reads=[xkeys[i], "ln_mbc"], writes=[tk])
        P.op("vector", lambda e, i=i, t=t: e.tensor_tensor(
            xlnt[i][:, sl], t[:, sl], rbc[:, sl], op=ALU.mult),
            reads=[tk, "ln_rbc"], writes=[xlnkeys[i]])


def load_w_bf16(P, ws, wdram, wkey, row_tiles, col0, cols, parity, conv_eng=None):
    assert cols == 512
    src3 = wdram[0:row_tiles * 128, col0:col0 + 512].rearrange(
        "(t p) c -> p t c", p=128)
    return load_wblock(P, ws, f"gw{parity}", 0, src3, wkey, row_tiles, True,
                       P.uname(f"wb{parity}"))


def load_bias_col(P, ws, bdram, bkey, o0, name):
    t = ws.get(name, [128, 1], F32)
    P.dma("sync", lambda e: e.dma_start(t[:], bdram[o0:o0 + 128, None]),
          reads=[bkey], writes=[name])
    return t


def emit_dense_fm(P, ws, tag, wT_dram, wkey, bias_dram, bias_key, in_dram, in_key,
                  out_dram, out_key, d_in, d_out, act_func,
                  res_dram=None, res_key=None, out_dtype=BF16, nseg=4, seg=512,
                  in_granular=False, out_granular=False):
    OG = 4
    n_din = d_in // 128
    assert n_din <= 32
    n_og = d_out // (OG * 128)
    rhblk, rh = ws.get_block("dn_rhall", 32, BF16)
    resb = ws.get("dn_res", [128, 512], F32) if res_dram is not None else None
    outb = [ws.get(f"dn_out{i}_{'f' if out_dtype==F32 else 'b'}", [128, 512], out_dtype)
            for i in range(2)]
    wts = {}

    def loadw(og):
        if og < n_og and og not in wts:
            wts[og] = load_w_bf16(P, ws, wT_dram, wkey, n_din, og * OG * 128, 512,
                                  "0" if og % 2 == 0 else "C")
        return wts.get(og)

    loadw(0)
    for og in range(n_og):
        wt = loadw(og)
        wts.pop(og - 1, None)
        btiles = []
        for o in range(OG):
            if bias_dram is not None:
                btiles.append(load_bias_col(P, ws, bias_dram, bias_key,
                                            og * OG * 128 + o * 128,
                                            f"dn_bias{og % 2}_{o}"))
        for sg in range(nseg):
            pb = 4 * ((og * nseg + sg) % 2)
            for bc in range(0, n_din, 8):
                bn = min(8, n_din - bc)
                if in_granular:
                    rkeys = [f"{in_key}.{sg}.{b}"
                             for b in range(bc // 4, (bc + bn + 3) // 4)]
                else:
                    rkeys = [in_key]
                P.dma("sync",
                      lambda e, bc=bc, bn=bn, sg=sg: e.dma_start(
                          rhblk[0:128, bc * 512:(bc + bn) * 512].rearrange(
                              "p (t c) -> p t c", t=bn)[:, :, 0:seg],
                          in_dram[bc * 128:(bc + bn) * 128,
                                  sg * seg:(sg + 1) * seg].rearrange(
                              "(t p) c -> p t c", p=128)),
                      reads=rkeys, writes=[f"dn_rh{i}" for i in range(bc, bc + bn)])
            for dt in range(n_din):
                r, rk = rh[dt], f"dn_rh{dt}"
                for o in range(OG):
                    ps, pk = ws.psa(pb + o)
                    P.op("tensor", lambda e, o=o, r=r, ps=ps, w=wt[dt][0], dt=dt: e.matmul(
                        ps[:, 0:seg], w[:, o * 128:(o + 1) * 128], r[:, 0:seg],
                        start=(dt == 0), stop=(dt == n_din - 1),
                        skip_group_check=True),
                        reads=[wt[dt][1], rk], writes=[pk],
                        quiet=not (o == OG - 1 and dt % 4 == 3))
            if sg == 0:
                loadw(og + 1)
            for o in range(OG):
                ps, pk = ws.psa(pb + o)
                orow = og * OG * 128 + o * 128
                ob, obk = outb[o % 2], f"dn_out{o % 2}_{'f' if out_dtype==F32 else 'b'}"
                if res_dram is not None:
                    P.dma("gpsimd", lambda e, sg=sg, orow=orow: e.dma_start(
                        resb[:, 0:seg], res_dram[orow:orow + 128, sg * seg:(sg + 1) * seg]),
                        reads=[res_key], writes=["dn_res"])
                    P.op("vector", lambda e, ps=ps, ob=ob, bt=btiles[o]: e.scalar_tensor_tensor(
                        ob[:, 0:seg], ps[:, 0:seg], bt[:], resb[:, 0:seg],
                        op0=ALU.add, op1=ALU.add),
                        reads=[pk, f"dn_bias{og % 2}_{o}", "dn_res"], writes=[obk])
                else:
                    P.op("scalar", lambda e, ps=ps, ob=ob, bt=btiles[o]: e.activation(
                        ob[:, 0:seg], ps[:, 0:seg], act_func, bias=bt[:]),
                        reads=[pk, f"dn_bias{og % 2}_{o}"], writes=[obk])
                wkeys = [out_key]
                if out_granular:
                    wkeys.append(f"{out_key}.{sg}.{orow // 512}")
                P.dma("sync", lambda e, sg=sg, orow=orow, ob=ob: e.dma_start(
                    out_dram[orow:orow + 128, sg * seg:(sg + 1) * seg], ob[:, 0:seg]),
                    reads=[obk], writes=wkeys, cls="st")


def emit_ln_stream(P, ws, tag, x_dram, x_key, xln_dram, xln_key, nseg=4, seg=512):
    xt = [ws.get(f"lnst_x{i}", [128, 512], F32) for i in range(ND)]
    xlnt = [ws.get(f"lnst_o{i}", [128, 512], BF16) for i in range(ND)]
    xkeys = [f"lnst_x{i}" for i in range(ND)]
    xlnkeys = [f"lnst_o{i}" for i in range(ND)]
    xblk = ws.get("lnstXT", [128, 16 * 512], F32)
    oblk = ws.get("lnstOT", [128, 16 * 512], BF16)
    for sg in range(nseg):
        for b0 in (0, 4, 8, 12):
            P.dma("sync", lambda e, b0=b0, sg=sg: e.dma_start(
                xblk[0:128, b0 * 512:(b0 + 4) * 512].rearrange(
                    "p (t c) -> p t c", t=4)[:, :, 0:seg],
                x_dram[b0 * 128:(b0 + 4) * 128,
                       sg * seg:(sg + 1) * seg].rearrange("(t p) c -> p t c", p=128)),
                reads=[x_key], writes=xkeys[b0:b0 + 4])
        emit_ln(P, ws, xt, xkeys, xlnt, xlnkeys, seg)
        for b0 in (0, 4, 8, 12):
            P.dma("gpsimd", lambda e, b0=b0, sg=sg: e.dma_start(
                xln_dram[b0 * 128:(b0 + 4) * 128,
                         sg * seg:(sg + 1) * seg].rearrange("(t p) c -> p t c", p=128),
                oblk[0:128, b0 * 512:(b0 + 4) * 512].rearrange(
                    "p (t c) -> p t c", t=4)[:, :, 0:seg]),
                reads=xlnkeys[b0:b0 + 4],
                writes=[xln_key, f"{xln_key}.{sg}.{b0 // 4}"], cls="st")


def emit_mlp(P, ws, io):
    xln = P.dram("mlp_xln", [DM, T], BF16)
    h1 = P.dram("mlp_h1", [4096, T], BF16)
    h2 = P.dram("mlp_h2", [4096, T], BF16)
    emit_ln_stream(P, ws, "mlpln", io["x3"], "x3", xln, "mlp_xln")
    emit_dense_fm(P, ws, "mw1", io["W1T"], "W1T", io["b1"], "b1",
                  xln, "mlp_xln", h1, "mlp_h1", DM, 4096, AF.Relu,
                  in_granular=True, out_granular=True)
    emit_dense_fm(P, ws, "mw2", io["W2T"], "W2T", io["b2"], "b2",
                  h1, "mlp_h1", h2, "mlp_h2", 4096, 4096, AF.Relu,
                  in_granular=True, out_granular=True)
    emit_dense_fm(P, ws, "mw3", io["W3T"], "W3T", io["b3"], "b3",
                  h2, "mlp_h2", io["x4"], "x4", 4096, DM, None,
                  res_dram=io["x3"], res_key="x3", out_dtype=F32,
                  in_granular=True)


def build_mlp_phase():
    P = Prog()
    ws = WS(P)
    io = dict(
        x3=P.dram("x3", [DM, T], F32, kind="ExternalInput"),
        W1T=P.dram("W1T", [DM, 4096], F32, kind="ExternalInput"),
        b1=P.dram("b1", [4096], F32, kind="ExternalInput"),
        W2T=P.dram("W2T", [4096, 4096], F32, kind="ExternalInput"),
        b2=P.dram("b2", [4096], F32, kind="ExternalInput"),
        W3T=P.dram("W3T", [4096, DM], F32, kind="ExternalInput"),
        b3=P.dram("b3", [DM], F32, kind="ExternalInput"),
        x4=P.dram("x4", [DM, T], F32, kind="ExternalOutput"),
    )
    emit_consts(P, ws)
    emit_mlp(P, ws, io)
    return P.finalize()


def emit_bf16_cache(P, ws, src, src_key, dst, dst_key, rows, cols, name,
                    conv_eng="scalar", bg=False):
    CW = 1024
    nr, ncol = rows // 128, cols // CW
    for rt in range(nr):
        for cc in range(ncol):
            i = rt * ncol + cc
            sl_in = src[rt * 128:(rt + 1) * 128, cc * CW:(cc + 1) * CW]
            sl_out = dst[rt * 128:(rt + 1) * 128, cc * CW:(cc + 1) * CW]
            def emit_tile(i=i, sl_in=sl_in, sl_out=sl_out):
                _cache_tile(P, ws, i, sl_in, sl_out, src_key, dst_key, name,
                            conv_eng)
            if bg:
                P.bg.append(emit_tile)
            else:
                emit_tile()
    return


def _cache_tile(P, ws, i, sl_in, sl_out, src_key, dst_key, name, conv_eng):
            P.dma("gpsimd",
                  lambda e, sl_in=sl_in, sl_out=sl_out: e.dma_start(sl_out, sl_in),
                  reads=[src_key], writes=[dst_key], cls="cvt")


def load_wblock(P, ws, pool, rt_off, src_ap, src_key, n_tiles, cast, key_tag,
                chunk=8):
    bt = ws.get(f"{pool}_b0", [128, 512], BF16)
    blk = ws.get(ws.BLOCKS[f"{pool}_b"][0],
                 [128, ws.BLOCKS[f"{pool}_b"][1] * 512], BF16)
    g = src_ap.shape[1]
    gc = src_ap.shape[2]
    tpg = gc // 512
    assert g * tpg == n_tiles
    out = []
    for g0 in range(0, g, max(1, chunk // tpg)):
        gn = min(max(1, chunk // tpg), g - g0)
        t0 = rt_off + g0 * tpg
        tn = gn * tpg
        tkeys = [f"{pool}_b{t0 + t}" for t in range(tn)]
        dst = blk[0:128, t0 * 512:(t0 + tn) * 512].rearrange(
            "p (g c) -> p g c", g=gn)
        sl = src_ap[:, g0:g0 + gn, :]
        P.dma("gpsimd" if cast else "sync",
              lambda e, dst=dst, sl=sl: e.dma_start(dst, sl),
              reads=[src_key], writes=tkeys)
        for t in range(tn):
            out.append((APView(blk, 128, (t0 + t) * 512, 512), tkeys[t]))
    return out


def load_wtiles(P, ws, pool, n_tiles, srcs, src_key, bf16_src=True,
                rt_off=0, conv_eng="gpsimd"):
    out = []
    for i in range(n_tiles):
        bt = ws.get(f"{pool}_b{rt_off + i}", [128, 512], BF16)
        bk = f"{pool}_b{rt_off + i}"
        if bf16_src:
            P.dma("sync", lambda e, bt=bt, s=srcs[i]: e.dma_start(bt[:], s),
                  reads=[src_key], writes=[bk])
        else:
            P.dma("gpsimd", lambda e, bt=bt, s=srcs[i]: e.dma_start(bt[:], s),
                  reads=[src_key], writes=[bk])
        out.append((bt, bk))
    return out


def emit_moe(P, ws, io, cache=None):
    xln = P.dram("moe_xln", [DM, T], BF16)
    hsh = P.dram("moe_hsh", [8192, T], BF16)
    x2p = P.dram("moe_x2p", [DM, T], F32)
    ones1 = ws.get("c_ones1", [1, 128], BF16)

    if cache is not None:
        w12b, w3b = cache
    else:
        w12b, w3b = io["w12T"], io["w3T"]

    emit_ln_stream(P, ws, "moeln", io["x1"], "x1", xln, "moe_xln")

    iota8 = ws.get("r_iota8", [128, 8], F32)
    P.op("gpsimd", lambda e: e.iota(iota8[:], pattern=[[1, 8]], base=0,
                                    channel_multiplier=0,
                                    allow_small_or_imprecise_dtypes=True),
         writes=["r_iota8"])
    ebrow = ws.get("r_ebrow", [1, 8], BF16)
    P.dma("gpsimd", lambda e: e.dma_start(ebrow[:], io["ebias"][None, :]),
          reads=["ebias"], writes=["r_ebrow"])
    p_eb, k_eb = ws.psa(4)
    ebbc = ws.get("r_ebbc", [128, 8], F32)
    P.op("tensor", lambda e: e.matmul(p_eb[:, 0:8], ones1[:], ebrow[:],
                                      start=True, stop=True, skip_group_check=True),
         reads=["r_ebrow", "c_ones1"], writes=[k_eb])
    P.op("vector", lambda e: e.tensor_copy(ebbc[:], p_eb[:, 0:8]),
         reads=[k_eb], writes=["r_ebbc"])
    rwt = []
    for dt in range(ND):
        rw = ws.get(f"r_w{dt}", [128, 8], BF16)
        P.dma("gpsimd", lambda e, dt=dt, rw=rw: e.dma_start(
            rw[:], io["rwT"][dt * 128:(dt + 1) * 128, :]),
            reads=["rwT"], writes=[f"r_w{dt}"])
        rwt.append((rw, f"r_w{dt}"))
    combT = ws.get("r_combT", [8, T], BF16)
    exsel = ws.get("r_exsel", [8, 1024], BF16)
    P.op("gpsimd", lambda e: e.memset(exsel[:], 0.0), writes=["r_exsel"])
    P.op("gpsimd", lambda e: e.affine_select(
        out=exsel.ap().rearrange("k (x m) -> k x m", x=8),
        in_=exsel.ap().rearrange("k (x m) -> k x m", x=8),
        compare_op=ALU.not_equal, fill=1.0, base=0,
        channel_multiplier=1, pattern=[[-1, 8], [0, 128]]),
        reads=["r_exsel"], writes=["r_exsel"])
    xs_t = ws.get("r_x", [128, 512], BF16)
    p_sc, k_sc = ws.psa(5)
    sc = ws.get("r_sc", [128, 8], F32)
    sel = ws.get("r_sel", [128, 8], F32)
    m1 = ws.get("r_m1", [128, 1], F32)
    cand = ws.get("r_cand", [128, 8], F32)
    i1 = ws.get("r_i1", [128, 1], F32)
    oh1 = ws.get("r_oh1", [128, 8], F32)
    s12 = ws.get("r_s12", [128, 4], F32)
    tmp8 = ws.get("r_tmp8", [128, 8], F32)
    comb = ws.get("r_comb", [128, 8], F32)
    p_tr, k_tr = ws.psa(6)
    identf = ws.get("c_identf", [128, 128], F32)
    P.op("gpsimd", lambda e: e.memset(identf[:], 0.0), writes=["c_identf"])
    P.op("gpsimd", lambda e: e.affine_select(
        out=identf[:], in_=identf[:], compare_op=ALU.not_equal, fill=1.0,
        base=0, pattern=[[-1, 128]], channel_multiplier=1),
        reads=["c_identf"], writes=["c_identf"])
    for ts in range(NT):
        sg, of = ts // 4, (ts % 4) * 128
        for dt in range(ND):
            P.dma("gpsimd", lambda e, dt=dt, sg=sg, of=of: e.dma_start(
                xs_t[:, 0:128], xln[dt * 128:(dt + 1) * 128,
                                    sg * 512 + of:sg * 512 + of + 128]),
                reads=["moe_xln"], writes=["r_x"])
            P.op("tensor", lambda e, dt=dt, w=rwt[dt][0]: e.matmul(
                p_sc[:, 0:8], xs_t[:, 0:128], w[:],
                start=(dt == 0), stop=(dt == ND - 1), skip_group_check=True),
                reads=["r_x", rwt[dt][1]], writes=[k_sc])
        P.op("scalar", lambda e: e.activation(sc[:], p_sc[:, 0:8], AF.Sigmoid),
             reads=[k_sc], writes=["r_sc"])
        P.op("vector", lambda e: e.tensor_tensor(sel[:], sc[:], ebbc[:], op=ALU.add),
             reads=["r_sc", "r_ebbc"], writes=["r_sel"])
        P.op("vector", lambda e: e.reduce_max(m1[:], sel[:], axis=AX.X),
             reads=["r_sel"], writes=["r_m1"])
        P.op("vector", lambda e: e.tensor_scalar(cand[:], sel[:], m1[:], 1e9,
                                                 op0=ALU.is_lt, op1=ALU.mult),
             reads=["r_sel", "r_m1"], writes=["r_cand"])
        P.op("vector", lambda e: e.tensor_tensor(cand[:], cand[:], iota8[:], op=ALU.add),
             reads=["r_cand", "r_iota8"], writes=["r_cand"])
        P.op("vector", lambda e: e.tensor_reduce(i1[:], cand[:], axis=AX.X, op=ALU.min),
             reads=["r_cand"], writes=["r_i1"])
        P.op("vector", lambda e: e.tensor_scalar(oh1[:], iota8[:], i1[:], None,
                                                 op0=ALU.is_equal),
             reads=["r_iota8", "r_i1"], writes=["r_oh1"])
        P.op("vector", lambda e: e.tensor_tensor(tmp8[:], sc[:], oh1[:], op=ALU.mult),
             reads=["r_sc", "r_oh1"], writes=["r_tmp8"])
        P.op("vector", lambda e: e.reduce_sum(s12[:, 0:1], tmp8[:], axis=AX.X),
             reads=["r_tmp8"], writes=["r_s12"])
        P.op("vector", lambda e: e.tensor_scalar(comb[:], oh1[:], s12[:, 0:1], None,
                                                 op0=ALU.mult),
             reads=["r_oh1", "r_s12"], writes=["r_comb"])
        P.op("vector", lambda e: e.scalar_tensor_tensor(
            sel[:], oh1[:], -1e9, sel[:], op0=ALU.mult, op1=ALU.add),
            reads=["r_oh1", "r_sel"], writes=["r_sel"])
        P.op("vector", lambda e: e.reduce_max(m1[:], sel[:], axis=AX.X),
             reads=["r_sel"], writes=["r_m1"])
        P.op("vector", lambda e: e.tensor_scalar(cand[:], sel[:], m1[:], 1e9,
                                                 op0=ALU.is_lt, op1=ALU.mult),
             reads=["r_sel", "r_m1"], writes=["r_cand"])
        P.op("vector", lambda e: e.tensor_tensor(cand[:], cand[:], iota8[:], op=ALU.add),
             reads=["r_cand", "r_iota8"], writes=["r_cand"])
        P.op("vector", lambda e: e.tensor_reduce(i1[:], cand[:], axis=AX.X, op=ALU.min),
             reads=["r_cand"], writes=["r_i1"])
        P.op("vector", lambda e: e.tensor_scalar(oh1[:], iota8[:], i1[:], None,
                                                 op0=ALU.is_equal),
             reads=["r_iota8", "r_i1"], writes=["r_oh1"])
        P.op("vector", lambda e: e.tensor_tensor(tmp8[:], sc[:], oh1[:], op=ALU.mult),
             reads=["r_sc", "r_oh1"], writes=["r_tmp8"])
        P.op("vector", lambda e: e.reduce_sum(s12[:, 1:2], tmp8[:], axis=AX.X),
             reads=["r_tmp8"], writes=["r_s12"])
        P.op("vector", lambda e: e.scalar_tensor_tensor(
            comb[:], oh1[:], s12[:, 1:2], comb[:], op0=ALU.mult, op1=ALU.add),
            reads=["r_oh1", "r_s12", "r_comb"], writes=["r_comb"])
        P.op("vector", lambda e: e.tensor_tensor(s12[:, 2:3], s12[:, 0:1], s12[:, 1:2],
                                                 op=ALU.add),
             reads=["r_s12"], writes=["r_s12"])
        P.op("vector", lambda e: e.tensor_scalar(s12[:, 2:3], s12[:, 2:3], 1e-9, None,
                                                 op0=ALU.add),
             reads=["r_s12"], writes=["r_s12"])
        P.op("vector", lambda e: e.reciprocal(s12[:, 3:4], s12[:, 2:3]),
             reads=["r_s12"], writes=["r_s12"])
        P.op("vector", lambda e: e.tensor_scalar(comb[:], comb[:], s12[:, 3:4], None,
                                                 op0=ALU.mult),
             reads=["r_comb", "r_s12"], writes=["r_comb"])
        P.op("tensor", lambda e: e.transpose(p_tr[0:8, 0:128], comb[:], identf[:]),
             reads=["r_comb", "c_identf"], writes=[k_tr])
        P.op("vector", lambda e, ts=ts: e.tensor_copy(
            combT[:, ts * 128:(ts + 1) * 128], p_tr[0:8, 0:128]),
            reads=[k_tr], writes=["r_combT"])

    mxl = [ws.get(f"lnst_o{i}", [128, 512], BF16) for i in range(ND)]
    hc = [ws.get(f"hc{i}", [128, 512], BF16) for i in range(8)]
    t1 = ws.get("sh_t1", [128, 512], BF16)
    x1t = ws.get("sh_x1t", [128, 512], F32)
    x2t = ws.get("sh_x2t", [128, 512], F32)
    for hgrp in range(16):
        w1t = load_wblock(
            P, ws, "gw0", 0,
            io["sw1T"][0:DM, hgrp * 512:(hgrp + 1) * 512].rearrange(
                "(t p) c -> p t c", p=128),
            "sw1T", ND, True, P.uname("sw1"))
        w2t = load_wblock(
            P, ws, "gw0", 16,
            io["sw2T"][0:DM, hgrp * 512:(hgrp + 1) * 512].rearrange(
                "(t p) c -> p t c", p=128),
            "sw2T", ND, True, P.uname("sw2"))
        oblk = ws.get("lnstOT", [128, 16 * 512], BF16)
        for sg in range(4):
            P.pump(6)
            for b0 in (0, 4, 8, 12):
                P.dma("gpsimd", lambda e, b0=b0, sg=sg: e.dma_start(
                    oblk[0:128, b0 * 512:(b0 + 4) * 512].rearrange(
                        "p (t c) -> p t c", t=4),
                    xln[b0 * 128:(b0 + 4) * 128,
                        sg * 512:(sg + 1) * 512].rearrange("(t p) c -> p t c", p=128)),
                    reads=[f"moe_xln.{sg}.{b0 // 4}"],
                    writes=[f"lnst_o{i}" for i in range(b0, b0 + 4)])
            for ht in range(4):
                pg, kg = ws.psa(ht)
                pu, ku = ws.psa(4 + ht)
                for dt in range(ND):
                    P.op("tensor", lambda e, dt=dt, ht=ht, pg=pg, w=w1t[dt][0]: e.matmul(
                        pg[:], w[:, ht * 128:(ht + 1) * 128], mxl[dt][:],
                        start=(dt == 0), stop=(dt == ND - 1), skip_group_check=True),
                        reads=[w1t[dt][1], f"lnst_o{dt}"], writes=[kg],
                        quiet=(dt % 4 != 3))
                    P.op("tensor", lambda e, dt=dt, ht=ht, pu=pu, w=w2t[dt][0]: e.matmul(
                        pu[:], w[:, ht * 128:(ht + 1) * 128], mxl[dt][:],
                        start=(dt == 0), stop=(dt == ND - 1), skip_group_check=True),
                        reads=[w2t[dt][1], f"lnst_o{dt}"], writes=[ku],
                        quiet=(dt % 4 != 3))
                P.op("scalar", lambda e, pg=pg: e.activation(t1[:], pg[:], AF.Silu),
                     reads=[kg], writes=["sh_t1"])
                P.op("vector", lambda e, pu=pu, ht=ht: e.tensor_tensor(
                    hc[ht][:], t1[:], pu[:], op=ALU.mult),
                    reads=["sh_t1", ku], writes=[f"hc{ht}"])
                P.dma("gpsimd", lambda e, hgrp=hgrp, ht=ht, sg=sg: e.dma_start(
                    hsh[hgrp * 512 + ht * 128:hgrp * 512 + (ht + 1) * 128,
                        sg * 512:(sg + 1) * 512], hc[ht][:]),
                    reads=[f"hc{ht}"], writes=["moe_hsh"], cls="st")
    for og in range(4):
        w3t = load_wblock(
            P, ws, "gwC", 0,
            io["sw3T"][0:32 * 128, og * 512:(og + 1) * 512].rearrange(
                "(t p) c -> p t c", p=128),
            "sw3T", 32, True, P.uname("sw3a"))
        w3t += load_wblock(
            P, ws, "gw0", 0,
            io["sw3T"][32 * 128:64 * 128, og * 512:(og + 1) * 512].rearrange(
                "(t p) c -> p t c", p=128),
            "sw3T", 32, True, P.uname("sw3b"))
        hshb = [ws.get(f"hc{i}", [128, 512], BF16) for i in range(8)]
        hcblk = ws.get("hcT", [128, 8 * 512], BF16)
        for sg in range(4):
            for rt in range(64):
                hb, hk = hshb[rt % 8], f"hc{rt % 8}"
                if rt % 4 == 0:
                    r8 = rt % 8
                    P.dma("scalar", lambda e, rt=rt, sg=sg, r8=r8: e.dma_start(
                        hcblk[0:128, r8 * 512:(r8 + 4) * 512].rearrange(
                            "p (t c) -> p t c", t=4),
                        hsh[rt * 128:(rt + 4) * 128,
                            sg * 512:(sg + 1) * 512].rearrange(
                            "(t p) c -> p t c", p=128)),
                        reads=["moe_hsh"],
                        writes=[f"hc{r8 + i}" for i in range(4)])
                for o in range(4):
                    po, ko = ws.psa(o)
                    P.op("tensor", lambda e, o=o, po=po, hb=hb, w=w3t[rt][0], rt=rt:
                         e.matmul(po[:], w[:, o * 128:(o + 1) * 128], hb[:],
                                  start=(rt == 0), stop=(rt == 63),
                                  skip_group_check=True),
                         reads=[w3t[rt][1], hk], writes=[ko],
                         quiet=not (o == 3))
            for o in range(4):
                po, ko = ws.psa(o)
                orow = og * 512 + o * 128
                P.dma("sync", lambda e, orow=orow, sg=sg: e.dma_start(
                    x1t[:], io["x1"][orow:orow + 128, sg * 512:(sg + 1) * 512]),
                    reads=["x1"], writes=["sh_x1t"])
                P.op("vector", lambda e, po=po: e.tensor_tensor(
                    x2t[:], x1t[:], po[:], op=ALU.add),
                    reads=["sh_x1t", ko], writes=["sh_x2t"])
                P.dma("sync", lambda e, orow=orow, sg=sg: e.dma_start(
                    x2p[orow:orow + 128, sg * 512:(sg + 1) * 512], x2t[:]),
                    reads=["sh_x2t"], writes=["moe_x2p"], cls="st")
    return dict(xln=xln, x2p=x2p, w12b=w12b, w3b=w3b, combT=combT,
                cached=(cache is not None))


def emit_moe_routed(P, ws, io, xln, x2p, w12b, w3b, combT, cache=True):
    ones1 = ws.get("c_ones1", [1, 128], BF16)
    exsel = ws.get("r_exsel", [8, 1024], BF16)
    mxl = [ws.get(f"lnst_o{i}", [128, 512], BF16) for i in range(ND)]
    yr = [ws.get(f"lnst_x{i}", [128, 512], F32) for i in range(16)]
    hcb = [ws.get(f"hc{i}", [128, 512], BF16) for i in range(8)]
    t1b = ws.get("sh_t1", [128, 512], BF16)
    cbc = ws.get("r_x", [128, 512], BF16)
    x2pt = ws.get("sh_x1t", [128, 512], F32)
    x2t = ws.get("sh_x2t", [128, 512], F32)
    oblk = ws.get("lnstOT", [128, 16 * 512], BF16)
    for sg in range(4):
        for b0 in (0, 4, 8, 12):
            P.dma("gpsimd", lambda e, b0=b0, sg=sg: e.dma_start(
                oblk[0:128, b0 * 512:(b0 + 4) * 512].rearrange(
                    "p (t c) -> p t c", t=4),
                xln[b0 * 128:(b0 + 4) * 128,
                    sg * 512:(sg + 1) * 512].rearrange("(t p) c -> p t c", p=128)),
                reads=[f"moe_xln.{sg}.{b0 // 4}"],
                writes=[f"lnst_o{i}" for i in range(b0, b0 + 4)])
        for ex in range(8):
            wcast = not cache
            def ldq(cc, ex=ex):
                return load_wblock(
                    P, ws, "gwC", (cc % 2) * 16,
                    w12b[ex][0:DM, cc * 512:(cc + 1) * 512].rearrange(
                        "(t p) c -> p t c", p=128),
                    "moe_w12b", 16, wcast, P.uname("w12q"))
            w3t = []
            for hcx in range(8):
                w3t += load_wblock(
                    P, ws, "gw0", hcx * 4,
                    w3b[ex][hcx * 128:(hcx + 1) * 128, 0:2048].rearrange(
                        "p (g c) -> p g c", g=4),
                    "moe_w3b", 4, wcast, P.uname("w3l"))
            p_cb, k_cb = ws.psa(4)
            P.op("tensor", lambda e, ex=ex, sg=sg: e.matmul(
                p_cb[:], exsel[:, ex * 128:(ex + 1) * 128],
                combT[:, sg * 512:(sg + 1) * 512],
                start=True, stop=True, skip_group_check=True),
                reads=["r_combT", "r_exsel"], writes=[k_cb])
            P.op("vector", lambda e, p_cb=p_cb: e.tensor_copy(cbc[:], p_cb[:]),
                 reads=[k_cb], writes=["r_x"])
            sh1 = [ws.get(f"sh1_{i}", [128, 512], BF16) for i in range(8)]
            for cc in range(4):
                wq4 = ldq(cc)
                for hq in range(4):
                    hp = (cc % 2) * 4 + hq
                    po, ko = ws.psa(hq)
                    for dt in range(ND):
                        P.op("tensor", lambda e, po=po, w=wq4[dt][0],
                             of=hq * 128, dt=dt: e.matmul(
                                 po[:], w[:, of:of + 128], mxl[dt][:],
                                 start=(dt == 0), stop=(dt == ND - 1),
                                 skip_group_check=True),
                             reads=[wq4[dt][1], f"lnst_o{dt}"],
                             writes=[ko], quiet=(dt % 4 != 3))
                    if cc < 2:
                        P.op("scalar", lambda e, po=po, hp=hp: e.activation(
                            sh1[hp][:], po[:], AF.Silu),
                            reads=[ko], writes=[f"sh1_{hp}"])
                    else:
                        P.op("vector", lambda e, po=po, hp=hp: e.tensor_tensor(
                            hcb[hp][:], sh1[hp][:], po[:], op=ALU.mult),
                            reads=[f"sh1_{hp}", ko], writes=[f"hc{hp}"])
                        P.op("gpsimd", lambda e, hp=hp: e.tensor_tensor(
                            hcb[hp][:], hcb[hp][:], cbc[:], op=ALU.mult),
                            reads=[f"hc{hp}", "r_x"], writes=[f"hc{hp}"])
            for o16 in range(16):
                po, ko = ws.psa(4 + o16 % 4)
                cc, of = o16 // 4, (o16 % 4) * 128
                for hcx in range(8):
                    P.op("tensor", lambda e, po=po, w=w3t[hcx * 4 + cc][0],
                         of=of, hcx=hcx: e.matmul(
                             po[:], w[:, of:of + 128], hcb[hcx][:],
                             start=(hcx == 0), stop=(hcx == 7),
                             skip_group_check=True),
                         reads=[w3t[hcx * 4 + cc][1], f"hc{hcx}"],
                         writes=[ko], quiet=(hcx % 4 != 3))
                if ex == 0:
                    P.op("vector", lambda e, po=po, o16=o16: e.tensor_copy(
                        yr[o16][:], po[:]),
                        reads=[ko], writes=[f"lnst_x{o16}"])
                else:
                    P.op("vector", lambda e, po=po, o16=o16: e.tensor_tensor(
                        yr[o16][:], yr[o16][:], po[:], op=ALU.add),
                        reads=[ko, f"lnst_x{o16}"], writes=[f"lnst_x{o16}"])
        for o16 in range(16):
            P.dma("sync", lambda e, o16=o16, sg=sg: e.dma_start(
                x2pt[:], x2p[o16 * 128:(o16 + 1) * 128, sg * 512:(sg + 1) * 512]),
                reads=["moe_x2p"], writes=["sh_x1t"])
            P.op("vector", lambda e, o16=o16: e.tensor_tensor(
                x2t[:], x2pt[:], yr[o16][:], op=ALU.add),
                reads=["sh_x1t", f"lnst_x{o16}"], writes=["sh_x2t"])
            P.dma("sync", lambda e, o16=o16, sg=sg: e.dma_start(
                io["x2"][o16 * 128:(o16 + 1) * 128, sg * 512:(sg + 1) * 512], x2t[:]),
                reads=["sh_x2t"], writes=["x2"], cls="st")


def build_moe_phase():
    P = Prog()
    ws = WS(P)
    io = dict(
        x1=P.dram("x1", [DM, T], F32, kind="ExternalInput"),
        w12T=P.dram("w12T", [8, DM, 2048], F32, kind="ExternalInput"),
        w3T=P.dram("w3T", [8, 1024, 2048], F32, kind="ExternalInput"),
        rwT=P.dram("rwT", [DM, 8], F32, kind="ExternalInput"),
        ebias=P.dram("ebias", [8], F32, kind="ExternalInput"),
        sw1T=P.dram("sw1T", [DM, 8192], F32, kind="ExternalInput"),
        sw2T=P.dram("sw2T", [DM, 8192], F32, kind="ExternalInput"),
        sw3T=P.dram("sw3T", [8192, DM], F32, kind="ExternalInput"),
        x2=P.dram("x2", [DM, T], F32, kind="ExternalOutput"),
    )
    emit_consts(P, ws)
    d = emit_moe(P, ws, io)
    emit_moe_routed(P, ws, io, d["xln"], d["x2p"], d["w12b"], d["w3b"], d["combT"],
                    cache=d["cached"])
    return P.finalize()


def emit_gla(P, ws, io, x1p, x1p_key, sfin, qhat):
    SCALE = 128.0 ** -0.5
    ones1 = ws.get("c_ones1", [1, 128], BF16)
    xt = [ws.get(f"lnst_x{i}", [128, 512], F32) for i in range(ND)]
    xln = [ws.get(f"lnst_o{i}", [128, 512], BF16) for i in range(ND)]
    xkeys = [f"lnst_x{i}" for i in range(ND)]
    xlnkeys = [f"lnst_o{i}" for i in range(ND)]
    wk = load_wblock(P, ws, "gw0", 0,
                     io["gWkT"][0:DM, 0:512].rearrange("(t p) c -> p t c", p=128),
                     "gWkT", 16, True, P.uname("gwk"))
    wv = load_wblock(P, ws, "gw0", 16,
                     io["gWvT"][0:DM, 0:512].rearrange("(t p) c -> p t c", p=128),
                     "gWvT", 16, True, P.uname("gwv"))
    bqs = []
    for ot in range(16):
        t = ws.get(f"gbq{ot}", [128, 1], F32)
        P.dma("sync", lambda e, ot=ot, t=t: e.dma_start(
            t[:], io["gbq"][ot * 128:(ot + 1) * 128, None]),
            reads=["gbq"], writes=[f"gbq{ot}"])
        P.op("vector", lambda e, t=t: e.tensor_scalar(
            t[:], t[:], SCALE, None, op0=ALU.mult),
            reads=[f"gbq{ot}"], writes=[f"gbq{ot}"])
        bqs.append(t)
    bks = []
    for kt in range(4):
        t = ws.get(f"gbk{kt}", [128, 1], F32)
        P.dma("sync", lambda e, kt=kt, t=t: e.dma_start(
            t[:], io["gbk"][kt * 128:(kt + 1) * 128, None]),
            reads=["gbk"], writes=[f"gbk{kt}"])
        bks.append(t)
    bvrow = ws.get("gbvrow", [1, 512], BF16)
    P.dma("gpsimd", lambda e: e.dma_start(bvrow[:], io["gbv"][None, :]),
          reads=["gbv"], writes=["gbvrow"])
    p_bv, k_bv = ws.psa(4)
    bvbc = ws.get("gbvbc", [128, 512], F32)
    P.op("tensor", lambda e: e.matmul(p_bv[:], ones1[:], bvrow[:],
                                      start=True, stop=True, skip_group_check=True),
         reads=["gbvrow", "c_ones1"], writes=[k_bv])
    P.op("vector", lambda e: e.tensor_copy(bvbc[:], p_bv[:]),
         reads=[k_bv], writes=["gbvbc"])
    maskT = ws.get("c_maskT", [32, 128], BF16)
    P.op("gpsimd", lambda e: e.memset(maskT[:], 0.0), writes=["c_maskT"])
    P.op("gpsimd", lambda e: e.affine_select(
        out=maskT.ap().rearrange("p (j t) -> p j t", j=4),
        in_=maskT.ap().rearrange("p (j t) -> p j t", j=4),
        compare_op=ALU.is_ge, fill=0.0, base=0,
        channel_multiplier=-1, pattern=[[0, 4], [1, 32]]),
        reads=["c_maskT"], writes=["c_maskT"])
    P.op("gpsimd", lambda e: e.memset(maskT[:], 1.0), writes=["c_maskT"])
    P.op("gpsimd", lambda e: e.affine_select(
        out=maskT.ap().rearrange("p (j t) -> p j t", j=4),
        in_=maskT.ap().rearrange("p (j t) -> p j t", j=4),
        compare_op=ALU.is_ge, fill=0.0, base=0,
        channel_multiplier=-1, pattern=[[0, 4], [1, 32]]),
        reads=["c_maskT"], writes=["c_maskT"])
    zero32 = ws.get("c_zero32", [128, 32], F32)
    P.op("vector", lambda e: e.memset(zero32[:], 0.0), writes=["c_zero32"])
    gS = [ws.get(f"gS{kt}", [128, 128], F32) for kt in range(4)]
    gSb = [ws.get(f"gSb{kt}", [128, 128], BF16) for kt in range(4)]
    gGr = [ws.get(f"gGr{kt}", [128, 1], F32) for kt in range(4)]
    for kt in range(4):
        P.op("vector", lambda e, kt=kt: e.memset(gS[kt][:], 0.0), writes=[f"gS{kt}"])
        P.op("vector", lambda e, kt=kt: e.memset(gSb[kt][:], 0.0), writes=[f"gSb{kt}"])
        P.op("vector", lambda e, kt=kt: e.memset(gGr[kt][:], 1.0), writes=[f"gGr{kt}"])
    gqk = [ws.get(f"gqk{kt}", [128, 2048], BF16) for kt in range(4)]
    gkt_ = [ws.get(f"gkt{kt}", [128, 512], BF16) for kt in range(4)]
    gkf = ws.get("gkf", [128, 512], F32)
    gg = ws.get("gg", [128, 512], F32)
    gB = ws.get("gB", [128, 512], F32)
    gBr = ws.get("gBr", [128, 512], F32)
    geP = ws.get("geP", [128, 512], BF16)
    geN = ws.get("geN", [128, 512], BF16)
    gKh = ws.get("gKh", [128, 512], F32)
    g15 = [ws.get(f"g15_{kt}", [128, 16], F32) for kt in range(4)]
    ge31 = ws.get("ge31", [128, 16], F32)
    gdec = [ws.get(f"gdec{kt}", [128, 16], F32) for kt in range(4)]
    gva = ws.get("gva", [64, 2048], BF16)
    gvb = ws.get("gvb", [64, 2048], BF16)
    gka = ws.get("gka", [64, 2048], BF16)
    gkb = ws.get("gkb", [64, 2048], BF16)
    gqG_ = [ws.get(f"gqG{k}", [128, 128], BF16) for k in range(4)]
    gPt_ = [ws.get(f"gPt{k}", [64, 128], BF16) for k in range(4)]
    gGm_ = [ws.get(f"gGm{k}", [128, 1], F32) for k in range(4)]
    gQc = [ws.get(f"gQc{i}", [128, 128], BF16) for i in range(4)]
    identf = ws.get("c_identf", [128, 128], F32)
    P.op("gpsimd", lambda e: e.memset(identf[:], 0.0), writes=["c_identf"])
    P.op("gpsimd", lambda e: e.affine_select(
        out=identf[:], in_=identf[:], compare_op=ALU.not_equal, fill=1.0,
        base=0, pattern=[[-1, 128]], channel_multiplier=1),
        reads=["c_identf"], writes=["c_identf"])

    xblk = ws.get("lnstXT", [128, 16 * 512], F32)
    for sg in range(4):
        for b0 in (0, 4, 8, 12):
            P.dma("sync", lambda e, b0=b0, sg=sg: e.dma_start(
                xblk[0:128, b0 * 512:(b0 + 4) * 512].rearrange(
                    "p (t c) -> p t c", t=4),
                io["x0"][b0 * 128:(b0 + 4) * 128,
                         sg * 512:(sg + 1) * 512].rearrange("(t p) c -> p t c", p=128)),
                reads=["x0"], writes=xkeys[b0:b0 + 4])
        emit_ln(P, ws, xt, xkeys, xln, xlnkeys, 512)
        for half in range(2):
            wq = load_wblock(
                P, ws, "gwC", 0,
                io["gWqT"][0:DM, half * 1024:(half + 1) * 1024].rearrange(
                    "(t p) c -> p t c", p=128),
                "gWqT", 32, True, P.uname("gwq"))
            for ot in range(half * 8, half * 8 + 8):
                po, ko = ws.psa(ot % 4)
                ccw = (ot // 4) % 2
                for dt in range(ND):
                    P.op("tensor", lambda e, po=po, w=wq[dt * 2 + ccw][0],
                         of=(ot % 4) * 128, dt=dt: e.matmul(
                             po[:], w[:, of:of + 128], xln[dt][:],
                             start=(dt == 0), stop=(dt == ND - 1),
                             skip_group_check=True),
                         reads=[wq[dt * 2 + ccw][1], xlnkeys[dt]], writes=[ko],
                         quiet=(dt % 4 != 3))
                kt, j = ot // 4, ot % 4
                P.op("scalar", lambda e, po=po, kt=kt, j=j, b=bqs[ot]: e.activation(
                    gqk[kt][:, j * 512:(j + 1) * 512], po[:], AF.Relu,
                    bias=b[:], scale=SCALE),
                    reads=[ko, f"gbq{ot}"], writes=[f"gqk{kt}"])
                P.pump(4)
        for kt in range(4):
            po, ko = ws.psa(4 + kt % 4)
            for dt in range(ND):
                P.op("tensor", lambda e, po=po, w=wk[dt][0], of=kt * 128, dt=dt:
                     e.matmul(po[:], w[:, of:of + 128], xln[dt][:],
                              start=(dt == 0), stop=(dt == ND - 1),
                              skip_group_check=True),
                     reads=[wk[dt][1], xlnkeys[dt]], writes=[ko],
                     quiet=(dt % 4 != 3))
            P.op("scalar", lambda e, po=po, b=bks[kt]: e.activation(
                gkf[:], po[:], AF.Sigmoid, bias=b[:]),
                reads=[ko, f"gbk{kt}"], writes=["gkf"])
            P.op("vector", lambda e: e.tensor_scalar(gkf[:], gkf[:], 0.95, None,
                                                     op0=ALU.min),
                 reads=["gkf"], writes=["gkf"])
            P.op("vector", lambda e: e.tensor_scalar(gg[:], gkf[:], -1.0, 1.0,
                                                     op0=ALU.mult, op1=ALU.add),
                 reads=["gkf"], writes=["gg"])
            P.op("scalar", lambda e: e.activation(gg[:], gg[:], AF.Ln),
                 reads=["gg"], writes=["gg"])
            for c in range(16):
                P.op("vector", lambda e, c=c: e.tensor_tensor_scan(
                    gB[:, c * 32:(c + 1) * 32], gg[:, c * 32:(c + 1) * 32],
                    zero32[:], 0.0, op0=ALU.add, op1=ALU.add),
                    reads=["gg", "c_zero32"], writes=["gB"])
            for c in range(16):
                P.op("vector", lambda e, c=c: e.tensor_scalar(
                    gBr[:, c * 32:(c + 1) * 32], gB[:, c * 32:(c + 1) * 32],
                    gB[:, c * 32 + 15:c * 32 + 16], None, op0=ALU.subtract),
                    reads=["gB"], writes=["gBr"])
            P.op("scalar", lambda e: e.activation(geP[:], gBr[:], AF.Exp),
                 reads=["gBr"], writes=["geP"])
            P.op("scalar", lambda e: e.activation(geN[:], gBr[:], AF.Exp, scale=-1.0),
                 reads=["gBr"], writes=["geN"])
            P.op("scalar", lambda e, kt=kt: e.activation(
                g15[kt][:], gB[:, 15::32], AF.Exp),
                reads=["gB"], writes=[f"g15_{kt}"])
            P.op("scalar", lambda e, kt=kt: e.activation(
                ge31[:], gBr[:, 31::32], AF.Exp),
                reads=["gBr"], writes=["ge31"])
            P.op("vector", lambda e, kt=kt: e.tensor_tensor(
                gdec[kt][:], ge31[:], g15[kt][:], op=ALU.mult),
                reads=["ge31", f"g15_{kt}"], writes=[f"gdec{kt}"])
            P.op("vector", lambda e, kt=kt: e.tensor_tensor(
                gkt_[kt][:], gkf[:], geN[:], op=ALU.mult),
                reads=["gkf", "geN"], writes=[f"gkt{kt}"])
            for c in range(16):
                P.op("vector", lambda e, kt=kt, c=c: e.tensor_scalar(
                    gKh[:, c * 32:(c + 1) * 32], gkt_[kt][:, c * 32:(c + 1) * 32],
                    ge31[:, c:c + 1], None, op0=ALU.mult),
                    reads=[f"gkt{kt}", "ge31"], writes=["gKh"])
            for j in range(4):
                P.op("vector", lambda e, kt=kt, j=j: e.tensor_tensor(
                    gqk[kt][:, j * 512:(j + 1) * 512],
                    gqk[kt][:, j * 512:(j + 1) * 512], geP[:], op=ALU.mult),
                    reads=[f"gqk{kt}", "geP"], writes=[f"gqk{kt}"])
            for tb in range(4):
                ptr, ktr = ws.psa(kt % 4)
                P.op("tensor", lambda e, ptr=ptr, tb=tb: e.transpose(
                    ptr[:, 0:128], gKh[:, tb * 128:(tb + 1) * 128], identf[:]),
                    reads=["gKh", "c_identf"], writes=[ktr])
                dst_a = gka if True else None
                P.op("vector", lambda e, ptr=ptr, tb=tb, kt=kt: e.tensor_copy(
                    gka[0:64, tb * 512 + kt * 128:tb * 512 + (kt + 1) * 128],
                    ptr[0:64, 0:128]),
                    reads=[ktr], writes=["gka"])
                P.op("vector", lambda e, ptr=ptr, tb=tb, kt=kt: e.tensor_copy(
                    gkb[0:64, tb * 512 + kt * 128:tb * 512 + (kt + 1) * 128],
                    ptr[64:128, 0:128]),
                    reads=[ktr], writes=["gkb"])
        for tb in range(4):
            po, ko = ws.psa(4 + tb % 4)
            for dt in range(ND):
                P.op("tensor", lambda e, po=po, tb=tb, dt=dt, w=wv[dt][0]: e.matmul(
                    po[:], xln[dt][:, tb * 128:(tb + 1) * 128], w[:],
                    start=(dt == 0), stop=(dt == ND - 1), skip_group_check=True),
                    reads=[wv[dt][1], xlnkeys[dt]], writes=[ko],
                    quiet=(dt % 4 != 3))
            P.op("vector", lambda e, po=po, tb=tb: e.scalar_tensor_tensor(
                gva[0:64, tb * 512:(tb + 1) * 512], po[0:64, :], 1.0, bvbc[0:64, :],
                op0=ALU.mult, op1=ALU.add),
                reads=[ko, "gbvbc"], writes=["gva"])
            P.op("vector", lambda e, po=po, tb=tb: e.scalar_tensor_tensor(
                gvb[0:64, tb * 512:(tb + 1) * 512], po[64:128, :], 1.0, bvbc[64:128, :],
                op0=ALU.mult, op1=ALU.add),
                reads=[ko, "gbvbc"], writes=["gvb"])
        for c in range(16):
            P.pump(2)
            gc = sg * 16 + c
            tb, q = c // 4, c % 4
            vsrc, vkey = (gva, "gva") if q < 2 else (gvb, "gvb")
            ksrc, kkey = (gka, "gka") if q < 2 else (gkb, "gkb")
            roff = (q % 2) * 32
            for kt in range(4):
                qstr = gqk[kt].ap().rearrange("p (j t) -> p j t", t=512)[
                    :, :, c * 32:(c + 1) * 32]
                pP, kP = ws.psa(5)
                pO, kO = ws.psa(6)
                pD, kD = ws.psa(7)
                P.op("tensor", lambda e, pP=pP, kt=kt, c=c, qstr=qstr: e.matmul(
                    pP[0:32, 0:128], gkt_[kt][:, c * 32:(c + 1) * 32], qstr,
                    start=True, stop=True, skip_group_check=True),
                    reads=[f"gkt{kt}", f"gqk{kt}"], writes=[kP])
                P.op("vector", lambda e, pP=pP, roff=roff, kt=kt: e.tensor_tensor(
                    gPt_[kt][roff:roff + 32, :], pP[0:32, 0:128], maskT[:], op=ALU.mult),
                    reads=[kP, "c_maskT"], writes=[f"gPt{kt}"])
                P.op("vector", lambda e, kt=kt, c=c, qstr=qstr: e.tensor_scalar(
                    gqG_[kt].ap().rearrange("p (j t) -> p j t", t=32), qstr,
                    g15[kt][:, c:c + 1], None, op0=ALU.mult),
                    reads=[f"gqk{kt}", f"g15_{kt}"], writes=[f"gqG{kt}"])
                P.op("tensor", lambda e, pO=pO, kt=kt: e.matmul(
                    pO[:, 0:128], gSb[kt][:], gqG_[kt][:],
                    start=True, stop=False, skip_group_check=True),
                    reads=[f"gSb{kt}", f"gqG{kt}"], writes=[kO], quiet=True)
                P.op("tensor", lambda e, pO=pO, kt=kt, vsrc=vsrc, roff=roff,
                     tb=tb: e.matmul(
                         pO[:, 0:128],
                         vsrc[roff:roff + 32, tb * 512 + kt * 128:tb * 512 + (kt + 1) * 128],
                         gPt_[kt][roff:roff + 32, :], start=False, stop=True,
                         skip_group_check=True),
                     reads=[vkey, f"gPt{kt}"], writes=[kO])
                P.op("tensor", lambda e, pD=pD, kt=kt, ksrc=ksrc, vsrc=vsrc,
                     roff=roff, tb=tb: e.matmul(
                         pD[:, 0:128],
                         ksrc[roff:roff + 32, tb * 512 + kt * 128:tb * 512 + (kt + 1) * 128],
                         vsrc[roff:roff + 32, tb * 512 + kt * 128:tb * 512 + (kt + 1) * 128],
                         start=True, stop=True, skip_group_check=True),
                     reads=[kkey, vkey], writes=[kD])
                P.op("vector", lambda e, kt=kt, c=c, pD=pD: e.scalar_tensor_tensor(
                    gS[kt][:], gS[kt][:], gdec[kt][:, c:c + 1], pD[:, 0:128],
                    op0=ALU.mult, op1=ALU.add),
                    reads=[f"gS{kt}", f"gdec{kt}", kD], writes=[f"gS{kt}"])
                P.op("vector", lambda e, kt=kt: e.tensor_copy(gSb[kt][:], gS[kt][:]),
                     reads=[f"gS{kt}"], writes=[f"gSb{kt}"])
                P.op("vector", lambda e, kt=kt, c=c: e.tensor_tensor(
                    gGm_[kt][:], gGr[kt][:], g15[kt][:, c:c + 1], op=ALU.mult),
                    reads=[f"gGr{kt}", f"g15_{kt}"], writes=[f"gGm{kt}"])
                qc, qck = gQc[kt], f"gQc{kt}"
                P.op("vector", lambda e, kt=kt, c=c, qstr=qstr, qc=qc: e.tensor_scalar(
                    qc.ap().rearrange("p (j t) -> p j t", t=32), qstr,
                    gGm_[kt][:], None, op0=ALU.mult),
                    reads=[f"gqk{kt}", f"gGm{kt}"], writes=[qck])
                P.dma("sync", lambda e, kt=kt, gc=gc, qc=qc: e.dma_start(
                    qhat[kt * 128:(kt + 1) * 128, :].rearrange(
                        "p (j t) -> p j t", j=4)[:, :, gc * 32:(gc + 1) * 32],
                    qc.ap().rearrange("p (j t) -> p j t", t=32)),
                    reads=[qck], writes=["qhat"], cls="st")
                P.op("vector", lambda e, kt=kt, c=c: e.tensor_scalar(
                    gGr[kt][:], gGr[kt][:], gdec[kt][:, c:c + 1], None, op0=ALU.mult),
                    reads=[f"gGr{kt}", f"gdec{kt}"], writes=[f"gGr{kt}"])
                for j in range(4):
                    h = kt * 4 + j
                    P.op("vector", lambda e, pO=pO, h=h, j=j, c=c: e.tensor_tensor(
                        xt[h][:, c * 32:(c + 1) * 32], xt[h][:, c * 32:(c + 1) * 32],
                        pO[:, j * 32:(j + 1) * 32], op=ALU.add),
                        reads=[kO, xkeys[h]], writes=[xkeys[h]])
        for i in range(ND):
            P.dma("sync", lambda e, i=i, sg=sg: e.dma_start(
                x1p[i * 128:(i + 1) * 128, sg * 512:(sg + 1) * 512], xt[i][:]),
                reads=[xkeys[i]], writes=[x1p_key])
    for kt in range(4):
        P.dma("sync", lambda e, kt=kt: e.dma_start(
            sfin[kt * 128:(kt + 1) * 128, :], gS[kt][:]),
            reads=[f"gS{kt}"], writes=["sfin"], cls="st")


def emit_glacorr(P, ws, io, x1p, x1p_key, qhat, qhat_key, xout, xout_key):
    sin = [ws.get(f"ci_S{kt}", [128, 128], F32) for kt in range(4)]
    sinb = [ws.get(f"ci_Sb{kt}", [128, 128], BF16) for kt in range(4)]
    hm = ws.get("ci_hm", [128, 1], F32)
    qh = [ws.get(f"ci_qh{i}", [128, 512], BF16) for i in range(4)]
    xg = [ws.get(f"ci_x{i}", [128, 512], F32) for i in range(2)]
    xo = [ws.get(f"ci_xo{i}", [128, 512], F32) for i in range(2)]
    P.dma("sync", lambda e: e.dma_start(hm[:], io["hmask"]),
          reads=["hmask"], writes=["ci_hm"])
    for kt in range(4):
        P.dma("sync", lambda e, kt=kt: e.dma_start(
            sin[kt][:], io["sin"][kt * 128:(kt + 1) * 128, :]),
            reads=["sin"], writes=[f"ci_S{kt}"])
        P.op("vector", lambda e, kt=kt: e.tensor_scalar(
            sin[kt][:], sin[kt][:], hm[:], None, op0=ALU.mult),
            reads=[f"ci_S{kt}", "ci_hm"], writes=[f"ci_S{kt}"])
        P.op("vector", lambda e, kt=kt: e.tensor_copy(sinb[kt][:], sin[kt][:]),
             reads=[f"ci_S{kt}"], writes=[f"ci_Sb{kt}"])
    for kt in range(4):
        for j in range(4):
            h = kt * 4 + j
            for tc in range(4):
                qb, qk_ = qh[tc % 4], f"ci_qh{tc % 4}"
                P.dma("gpsimd", lambda e, kt=kt, j=j, tc=tc, qb=qb: e.dma_start(
                    qb[:], qhat[kt * 128:(kt + 1) * 128,
                                j * 2048 + tc * 512:j * 2048 + (tc + 1) * 512]),
                    reads=[qhat_key], writes=[qk_])
                po, ko = ws.psa(tc % 4)
                P.op("tensor", lambda e, po=po, kt=kt, qb=qb: e.matmul(
                    po[:], sinb[kt][:], qb[:],
                    start=True, stop=True, skip_group_check=True),
                    reads=[f"ci_Sb{kt}", qk_], writes=[ko])
                xa, xak = xg[tc % 2], f"ci_x{tc % 2}"
                xb_, xbk = xo[tc % 2], f"ci_xo{tc % 2}"
                P.dma("sync", lambda e, h=h, tc=tc, xa=xa: e.dma_start(
                    xa[:], x1p[h * 128:(h + 1) * 128, tc * 512:(tc + 1) * 512]),
                    reads=[x1p_key], writes=[xak])
                P.op("vector", lambda e, po=po, xa=xa, xb_=xb_: e.tensor_tensor(
                    xb_[:], xa[:], po[:], op=ALU.add),
                    reads=[ko, xak], writes=[xbk])
                P.dma("sync", lambda e, h=h, tc=tc, xb_=xb_: e.dma_start(
                    xout[h * 128:(h + 1) * 128, tc * 512:(tc + 1) * 512], xb_[:]),
                    reads=[xbk], writes=[xout_key], cls="st")


def build_gla_phase():
    P = Prog()
    ws = WS(P)
    io = dict(
        x0=P.dram("x0", [DM, T], F32, kind="ExternalInput"),
        gWqT=P.dram("gWqT", [DM, DM], F32, kind="ExternalInput"),
        gbq=P.dram("gbq", [DM], F32, kind="ExternalInput"),
        gWkT=P.dram("gWkT", [DM, 512], F32, kind="ExternalInput"),
        gbk=P.dram("gbk", [512], F32, kind="ExternalInput"),
        gWvT=P.dram("gWvT", [DM, 512], F32, kind="ExternalInput"),
        gbv=P.dram("gbv", [512], F32, kind="ExternalInput"),
    )
    x1p = P.dram("x1p", [DM, T], F32, kind="ExternalOutput")
    sfin = P.dram("sfin", [512, 128], F32, kind="ExternalOutput")
    qhat = P.dram("qhat", [512, 8192], BF16, kind="ExternalOutput")
    emit_consts(P, ws)
    emit_gla(P, ws, io, x1p, "x1p", sfin, qhat)
    return P.finalize()


def build_glacorr_phase():
    P = Prog()
    ws = WS(P)
    io = dict(
        hmask=P.dram("hmask", [128, 1], F32, kind="ExternalInput"),
        sin=P.dram("sin", [512, 128], F32, kind="ExternalInput"),
    )
    x1p = P.dram("x1p", [DM, T], F32, kind="ExternalInput")
    qhat = P.dram("qhat", [512, 8192], BF16, kind="ExternalInput")
    x1 = P.dram("x1", [DM, T], F32, kind="ExternalOutput")
    emit_consts(P, ws)
    emit_glacorr(P, ws, io, x1p, "x1p", qhat, "qhat", x1, "x1")
    return P.finalize()


def emit_attn_proj(P, ws, io, qT, qT_key, qn2d, qn2_key):
    SC = 128 ** -0.5
    ones1 = ws.get("c_ones1", [1, 128], BF16)
    onesK = ws.get("c_onesK", [128, 1], BF16)
    xln = P.dram("attn_xln", [DM, T], BF16)
    emit_ln_stream(P, ws, "attnln", io["x2"], "x2", xln, "attn_xln")

    xlt = [ws.get(f"lnst_o{i}", [128, 512], BF16) for i in range(ND)]
    wk = load_wblock(P, ws, "gw0", 0,
                     io["aWkT"][0:DM, 0:512].rearrange("(t p) c -> p t c", p=128),
                     "aWkT", 16, True, P.uname("awk"))
    wv = load_wblock(P, ws, "gw0", 16,
                     io["aWvT"][0:DM, 0:512].rearrange("(t p) c -> p t c", p=128),
                     "aWvT", 16, True, P.uname("awv"))
    q2b = ws.get("a_q2", [128, 512], BF16)
    aks = ws.get("a_ks", [128, 512], BF16)
    nrow = ws.get("a_nrow", [1, 512], F32)
    oblk = ws.get("lnstOT", [128, 16 * 512], BF16)
    for sg in range(4):
        for b0 in (0, 4, 8, 12):
            P.dma("sync", lambda e, b0=b0, sg=sg: e.dma_start(
                oblk[0:128, b0 * 512:(b0 + 4) * 512].rearrange(
                    "p (t c) -> p t c", t=4),
                xln[b0 * 128:(b0 + 4) * 128,
                    sg * 512:(sg + 1) * 512].rearrange("(t p) c -> p t c", p=128)),
                reads=[f"attn_xln.{sg}.{b0 // 4}"],
                writes=[f"lnst_o{i}" for i in range(b0, b0 + 4)])
        for half in range(2):
            wq = load_wblock(
                P, ws, "gwC", 0,
                io["aWqT"][0:DM, half * 1024:(half + 1) * 1024].rearrange(
                    "(t p) c -> p t c", p=128),
                "aWqT", 32, True, P.uname("awq"))
            for ot in range(half * 8, half * 8 + 8):
                po, ko = ws.psa(ot % 4)
                ccw = (ot // 4) % 2
                for dt in range(ND):
                    P.op("tensor", lambda e, po=po, w=wq[dt * 2 + ccw][0],
                         of=(ot % 4) * 128, dt=dt: e.matmul(
                             po[:], w[:, of:of + 128], xlt[dt][:],
                             start=(dt == 0), stop=(dt == ND - 1),
                             skip_group_check=True),
                         reads=[wq[dt * 2 + ccw][1], f"lnst_o{dt}"], writes=[ko],
                         quiet=(dt % 4 != 3))
                P.op("scalar", lambda e, po=po: e.activation(q2b[:], po[:], AF.Copy,
                                                             scale=SC),
                     reads=[ko], writes=["a_q2"])
                P.dma("sync", lambda e, ot=ot, sg=sg: e.dma_start(
                    qT[ot * 128:(ot + 1) * 128, sg * 512:(sg + 1) * 512], q2b[:]),
                    reads=["a_q2"], writes=[qT_key], cls="st")
                P.op("scalar", lambda e: e.activation(q2b[:], q2b[:], AF.Square),
                     reads=["a_q2"], writes=["a_q2"])
                pn, kn = ws.psa(4 + ot % 2)
                P.op("tensor", lambda e, pn=pn: e.matmul(
                    pn[0:1, :], onesK[:], q2b[:], start=True, stop=True,
                    skip_group_check=True),
                    reads=["a_q2", "c_onesK"], writes=[kn])
                P.op("vector", lambda e, pn=pn: e.tensor_copy(nrow[:], pn[0:1, :]),
                     reads=[kn], writes=["a_nrow"])
                P.dma("sync", lambda e, ot=ot, sg=sg: e.dma_start(
                    qn2d[ot, sg * 512:(sg + 1) * 512][None, :], nrow[:]),
                    reads=["a_nrow"], writes=[qn2_key], cls="st")
        for kt in range(4):
            po, ko = ws.psa(4 + kt % 2)
            for dt in range(ND):
                P.op("tensor", lambda e, po=po, w=wk[dt][0], of=kt * 128, dt=dt:
                     e.matmul(po[:], w[:, of:of + 128], xlt[dt][:],
                              start=(dt == 0), stop=(dt == ND - 1),
                              skip_group_check=True),
                     reads=[wk[dt][1], f"lnst_o{dt}"], writes=[ko],
                     quiet=(dt % 4 != 3))
            P.op("scalar", lambda e, po=po: e.activation(aks[:], po[:], AF.Copy),
                 reads=[ko], writes=["a_ks"])
            P.dma("sync", lambda e, kt=kt, sg=sg: e.dma_start(
                io["kown"][kt * 128:(kt + 1) * 128, sg * 512:(sg + 1) * 512], aks[:]),
                reads=["a_ks"], writes=["kown"], cls="st")
        for tb in range(4):
            po, ko = ws.psa(6 + tb % 2)
            for dt in range(ND):
                P.op("tensor", lambda e, po=po, tb=tb, dt=dt, w=wv[dt][0]: e.matmul(
                    po[:], xlt[dt][:, tb * 128:(tb + 1) * 128], w[:],
                    start=(dt == 0), stop=(dt == ND - 1), skip_group_check=True),
                    reads=[wv[dt][1], f"lnst_o{dt}"], writes=[ko],
                    quiet=(dt % 4 != 3))
            P.op("scalar", lambda e, po=po: e.activation(q2b[:], po[:], AF.Copy),
                 reads=[ko], writes=["a_q2"])
            P.dma("sync", lambda e, tb=tb, sg=sg: e.dma_start(
                io["vown"][sg * 512 + tb * 128:sg * 512 + (tb + 1) * 128, :], q2b[:]),
                reads=["a_q2"], writes=["vown"], cls="st")


def emit_attn_core(P, ws, io, qT, qT_key, qn2d, qn2_key, x3, x3_key):
    SC = 128 ** -0.5
    ones1 = ws.get("c_ones1", [1, 128], BF16)
    onesK = ws.get("c_onesK", [128, 1], BF16)
    onesM = ws.get("c_onesM", [128, 128], BF16)
    oT = P.dram("attn_oT", [DM, T], BF16)
    q2b = ws.get("a_q2", [128, 512], BF16)
    aks = ws.get("a_ks", [128, 512], BF16)
    kn2 = ws.get("a_kn2", [1, 512], F32)
    kmax = ws.get("a_kmax", [1, 2], F32)
    P.op("vector", lambda e: e.memset(kmax[:], 0.0), writes=["a_kmax"])
    for src_d, srk in [(io["kown"], "kown"), (io["kpart"], "kpart")]:
        for kt in range(4):
            for sc in range(4):
                P.dma("gpsimd", lambda e, kt=kt, sc=sc, src_d=src_d: e.dma_start(
                    aks[:], src_d[kt * 128:(kt + 1) * 128, sc * 512:(sc + 1) * 512]),
                    reads=[srk], writes=["a_ks"])
                P.op("scalar", lambda e: e.activation(q2b[:], aks[:], AF.Square),
                     reads=["a_ks"], writes=["a_q2"])
                pn, kn = ws.psa(4)
                P.op("tensor", lambda e, pn=pn: e.matmul(
                    pn[0:1, :], onesK[:], q2b[:], start=True, stop=True,
                    skip_group_check=True),
                    reads=["a_q2", "c_onesK"], writes=[kn])
                P.op("vector", lambda e, pn=pn: e.reduce_max(
                    kmax[:, 1:2], pn[0:1, :], axis=AX.X),
                    reads=[kn], writes=["a_kmax"])
                P.op("vector", lambda e: e.tensor_tensor(
                    kmax[:, 0:1], kmax[:, 0:1], kmax[:, 1:2], op=ALU.max),
                    reads=["a_kmax"], writes=["a_kmax"])
    P.op("scalar", lambda e: e.activation(kmax[:, 0:1], kmax[:, 0:1], AF.Sqrt),
         reads=["a_kmax"], writes=["a_kmax"])
    maskTD = ws.get("a_maskTD", [128, 128], BF16)
    P.op("gpsimd", lambda e: e.memset(maskTD[:], 0.0), writes=["a_maskTD"])
    P.op("gpsimd", lambda e: e.affine_select(
        out=maskTD[:], in_=maskTD[:], compare_op=ALU.is_ge, fill=-1e30,
        base=0, channel_multiplier=-1, pattern=[[1, 128]]),
        reads=["a_maskTD"], writes=["a_maskTD"])
    cmask = ws.get("a_cmask", [128, 1], F32)
    P.dma("sync", lambda e: e.dma_start(cmask[:], io["cmask"]),
          reads=["cmask"], writes=["a_cmask"])
    aqt = ws.get("a_qt", [128, T], BF16)
    mrow = ws.get("a_mrow", [1, T], BF16)
    ambc = ws.get("a_mbc", [128, T], BF16)
    aPt = [ws.get(f"a_Pt{i}", [128, 512], BF16) for i in range(4)]
    akv = ws.get("a_kv", [128, 8192], BF16)
    aTt = [ws.get(f"a_Tt{i}", [128, 512], F32) for i in range(2)]
    linv = ws.get("a_linv", [128, 1024], F32)
    aot = ws.get("a_ot", [128, 512], BF16)
    for h in range(16):
        kv = h // 4
        P.dma("sync", lambda e, h=h: e.dma_start(aqt[:], qT[h * 128:(h + 1) * 128, :]),
              reads=[qT_key], writes=["a_qt"])
        if h % 4 == 0:
            P.dma("sync", lambda e, kv=kv: e.dma_start(
                akv[0:128, 0:2048], io["kown"][kv * 128:(kv + 1) * 128, :]),
                reads=["kown"], writes=["akv_ko"])
            P.dma("gpsimd", lambda e, kv=kv: e.dma_start(
                akv[0:128, 2048:4096], io["kpart"][kv * 128:(kv + 1) * 128, :]),
                reads=["kpart"], writes=["akv_kp"])
            for q4 in range(4):
                P.dma("sync", lambda e, kv=kv, q4=q4: e.dma_start(
                    akv[0:128, 4096 + q4 * 512:4096 + (q4 + 1) * 512].rearrange(
                        "p (t c) -> p t c", t=4),
                    io["vown"][q4 * 512:(q4 + 1) * 512,
                               kv * 128:(kv + 1) * 128].rearrange(
                        "(t p) c -> p t c", p=128)),
                    reads=["vown"], writes=["akv_vo"])
                P.dma("gpsimd", lambda e, kv=kv, q4=q4: e.dma_start(
                    akv[0:128, 6144 + q4 * 512:6144 + (q4 + 1) * 512].rearrange(
                        "p (t c) -> p t c", t=4),
                    io["vpart"][q4 * 512:(q4 + 1) * 512,
                                kv * 128:(kv + 1) * 128].rearrange(
                        "(t p) c -> p t c", p=128)),
                    reads=["vpart"], writes=["akv_vp"])
        P.dma("gpsimd", lambda e, h=h: e.dma_start(mrow[:], qn2d[h, :][None, :]),
              reads=[qn2_key], writes=["a_mrow"])
        P.op("scalar", lambda e: e.activation(mrow[:], mrow[:], AF.Sqrt),
             reads=["a_mrow"], writes=["a_mrow"])
        P.op("vector", lambda e: e.tensor_scalar(
            mrow[:], mrow[:], kmax[:, 0:1], 1.01, op0=ALU.mult, op1=ALU.mult),
            reads=["a_mrow", "a_kmax"], writes=["a_mrow"])
        P.op("vector", lambda e: e.tensor_scalar(
            mrow[:], mrow[:], 1.0, None, op0=ALU.add),
            reads=["a_mrow"], writes=["a_mrow"])
        for qc in range(4):
            pb, kb = ws.psa(6 + qc % 2)
            P.op("tensor", lambda e, pb=pb, qc=qc: e.matmul(
                pb[:], ones1[:], mrow[:, qc * 512:(qc + 1) * 512],
                start=True, stop=True, skip_group_check=True),
                reads=["a_mrow", "c_ones1"], writes=[kb])
            P.op("vector", lambda e, pb=pb, qc=qc: e.tensor_copy(
                ambc[:, qc * 512:(qc + 1) * 512], pb[:]),
                reads=[kb], writes=["a_mbc"])
        for qh in range(2):
            q0 = qh * 1024
            plan = []
            for kt in range(16):
                plan.append(("p", kt, q0, 1024, "cross"))
            for kt in range(qh * 8, qh * 8 + 8):
                kq0 = max(kt * 128, q0)
                plan.append(("o", kt, kq0, 128, "diag"))
                if q0 + 1024 - (kq0 + 128) > 0:
                    plan.append(("o", kt, kq0 + 128, q0 + 1024 - (kq0 + 128), "none"))
            if qh == 1:
                for kt in range(0, 8):
                    plan.append(("o", kt, q0, 1024, "none"))
            def subchunks(qs, qn):
                cq = qs
                while cq < qs + qn:
                    nb = q0 + ((cq - q0) // 512 + 1) * 512
                    cw = min(qs + qn, nb) - cq
                    yield cq, cw
                    cq += cw
            lastmm = {}
            for idx, (src, kt, qs, qn, mk) in enumerate(plan):
                for cq0, cw in subchunks(qs, qn):
                    lastmm[(cq0 - q0) // 512] = (idx, cq0)
            def emit_st(idx, entry):
                src, kt, qs, qn, mk = entry
                koff = (0 if src == "o" else 2048) + kt * 128
                voff = (4096 if src == "o" else 6144) + kt * 128
                akb = akv.t if isinstance(akv, APView) else akv
                ak0 = akv.c0 if isinstance(akv, APView) else 0
                ktb = APView(akb, 128, ak0 + koff, 128)
                ktbk = "akv_ko" if src == "o" else "akv_kp"
                vt = APView(akb, 128, ak0 + voff, 128)
                vtk = "akv_vo" if src == "o" else "akv_vp"
                outs = []
                for ci, (cq0, cw) in enumerate(subchunks(qs, qn)):
                    rot = (2 * idx + ci) % 4
                    pS, kS = ws.psa(4 + rot % 2)
                    P.op("tensor", lambda e, pS=pS, ktb=ktb, cq0=cq0,
                         cw=cw: e.matmul(
                             pS[:, 0:cw], ktb[:], aqt[:, cq0:cq0 + cw],
                             start=True, stop=True, skip_group_check=True),
                         reads=[ktbk, "a_qt"], writes=[kS])
                    pt, ptk = aPt[rot], f"a_Pt{rot}"
                    tt, ttk = aTt[rot % 2], f"a_Tt{rot % 2}"
                    if mk == "cross":
                        P.op("vector", lambda e, pS=pS, cq0=cq0, cw=cw, tt=tt:
                             e.scalar_tensor_tensor(
                                 tt[:, 0:cw], pS[:, 0:cw], cmask[:],
                                 ambc[:, cq0:cq0 + cw],
                                 op0=ALU.add, op1=ALU.subtract),
                             reads=[kS, "a_cmask", "a_mbc"], writes=[ttk])
                    elif mk == "diag":
                        P.op("vector", lambda e, pS=pS, cw=cw, tt=tt: e.tensor_tensor(
                            tt[:, 0:cw], pS[:, 0:cw], maskTD[:, 0:cw], op=ALU.add),
                            reads=[kS, "a_maskTD"], writes=[ttk])
                        P.op("vector", lambda e, cq0=cq0, cw=cw, tt=tt: e.tensor_tensor(
                            tt[:, 0:cw], tt[:, 0:cw], ambc[:, cq0:cq0 + cw],
                            op=ALU.subtract),
                            reads=[ttk, "a_mbc"], writes=[ttk])
                    else:
                        P.op("vector", lambda e, pS=pS, cq0=cq0, cw=cw, tt=tt:
                             e.tensor_tensor(
                                 tt[:, 0:cw], pS[:, 0:cw], ambc[:, cq0:cq0 + cw],
                                 op=ALU.subtract),
                             reads=[kS, "a_mbc"], writes=[ttk])
                    P.op("scalar", lambda e, pt=pt, cw=cw, tt=tt: e.activation(
                        pt[:, 0:cw], tt[:, 0:cw], AF.Exp),
                        reads=[ttk], writes=[ptk])
                    outs.append((cq0, cw, pt, ptk))
                return vt, vtk, outs

            def emit_pv(idx, staged):
                vt, vtk, outs = staged
                for cq0, cw, pt, ptk in outs:
                    bank = (cq0 - q0) // 512
                    pO, kO = ws.psa(bank)
                    ob0 = cq0 - q0 - bank * 512
                    first = bank not in seen_first_o
                    seen_first_o.add(bank)
                    last = (lastmm.get(bank) == (idx, cq0))
                    P.op("tensor", lambda e, pO=pO, vt=vt, pt=pt, ob0=ob0, cw=cw,
                         first=first, last=last: e.matmul(
                             pO[:, ob0:ob0 + cw], vt[:], pt[:, 0:cw],
                             start=first, stop=last, skip_group_check=True),
                         reads=[vtk, ptk], writes=[kO], quiet=not last)
                for cq0, cw, pt, ptk in outs:
                    bank = (cq0 - q0) // 512
                    pL, kL = ws.psa(2 + bank)
                    ob0 = cq0 - q0 - bank * 512
                    first = bank not in seen_first_l
                    seen_first_l.add(bank)
                    last = (lastmm.get(bank) == (idx, cq0))
                    P.op("tensor", lambda e, pL=pL, pt=pt, ob0=ob0, cw=cw,
                         first=first, last=last: e.matmul(
                             pL[:, ob0:ob0 + cw], onesM[:], pt[:, 0:cw],
                             start=first, stop=last, skip_group_check=True),
                         reads=["c_onesM", ptk], writes=[kL], quiet=not last)

            seen_first_o = set()
            seen_first_l = set()
            prev = None
            for idx, entry in enumerate(plan):
                staged = emit_st(idx, entry)
                if prev is not None:
                    emit_pv(prev[0], prev[1])
                prev = (idx, staged)
            emit_pv(prev[0], prev[1])
            for bank in range(2):
                pO, kO = ws.psa(bank)
                pL, kL = ws.psa(2 + bank)
                P.op("vector", lambda e, pL=pL, bank=bank: e.tensor_scalar(
                    linv[:, bank * 512:(bank + 1) * 512], pL[:, 0:512],
                    1e-37, None, op0=ALU.max),
                    reads=[kL], writes=["a_linv"])
                P.op("scalar", lambda e, bank=bank: e.activation(
                    linv[:, bank * 512:(bank + 1) * 512],
                    linv[:, bank * 512:(bank + 1) * 512], AF.Ln),
                    reads=["a_linv"], writes=["a_linv"])
                P.op("scalar", lambda e, bank=bank: e.activation(
                    linv[:, bank * 512:(bank + 1) * 512],
                    linv[:, bank * 512:(bank + 1) * 512], AF.Exp, scale=-1.0),
                    reads=["a_linv"], writes=["a_linv"])
                P.op("vector", lambda e, pO=pO, bank=bank: e.tensor_tensor(
                    aot[:], pO[:, 0:512], linv[:, bank * 512:(bank + 1) * 512],
                    op=ALU.mult),
                    reads=[kO, "a_linv"], writes=["a_ot"])
                P.dma("sync", lambda e, h=h, q0=q0, bank=bank: e.dma_start(
                    oT[h * 128:(h + 1) * 128,
                       q0 + bank * 512:q0 + (bank + 1) * 512], aot[:]),
                    reads=["a_ot"],
                    writes=["attn_oT",
                            f"attn_oT.{(q0 + bank * 512) // 512}.{h // 4}"], cls="st")
    P.barrier()
    ws.new_epoch()
    emit_dense_fm(P, ws, "awp", io["aWpT"], "aWpT", io["abp"], "abp",
                  oT, "attn_oT", x3, x3_key, DM, DM, None,
                  res_dram=io["x2"], res_key="x2", out_dtype=F32,
                  in_granular=True)


def build_attnproj_phase():
    P = Prog()
    ws = WS(P)
    io = dict(
        x2=P.dram("x2", [DM, T], F32, kind="ExternalInput"),
        aWqT=P.dram("aWqT", [DM, DM], F32, kind="ExternalInput"),
        aWkT=P.dram("aWkT", [DM, 512], F32, kind="ExternalInput"),
        aWvT=P.dram("aWvT", [DM, 512], F32, kind="ExternalInput"),
        kown=P.dram("kown", [512, T], BF16, kind="ExternalOutput"),
        vown=P.dram("vown", [T, 512], BF16, kind="ExternalOutput"),
    )
    qT = P.dram("qTo", [DM, T], BF16, kind="ExternalOutput")
    qn2d = P.dram("qn2o", [16, T], F32, kind="ExternalOutput")
    emit_consts(P, ws)
    emit_attn_proj(P, ws, io, qT, "qTo", qn2d, "qn2o")
    return P.finalize()


def build_attncore_phase():
    P = Prog()
    ws = WS(P)
    io = dict(
        x2=P.dram("x2", [DM, T], F32, kind="ExternalInput"),
        aWpT=P.dram("aWpT", [DM, DM], F32, kind="ExternalInput"),
        abp=P.dram("abp", [DM], F32, kind="ExternalInput"),
        kown=P.dram("kown", [512, T], BF16, kind="ExternalInput"),
        vown=P.dram("vown", [T, 512], BF16, kind="ExternalInput"),
        kpart=P.dram("kpart", [512, T], BF16, kind="ExternalInput"),
        vpart=P.dram("vpart", [T, 512], BF16, kind="ExternalInput"),
        cmask=P.dram("cmask", [128, 1], F32, kind="ExternalInput"),
    )
    qT = P.dram("qTo", [DM, T], BF16, kind="ExternalInput")
    qn2d = P.dram("qn2o", [16, T], F32, kind="ExternalInput")
    x3 = P.dram("x3", [DM, T], F32, kind="ExternalOutput")
    emit_consts(P, ws)
    emit_attn_core(P, ws, io, qT, "qTo", qn2d, "qn2o", x3, "x3")
    return P.finalize()


_BUILD_CACHE = {}


def _get(name, builder):
    if name not in _BUILD_CACHE:
        _BUILD_CACHE[name] = builder()
    return _BUILD_CACHE[name]


def kernel(**inputs):
    return _kernel_fused(**inputs)


def _kernel_fused(L0_attn_hidden_states, L0_attn_Wq, L0_attn_bq, L0_attn_Wk,
                  L0_attn_bk, L0_attn_Wv, L0_attn_bv, L0_ffn_w12, L0_ffn_w3,
                  L0_ffn_router_w, L0_ffn_expert_bias, L0_ffn_sw1, L0_ffn_sw2,
                  L0_ffn_sw3, L1_attn_Wq, L1_attn_Wk, L1_attn_Wv, L1_attn_Wp,
                  L1_attn_bp, L1_ffn_W1, L1_ffn_b1, L1_ffn_W2, L1_ffn_b2,
                  L1_ffn_W3, L1_ffn_b3):
    C = list(range(NCORE))
    x = np.asarray(L0_attn_hidden_states)
    shards = [np.ascontiguousarray(x[c // 2, (c % 2) * T:(c % 2 + 1) * T, :].T)
              for c in C]
    tr = lambda a: np.ascontiguousarray(np.asarray(a).T)
    f32 = lambda a: np.ascontiguousarray(np.asarray(a), dtype=np.float32)
    Wk_sl = np.concatenate([np.asarray(L1_attn_Wk)[g * 512:g * 512 + 128]
                            for g in range(4)], axis=0)
    Wv_sl = np.concatenate([np.asarray(L1_attn_Wv)[g * 512:g * 512 + 128]
                            for g in range(4)], axis=0)
    wshared = dict(
        gWqT=tr(L0_attn_Wq), gbq=f32(L0_attn_bq),
        gWkT=tr(L0_attn_Wk), gbk=f32(L0_attn_bk),
        gWvT=tr(L0_attn_Wv), gbv=f32(L0_attn_bv),
        w12T=np.ascontiguousarray(np.asarray(L0_ffn_w12).transpose(0, 2, 1)),
        w3T=np.ascontiguousarray(np.asarray(L0_ffn_w3).transpose(0, 2, 1)),
        rwT=tr(L0_ffn_router_w), ebias=f32(L0_ffn_expert_bias),
        sw1T=tr(L0_ffn_sw1), sw2T=tr(L0_ffn_sw2), sw3T=tr(L0_ffn_sw3),
        aWqT=tr(L1_attn_Wq), aWkT=tr(Wk_sl), aWvT=tr(Wv_sl),
        aWpT=tr(L1_attn_Wp), abp=f32(L1_attn_bp),
        W1T=tr(L1_ffn_W1), b1=f32(L1_ffn_b1),
        W2T=tr(L1_ffn_W2), b2=f32(L1_ffn_b2),
        W3T=tr(L1_ffn_W3), b3=f32(L1_ffn_b3))
    in_maps = []
    for c in C:
        h = c % 2
        in_maps.append(dict(
            x0=shards[c],
            hmask=np.full((128, 1), float(h), np.float32),
            cmask=np.full((128, 1), 0.0 if h == 1 else -1e30, np.float32),
            **wshared))
    nc = _get("fused", build_fused)
    res = run_bass_kernel_spmd(nc, in_maps, core_ids=C)
    out = np.empty((B, S, DM), np.float32)
    for c in C:
        out[c // 2, (c % 2) * T:(c % 2 + 1) * T, :] =             res.results[c]["x4"].reshape(DM, T).T
    return out


def _kernel_unfused(L0_attn_hidden_states, L0_attn_Wq, L0_attn_bq, L0_attn_Wk, L0_attn_bk,
           L0_attn_Wv, L0_attn_bv, L0_ffn_w12, L0_ffn_w3, L0_ffn_router_w,
           L0_ffn_expert_bias, L0_ffn_sw1, L0_ffn_sw2, L0_ffn_sw3,
           L1_attn_Wq, L1_attn_Wk, L1_attn_Wv, L1_attn_Wp, L1_attn_bp,
           L1_ffn_W1, L1_ffn_b1, L1_ffn_W2, L1_ffn_b2, L1_ffn_W3, L1_ffn_b3):
    C = list(range(NCORE))
    x = np.asarray(L0_attn_hidden_states)
    shards = [np.ascontiguousarray(x[c // 2, (c % 2) * T:(c % 2 + 1) * T, :].T)
              for c in C]

    tr = lambda a: np.ascontiguousarray(np.asarray(a).T)
    f32 = lambda a: np.ascontiguousarray(np.asarray(a), dtype=np.float32)

    gla_w = dict(gWqT=tr(L0_attn_Wq), gbq=f32(L0_attn_bq),
                 gWkT=tr(L0_attn_Wk), gbk=f32(L0_attn_bk),
                 gWvT=tr(L0_attn_Wv), gbv=f32(L0_attn_bv))
    nc = _get("gla", build_gla_phase)
    r1 = run_bass_kernel_spmd(nc, [dict(x0=shards[c], **gla_w) for c in C],
                              core_ids=C)
    nc = _get("glacorr", build_glacorr_phase)
    in2 = []
    for c in C:
        h = c % 2
        sin = (r1.results[c - 1]["sfin"] if h == 1
               else np.zeros((512, 128), np.float32))
        in2.append(dict(
            hmask=np.full((128, 1), float(h), np.float32),
            sin=np.ascontiguousarray(sin.reshape(512, 128)),
            x1p=r1.results[c]["x1p"].reshape(DM, T),
            qhat=r1.results[c]["qhat"].reshape(512, 8192)))
    r2 = run_bass_kernel_spmd(nc, in2, core_ids=C)
    moe_w = dict(
        w12T=np.ascontiguousarray(np.asarray(L0_ffn_w12).transpose(0, 2, 1)),
        w3T=np.ascontiguousarray(np.asarray(L0_ffn_w3).transpose(0, 2, 1)),
        rwT=tr(L0_ffn_router_w), ebias=f32(L0_ffn_expert_bias),
        sw1T=tr(L0_ffn_sw1), sw2T=tr(L0_ffn_sw2), sw3T=tr(L0_ffn_sw3))
    nc = _get("moe", build_moe_phase)
    r3 = run_bass_kernel_spmd(
        nc, [dict(x1=r2.results[c]["x1"].reshape(DM, T), **moe_w) for c in C],
        core_ids=C)
    Wk_sl = np.concatenate([np.asarray(L1_attn_Wk)[g * 512:g * 512 + 128]
                            for g in range(4)], axis=0)
    Wv_sl = np.concatenate([np.asarray(L1_attn_Wv)[g * 512:g * 512 + 128]
                            for g in range(4)], axis=0)
    ap_w = dict(aWqT=tr(L1_attn_Wq), aWkT=tr(Wk_sl), aWvT=tr(Wv_sl))
    nc = _get("attnproj", build_attnproj_phase)
    r4 = run_bass_kernel_spmd(
        nc, [dict(x2=r3.results[c]["x2"].reshape(DM, T), **ap_w) for c in C],
        core_ids=C)
    ac_w = dict(aWpT=tr(L1_attn_Wp), abp=f32(L1_attn_bp))
    nc = _get("attncore", build_attncore_phase)
    in5 = []
    for c in C:
        h = c % 2
        part = c - 1 if h == 1 else c
        in5.append(dict(
            x2=r3.results[c]["x2"].reshape(DM, T),
            kown=r4.results[c]["kown"].reshape(512, T),
            vown=r4.results[c]["vown"].reshape(T, 512),
            kpart=r4.results[part]["kown"].reshape(512, T),
            vpart=r4.results[part]["vown"].reshape(T, 512),
            qTo=r4.results[c]["qTo"].reshape(DM, T),
            qn2o=r4.results[c]["qn2o"].reshape(16, T),
            cmask=np.full((128, 1), 0.0 if h == 1 else -1e30, np.float32),
            **ac_w))
    r5 = run_bass_kernel_spmd(nc, in5, core_ids=C)
    mlp_w = dict(W1T=tr(L1_ffn_W1), b1=f32(L1_ffn_b1),
                 W2T=tr(L1_ffn_W2), b2=f32(L1_ffn_b2),
                 W3T=tr(L1_ffn_W3), b3=f32(L1_ffn_b3))
    nc = _get("mlp", build_mlp_phase)
    r6 = run_bass_kernel_spmd(
        nc, [dict(x3=r5.results[c]["x3"].reshape(DM, T), **mlp_w) for c in C],
        core_ids=C)
    out = np.empty((B, S, DM), np.float32)
    for c in C:
        out[c // 2, (c % 2) * T:(c % 2 + 1) * T, :] = \
            r6.results[c]["x4"].reshape(DM, T).T
    return out


def build_fused():
    P = Prog()
    ws = WS(P, pooled=True)
    io = dict(
        x0=P.dram("x0", [DM, T], F32, kind="ExternalInput"),
        gWqT=P.dram("gWqT", [DM, DM], F32, kind="ExternalInput"),
        gbq=P.dram("gbq", [DM], F32, kind="ExternalInput"),
        gWkT=P.dram("gWkT", [DM, 512], F32, kind="ExternalInput"),
        gbk=P.dram("gbk", [512], F32, kind="ExternalInput"),
        gWvT=P.dram("gWvT", [DM, 512], F32, kind="ExternalInput"),
        gbv=P.dram("gbv", [512], F32, kind="ExternalInput"),
        hmask=P.dram("hmask", [128, 1], F32, kind="ExternalInput"),
        cmask=P.dram("cmask", [128, 1], F32, kind="ExternalInput"),
        w12T=P.dram("w12T", [8, DM, 2048], F32, kind="ExternalInput"),
        w3T=P.dram("w3T", [8, 1024, 2048], F32, kind="ExternalInput"),
        rwT=P.dram("rwT", [DM, 8], F32, kind="ExternalInput"),
        ebias=P.dram("ebias", [8], F32, kind="ExternalInput"),
        sw1T=P.dram("sw1T", [DM, 8192], F32, kind="ExternalInput"),
        sw2T=P.dram("sw2T", [DM, 8192], F32, kind="ExternalInput"),
        sw3T=P.dram("sw3T", [8192, DM], F32, kind="ExternalInput"),
        aWqT=P.dram("aWqT", [DM, DM], F32, kind="ExternalInput"),
        aWkT=P.dram("aWkT", [DM, 512], F32, kind="ExternalInput"),
        aWvT=P.dram("aWvT", [DM, 512], F32, kind="ExternalInput"),
        aWpT=P.dram("aWpT", [DM, DM], F32, kind="ExternalInput"),
        abp=P.dram("abp", [DM], F32, kind="ExternalInput"),
        W1T=P.dram("W1T", [DM, 4096], F32, kind="ExternalInput"),
        b1=P.dram("b1", [4096], F32, kind="ExternalInput"),
        W2T=P.dram("W2T", [4096, 4096], F32, kind="ExternalInput"),
        b2=P.dram("b2", [4096], F32, kind="ExternalInput"),
        W3T=P.dram("W3T", [4096, DM], F32, kind="ExternalInput"),
        b3=P.dram("b3", [DM], F32, kind="ExternalInput"),
    )
    x1p = P.dram("x1p", [DM, T], F32)
    qhat = P.dram("qhat", [512, 8192], BF16)
    sfin = P.dram("sfin", [512, 128], F32)
    x1 = P.dram("x1", [DM, T], F32)
    x2 = P.dram("x2", [DM, T], F32)
    x3 = P.dram("x3", [DM, T], F32)
    kown = P.dram("kown", [512, T], BF16)
    vown = P.dram("vown", [T, 512], BF16)
    qT = P.dram("qTo", [DM, T], BF16)
    qn2d = P.dram("qn2o", [16, T], F32)
    cc1out = P.dram("cc1out", [1024, 128], F32)
    cc2k = P.dram("cc2k", [1024, T], BF16)
    cc2v = P.dram("cc2v", [2 * T, 512], BF16)
    x4 = P.dram("x4", [DM, T], F32, kind="ExternalOutput")
    io["x1"] = x1
    io["x2"] = x2
    io["kown"] = kown
    io["vown"] = vown
    io["sin"] = cc1out[0:512, :]
    io["kpart"] = cc2k[0:512, :]
    io["vpart"] = cc2v[0:T, :]

    emit_consts(P, ws)
    w12b = P.dram("moe_w12b", [8, DM, 2048], BF16)
    w3b = P.dram("moe_w3b", [8, 1024, 2048], BF16)
    for ex in range(8):
        emit_bf16_cache(P, ws, io["w12T"][ex], "w12T", w12b[ex], "moe_w12b",
                        DM, 2048, "cw12", bg=True)
        emit_bf16_cache(P, ws, io["w3T"][ex], "w3T", w3b[ex], "moe_w3b",
                        1024, 2048, "cw12", bg=True)
    emit_gla(P, ws, io, x1p, "x1p", sfin, qhat)
    P.barrier()
    ws.new_epoch()
    P.op("gpsimd", lambda e: e.collective_compute(
        "AllGather", ALU.bypass,
        replica_groups=[[0, 1], [2, 3], [4, 5], [6, 7]],
        ins=[sfin.opt()], outs=[cc1out.opt()]),
        reads=["sfin"], writes=["sin"])
    emit_glacorr(P, ws, io, x1p, "x1p", qhat, "qhat", x1, "x1")
    P.barrier()
    ws.new_epoch()
    d = emit_moe(P, ws, io, cache=(w12b, w3b))
    P.pump_all()
    emit_moe_routed(P, ws, io, d["xln"], d["x2p"], d["w12b"], d["w3b"], d["combT"],
                    cache=d["cached"])
    P.barrier()
    ws.new_epoch()
    emit_attn_proj(P, ws, io, qT, "qTo", qn2d, "qn2o")
    P.op("gpsimd", lambda e: e.collective_compute(
        "AllGather", ALU.bypass,
        replica_groups=[[0, 1], [2, 3], [4, 5], [6, 7]],
        ins=[kown.opt()], outs=[cc2k.opt()]),
        reads=["kown"], writes=["kpart"])
    P.op("gpsimd", lambda e: e.collective_compute(
        "AllGather", ALU.bypass,
        replica_groups=[[0, 1], [2, 3], [4, 5], [6, 7]],
        ins=[vown.opt()], outs=[cc2v.opt()]),
        reads=["vown"], writes=["vpart"])
    emit_attn_core(P, ws, io, qT, "qTo", qn2d, "qn2o", x3, "x3")
    P.barrier()
    ws.new_epoch()
    emit_mlp(P, ws, dict(x3=x3, W1T=io["W1T"], b1=io["b1"], W2T=io["W2T"],
                         b2=io["b2"], W3T=io["W3T"], b3=io["b3"], x4=x4))
    return P.finalize()


def kernel_fused(inputs_by_core):
    nc = _get("fused", build_fused)
    res = run_bass_kernel_spmd(nc, inputs_by_core, core_ids=list(range(NCORE)))
    return res
